# version 10
# baseline (speedup 1.0000x reference)
import numpy as np

import concourse.bass as bass
import concourse.mybir as mybir

F32 = mybir.dt.float32
BF16 = mybir.dt.bfloat16
AF = mybir.ActivationFunctionType
ALU = mybir.AluOpType
AX = mybir.AxisListType


class Sem:
    def __init__(self, name):
        self.name = name
        self.n = 0
        self.h = None
        self.is_dma = False
        self.batches = []
        self.open_start = None
        self.close_pending = False


class Prog:
    ENGINES = ("sync", "scalar", "vector", "tensor", "gpsimd")

    def __init__(self, nc):
        self.nc = nc
        self.q = {e: [] for e in self.ENGINES}
        self.sems = []
        self.ninst = 0
        self.auto = {e: self.sem("a_" + e) for e in self.ENGINES}
        self.mem = {}
        self.sat = {e: {} for e in self.ENGINES}
        self.ro = set()
        self.psum_names = set()
        self._pegs = {}

    def sem(self, name):
        s = Sem(name)
        self.sems.append(s)
        return s

    def emit(self, engine, fn):
        self.q[engine].append(fn)
        self.ninst += 1

    def raw_wait(self, engine, sem, val):
        if self.sat[engine].get(sem, 0) >= val or val <= 0:
            return
        self.sat[engine][sem] = val
        self.emit(engine, lambda eng, s=sem, v=val: eng.wait_ge(s.h, v))

    @staticmethod
    def rect(ap):
        t = ap.tensor
        name = t.name
        kind = type(t).__name__
        shape = list(t.shape)
        dims = list(ap.ap)
        off = ap.offset
        if kind == "DRamTensorHandle":
            span = 1
            for st, cnt in dims:
                span += (cnt - 1) * st
            return (name, 0, 1, off, off + span)
        if kind == "PSumTensorHandle":
            return (name, 0, 128, 0, 99999)
        W = shape[1] if len(shape) == 2 else int(np.prod(shape[1:]))
        p0 = off // W
        c0 = off % W
        pn = dims[0][1] if dims else 1
        span = 1
        for st, cnt in dims[1:]:
            span += (cnt - 1) * st
        return (name, p0, p0 + pn, c0, c0 + span)

    @staticmethod
    def _overlap(a, b):
        return a[1] < b[2] and b[1] < a[2] and a[3] < b[4] and b[3] < a[4]

    @staticmethod
    def _covers(a, b):
        return a[1] <= b[1] and a[2] >= b[2] and a[3] <= b[3] and a[4] >= b[4]

    def _dep_val(self, sem, val):
        if not sem.is_dma:
            return val
        for st, en in sem.batches:
            if st < val <= en:
                return en
        if sem.open_start is not None and val > sem.open_start:
            sem.batches.append((sem.open_start, sem.n))
            sem.open_start = None
            sem.close_pending = True
            return sem.n
        return val

    def _dep_wait(self, engine, sem, val):
        self.raw_wait(engine, sem, self._dep_val(sem, val))

    def _resolve(self, engine, reads, writes):
        deps = {}
        rrects, wrects = [], []
        own = self.auto.get(engine)
        for ap in reads:
            r = self.rect(ap)
            rrects.append(r)
            excl = r[0] in self.psum_names
            for rec in self.mem.get(r[0], ()):
                if self._overlap(r, rec) and (
                        rec[5] == "W" or (excl and rec[6] is not own)):
                    deps[rec[6]] = max(deps.get(rec[6], 0), rec[7])
        for ap in writes:
            r = self.rect(ap)
            wrects.append(r)
            for rec in self.mem.get(r[0], ()):
                if self._overlap(r, rec):
                    deps[rec[6]] = max(deps.get(rec[6], 0), rec[7])
        for sem, val in deps.items():
            self._dep_wait(engine, sem, val)
        return rrects, wrects

    def _record(self, engine, rrects, wrects, sem, val):
        for r in rrects:
            if r[0] in self.ro:
                continue
            recs = self.mem.setdefault(r[0], [])
            recs[:] = [q for q in recs
                       if not (q[5] == "R" and q[6] is sem and q[:5] == r[:5])]
            recs.append(r + ("R", sem, val))
        for r in wrects:
            recs = self.mem.setdefault(r[0], [])
            recs[:] = [q for q in recs if not self._covers(r, q)]
            recs.append(r + ("W", sem, val))

    def op(self, engine, fn, r=(), w=(), pe=None):
        sem = self.auto[engine]
        rrects, wrects = self._resolve(engine, r, w)
        if pe is None or pe == "single":
            val = sem.n + 1
            sem.n += 1
            self.emit(engine, lambda eng, f=fn, s=sem: f(eng).then_inc(s.h, 1))
            self._record(engine, rrects, wrects, sem, val)
            return
        key = wrects[0][0]
        if pe == "start":
            assert key not in self._pegs, f"group already open on {key}"
            self._pegs[key] = {"reads": list(rrects), "writes": list(wrects)}
            self.emit(engine, fn)
        elif pe == "mid":
            g = self._pegs[key]
            g["reads"] += rrects
            g["writes"] += wrects
            self.emit(engine, fn)
        elif pe == "stop":
            g = self._pegs.pop(key)
            val = sem.n + 1
            sem.n += 1
            self.emit(engine, lambda eng, f=fn, s=sem: f(eng).then_inc(s.h, 1))
            self._record(engine, g["reads"] + rrects, g["writes"] + wrects,
                         sem, val)
        else:
            raise ValueError(pe)

    def dma(self, engine, out, in_, sem, **kw):
        sem.is_dma = True
        rrects, wrects = self._resolve(engine, [in_], [out])
        if sem.close_pending:
            self.raw_wait(engine, sem, sem.n)
            sem.close_pending = False
        if sem.open_start is None:
            sem.open_start = sem.n
        sem.n += 16
        val = sem.n
        self.emit(engine, lambda eng, o=out, i=in_, s=sem, kw=kw:
                  eng.dma_start(out=o, in_=i, **kw).then_inc(s.h, 16))
        self._record(engine, rrects, wrects, sem, val)
        return val

    def mark_ro(self, *names):
        for n in names:
            self.ro.add(n)
            if n in self.mem:
                self.mem[n] = [q for q in self.mem[n] if q[5] == "W"]

    def join_all(self):
        targets = []
        for e in self.ENGINES:
            s = self.auto[e]
            self.raw_wait(e, s, s.n)
            s.n += 1
            self.emit(e, lambda eng, ss=s: eng.nop().then_inc(ss.h, 1))
            targets.append((s, s.n))
        for e in self.ENGINES:
            for s, v in targets:
                self.raw_wait(e, s, v)

    def build_block(self, ctx):
        for s in self.sems:
            s.h = ctx.enter_context(self.nc.semaphore(s.name))
        with self.nc.Block() as block:
            for name in self.ENGINES:
                cmds = self.q[name]
                def runner(eng, cmds=cmds):
                    for fn in cmds:
                        fn(eng)
                getattr(block, name)(runner)


class SchedProg(Prog):

    def __init__(self, nc):
        super().__init__(nc)
        self.cmds = []
        self.cmem = {}
        self._open_groups = {}
        self.buffering = True

    def _cdeps(self, rrects, wrects):
        deps = set()
        for r in rrects:
            for rec in self.cmem.get(r[0], ()):
                if rec[5] == "W" and self._overlap(r, rec):
                    deps.add(rec[6])
                elif r[0] in self.psum_names and self._overlap(r, rec):
                    deps.add(rec[6])
        for r in wrects:
            for rec in self.cmem.get(r[0], ()):
                if self._overlap(r, rec):
                    deps.add(rec[6])
        return deps

    def _crecord(self, rrects, wrects, cid):
        for r in rrects:
            if r[0] in self.ro:
                continue
            recs = self.cmem.setdefault(r[0], [])
            recs.append(r + ("R", cid))
        for r in wrects:
            recs = self.cmem.setdefault(r[0], [])
            recs[:] = [q for q in recs if not self._covers(r, q)]
            recs.append(r + ("W", cid))

    def op(self, engine, fn, r=(), w=(), pe=None):
        if not self.buffering:
            return super().op(engine, fn, r=r, w=w, pe=pe)
        rrects = [self.rect(ap) for ap in r]
        wrects = [self.rect(ap) for ap in w]
        if pe in (None, "single"):
            cid = len(self.cmds)
            cmd = {"engine": engine, "parts": [(fn, r, w, "single" if pe else None)],
                   "deps": self._cdeps(rrects, wrects), "id": cid}
            self.cmds.append(cmd)
            self._crecord(rrects, wrects, cid)
            return
        key = wrects[0][0]
        if pe == "start":
            cid = len(self.cmds)
            cmd = {"engine": engine, "parts": [(fn, r, w, "start")],
                   "deps": self._cdeps(rrects, wrects), "id": cid,
                   "rr": list(rrects), "ww": list(wrects)}
            self.cmds.append(cmd)
            self._open_groups[key] = cmd
        elif pe in ("mid", "stop"):
            cmd = self._open_groups[key]
            cmd["parts"].append((fn, r, w, pe))
            cmd["deps"] |= self._cdeps(rrects, wrects)
            cmd["rr"] += rrects
            cmd["ww"] += wrects
            if pe == "stop":
                del self._open_groups[key]
                self._crecord(cmd["rr"], cmd["ww"], cmd["id"])

    def dma(self, engine, out, in_, sem, **kw):
        if not self.buffering:
            return super().dma(engine, out, in_, sem, **kw)
        rrects = [self.rect(in_)]
        wrects = [self.rect(out)]
        cid = len(self.cmds)
        cmd = {"engine": engine, "dma": (out, in_, sem, kw),
               "deps": self._cdeps(rrects, wrects), "id": cid}
        self.cmds.append(cmd)
        self._crecord(rrects, wrects, cid)

    def join_all(self):
        if not self.buffering:
            return super().join_all()
        return

    def schedule_and_emit(self):
        import heapq
        n = len(self.cmds)
        fwd = [[] for _ in range(n)]
        indeg = [0] * n
        for c in self.cmds:
            i = c["id"]
            for d in c["deps"]:
                if d != i:
                    fwd[d].append(i)
                    indeg[i] += 1
        level = [0] * n
        ready = [i for i in range(n) if indeg[i] == 0]
        heapq.heapify(ready)
        order = []
        while ready:
            i = heapq.heappop(ready)
            order.append(i)
            for j in fwd[i]:
                if level[i] + 1 > level[j]:
                    level[j] = level[i] + 1
                indeg[j] -= 1
                if indeg[j] == 0:
                    heapq.heappush(ready, j)
        assert len(order) == n, f"cycle in dep graph: {len(order)} != {n}"
        order = sorted(order, key=lambda i: (level[i], i))
        self.buffering = False
        for i in order:
            c = self.cmds[i]
            if "dma" in c:
                out, in_, sem, kw = c["dma"]
                super().dma(c["engine"], out, in_, sem, **kw)
            else:
                for fn, r, w, pe in c["parts"]:
                    super().op(c["engine"], fn, r=r, w=w, pe=pe)


"""Builder for the mk_seq_postln_d2 megakernel (auto-dependency version).

Layout (per core, 2048 tokens = 16 tiles of 128):
- X  : residual stream, token-major fp32 [128, 16*1024]; tile t cols [t*1024,+1024)
- XF : feature-major bf16 [128, 8*2048]; d-chunk dc cols [dc*2048,+2048)
- matmul operands at partition base 0; PSUM: 8 banks [128, 512] f32.
"""
import numpy as np
import ml_dtypes
from contextlib import ExitStack



NT = 16
T = 2048
D = 1024
NCH = 16
NH0 = 8
HD0 = 128
NST = 64
NH1 = 16
HD1 = 64
FD = 64
NE = 10
HID = 2048


def host_prep(inputs):
    f32 = np.float32
    x0 = np.asarray(inputs["L0_attn_x"], f32)
    shared = {}
    shared["wxT"] = np.ascontiguousarray(np.asarray(inputs["L0_attn_W_xproj"], f32).T)
    shared["dtbias"] = np.asarray(inputs["L0_attn_dt_bias"], f32).reshape(1, 8)
    shared["alog"] = np.asarray(inputs["L0_attn_A_log"], f32).reshape(1, 8)
    shared["dvec"] = np.asarray(inputs["L0_attn_D"], f32).reshape(1, 8)
    w1 = np.asarray(inputs["L0_ffn_w1"], f32)
    shared["w1L"] = np.ascontiguousarray(
        w1.reshape(8, 128, 32, 128).transpose(2, 1, 0, 3).reshape(32, 128, 1024))
    shared["b1col"] = np.ascontiguousarray(
        np.asarray(inputs["L0_ffn_b1"], f32).reshape(32, 128).T)
    shared["w2"] = np.asarray(inputs["L0_ffn_w2"], f32)
    shared["b2row"] = np.asarray(inputs["L0_ffn_b2"], f32).reshape(1, 1024)
    wqkv = np.asarray(inputs["L1_attn_Wqkv"], f32)
    shared["qkL"] = np.ascontiguousarray(
        wqkv[:, 0:2048].reshape(8, 128, 32, 64).transpose(2, 1, 0, 3)
        .reshape(32, 128, 512))
    shared["qprL"] = np.ascontiguousarray(
        wqkv[:, 0:1024].reshape(8, 128, 8, 128).transpose(2, 1, 0, 3)
        .reshape(8, 128, 1024))
    shared["kprL"] = np.ascontiguousarray(
        wqkv[:, 1024:2048].reshape(8, 128, 8, 128).transpose(2, 1, 0, 3)
        .reshape(8, 128, 1024))
    blk2r = np.zeros((2, 128), np.float32)
    blk2r[0, 0:64] = 1.0
    blk2r[1, 64:128] = 1.0
    shared["blk2r"] = blk2r
    shared["vL"] = np.ascontiguousarray(
        wqkv[:, 2048:3072].reshape(8, 128, 16, 64).transpose(2, 1, 0, 3)
        .reshape(16, 128, 512))
    shared["bqkvrow"] = np.asarray(inputs["L1_attn_bqkv"], f32).reshape(1, 3072)
    shared["wout"] = np.asarray(inputs["L1_attn_Wout"], f32)
    shared["boutrow"] = np.asarray(inputs["L1_attn_bout"], f32).reshape(1, 1024)
    om = np.asarray(inputs["L1_attn_omega"], f32)
    shared["omega"] = np.ascontiguousarray(om.transpose(1, 2, 0, 3).reshape(16, 64, 64))
    shared["qnodes"] = np.asarray(inputs["L1_attn_quad_nodes"], f32).reshape(1, 2)
    shared["qweights"] = np.asarray(inputs["L1_attn_quad_weights"], f32).reshape(1, 2)
    shared["rwT"] = np.ascontiguousarray(np.asarray(inputs["L1_ffn_router_w"], f32).T)
    shared["ebrow"] = np.asarray(inputs["L1_ffn_expert_bias"], f32).reshape(1, 8)
    mw1 = np.concatenate([np.asarray(inputs["L1_ffn_w1"], f32),
                          np.asarray(inputs["L1_ffn_sw1"], f32)], 0)
    mw3 = np.concatenate([np.asarray(inputs["L1_ffn_w3"], f32),
                          np.asarray(inputs["L1_ffn_sw3"], f32)], 0)
    mw2 = np.concatenate([np.asarray(inputs["L1_ffn_w2"], f32),
                          np.asarray(inputs["L1_ffn_sw2"], f32)], 0)
    shared["mw1L"] = np.ascontiguousarray(
        mw1.transpose(0, 2, 1).reshape(10, 8, 128, 16, 128)
        .transpose(0, 3, 2, 1, 4).reshape(10, 16, 128, 1024))
    shared["mw3L"] = np.ascontiguousarray(
        mw3.transpose(0, 2, 1).reshape(10, 8, 128, 16, 128)
        .transpose(0, 3, 2, 1, 4).reshape(10, 16, 128, 1024))
    shared["mw2T"] = np.ascontiguousarray(mw2.transpose(0, 2, 1))
    tri = np.triu(np.ones((128, 128), f32))
    shared["maskneg"] = np.where(tri > 0, 0.0, -1e30).astype(f32)
    shared["mask01"] = tri
    shared["identity"] = np.eye(128, dtype=f32)
    shared["onesmat"] = np.ones((128, 128), f32)
    sel = np.zeros((8, 128, 16), f32)
    for c in range(8):
        for f in range(128):
            sel[c, f, (c * 128 + f) // 64] = 1.0
    shared["sel16"] = sel
    blk = np.zeros((2, 64), f32)
    blk[0, 0:32] = 1.0
    blk[1, 32:64] = 1.0
    shared["blk232"] = blk
    per_core = []
    for c in range(8):
        b, h = c // 2, c % 2
        m = dict(shared)
        m["x"] = np.ascontiguousarray(x0[b, h * 2048:(h + 1) * 2048])
        m["px"] = np.ascontiguousarray(x0[b, 0:2048])
        m["oddmask"] = np.full((128, 1), float(h), f32)
        per_core.append(m)
    return per_core


class MK:
    def __init__(self, sched=True):
        nc = bass.Bass("TRN2", target_bir_lowering=False)
        self.nc = nc
        self.p = SchedProg(nc) if sched else Prog(nc)
        self.ctx = ExitStack()
        self._mk_io()
        self._mk_sbuf()
        p = self.p
        self.s_ld = p.sem("ld")
        self.s_w = p.sem("w")
        self.s_w2 = p.sem("w2")
        self.s_io = p.sem("io")
        self.s_row = p.sem("rowq")
        self.s_st = p.sem("st")

    def _in(self, name, shape):
        return self.nc.dram_tensor(name, list(shape), F32, kind="ExternalInput").ap()

    def _mk_io(self):
        i = self._in
        self.i_x = i("x", (T, D))
        self.i_px = i("px", (T, D))
        self.i_oddmask = i("oddmask", (128, 1))
        self.i_wxT = i("wxT", (D, 136))
        self.i_dtbias = i("dtbias", (1, 8))
        self.i_alog = i("alog", (1, 8))
        self.i_dvec = i("dvec", (1, 8))
        self.i_w1L = i("w1L", (32, 128, 1024))
        self.i_b1col = i("b1col", (128, 32))
        self.i_w2 = i("w2", (4096, D))
        self.i_b2row = i("b2row", (1, D))
        self.i_qkL = i("qkL", (32, 128, 512))
        self.i_qprL = i("qprL", (8, 128, 1024))
        self.i_kprL = i("kprL", (8, 128, 1024))
        self.i_blk2r = i("blk2r", (2, 128))
        self.i_vL = i("vL", (16, 128, 512))
        self.i_bqkvrow = i("bqkvrow", (1, 3072))
        self.i_wout = i("wout", (D, D))
        self.i_boutrow = i("boutrow", (1, D))
        self.i_omega = i("omega", (16, 64, 64))
        self.i_qnodes = i("qnodes", (1, 2))
        self.i_qweights = i("qweights", (1, 2))
        self.i_rwT = i("rwT", (D, 8))
        self.i_ebrow = i("ebrow", (1, 8))
        self.i_mw1L = i("mw1L", (NE, 16, 128, 1024))
        self.i_mw3L = i("mw3L", (NE, 16, 128, 1024))
        self.i_mw2T = i("mw2T", (NE, HID, D))
        self.i_maskneg = i("maskneg", (128, 128))
        self.i_mask01 = i("mask01", (128, 128))
        self.i_identity = i("identity", (128, 128))
        self.i_onesmat = i("onesmat", (128, 128))
        self.i_sel16 = i("sel16", (8, 128, 16))
        self.i_blk = i("blk232", (2, 64))
        self.o_out = self.nc.dram_tensor("out", [T, D], F32, kind="ExternalOutput").ap()
        self.d_cum = self.nc.dram_tensor("d_cum", [8, T], F32).ap()
        self.d_z = self.nc.dram_tensor("d_z", [8, T], F32).ap()
        self.d_rn = self.nc.dram_tensor("d_rn", [16, T], F32).ap()
        self.d_cT = self.nc.dram_tensor("d_cT", [16, 160], F32).ap()
        self.dumps = {}

    def dump_dram(self, name, shape):
        if name not in self.dumps:
            self.dumps[name] = self.nc.dram_tensor(
                "dump_" + name, list(shape), F32, kind="ExternalOutput").ap()
        return self.dumps[name]

    def _sb(self, name, shape, dt):
        return self.ctx.enter_context(self.nc.sbuf_tensor(name, list(shape), dt))

    def _mk_sbuf(self):
        sb = self._sb
        self.X = sb("X", (128, NT * 1024), F32)
        self.XF = sb("XF", (128, 8 * T), BF16)
        self.S16 = sb("S16", (128, 8192), BF16)
        self.S32 = sb("S32", (128, 768), F32)
        self.WBUF = sb("WBUF", (128, 20480), BF16)
        self.CONST = sb("CONST", (128, 768), F32)
        self.CB16 = sb("CB16", (128, 512), BF16)
        self.SMALL = sb("SMALL", (128, 512), F32)
        self.ROW2A = sb("ROW2A", (2, 1024), F32)
        self.ROW2B = sb("ROW2B", (2, 1024), F32)
        self.ROWZ = sb("ROWZ", (1, 1024), F32)
        self.ZTM = sb("ZTM", (128, 128), F32)
        self.ROWD2 = sb("ROWD2", (8, T), F32)
        self.STATE = sb("STATE", (64, NH0 * HD0), F32)
        self.SINIT = sb("SINIT", (64, NH0 * HD0), F32)
        self.KVI = sb("KVI", (64, NH1 * HD1), F32)
        self.KS = sb("KS", (64, NH1), F32)
        self.KSI = sb("KSI", (64, NH1), F32)
        self.KV = self.STATE
        self.BROWS = sb("BROWS", (1, 2080), BF16)
        self.WX = sb("WX", (128, 8 * 136), BF16)
        self.OM = sb("OM", (64, 16 * 64), BF16)
        self.B1C = sb("B1C", (128, 32), F32)
        self.DROW = sb("DROW", (1, 8), F32)
        self.BLK = sb("BLK", (2, 64), F32)
        self.PSTAT = sb("PSTAT", (128, 96), F32)
        self.RSTAT = sb("RSTAT", (128, 640), F32)
        self.psum = [self.ctx.enter_context(
            self.nc.psum_tensor(f"ps{i}", [128, 512], F32)) for i in range(7)]
        self.psumb = self.ctx.enter_context(
            self.nc.psum_tensor("psb", [128, 1024], BF16))
        self.p.psum_names = {f"ps{i}" for i in range(7)} | {"psb"}

    def mm(self, out, lhsT, rhs, start=True, stop=True):
        pe = "single" if (start and stop) else (
            "start" if start else ("stop" if stop else "mid"))
        self.p.op("tensor",
                  lambda t, o=out, l=lhsT, r=rhs, st=start, sp=stop:
                  t.matmul(o, l, r, start=st, stop=sp),
                  r=[lhsT, rhs], w=[out], pe=pe)

    def tr(self, out, in_, n):
        if in_.tensor.dtype == BF16:
            ident = self.cb_ident[0:n, 0:n]
        else:
            ident = self.c_ident[0:n, 0:n]
        self.p.op("tensor",
                  lambda t, o=out, i=in_, idn=ident: t.transpose(o, i, idn),
                  r=[in_, ident], w=[out], pe="single")

    def setup(self):
        p = self.p
        C, S = self.CONST, self.SMALL
        self.c_ident = C[:, 0:128]
        self.c_maskneg = C[:, 128:256]
        self.c_mask01 = C[:, 256:384]
        self.c_ones = C[:, 384:512]
        self.c_dbc1 = C[:, 512:520]
        self.c_sel16 = C[:, 520:648]
        self.cb_ones = self.CB16[:, 0:128]
        self.cb_ident = self.CB16[:, 128:256]
        self.br_b2 = self.BROWS[0:1, 0:1024]
        self.br_out = self.BROWS[0:1, 1024:2048]
        self.br_eb = self.BROWS[0:1, 2048:2056]
        self.br_dtb = self.BROWS[0:1, 2056:2064]
        self.sm_dtsp = S[:, 0:128]
        self.sm_wtm = S[:, 128:256]
        self.sm_zlast = S[0:64, 256:257]
        self.sm_wtmp = S[0:8, 260:388]
        self.sm_arow = S[0:1, 388:396]
        self.sm_acol = S[0:8, 396:397]
        self.sm_qrow = S[0:1, 400:402]
        self.sm_qwrow = S[0:1, 402:404]
        self.sm_s2col = S[0:64, 404:405]
        self.sm_negscol = S[0:64, 405:406]
        self.sm_lnccol = S[0:64, 406:407]
        self.sm_oddm = S[:, 408:409]
        self.sm_rn = S[:, 412:428]
        self.sm_den = S[:, 428:444]
        self.sm_c10 = S[:, 444:454]
        self.sm_m1 = S[:, 456:457]
        self.sm_m2 = S[:, 457:458]
        self.sm_lg = S[:, 460:470]
        ld = self.s_ld
        w = self.s_w
        d = lambda out, in_: p.dma("sync", out, in_, ld)
        dc = lambda out, in_: p.dma("gpsimd", out, in_, w)
        d(self.c_ident, self.i_identity)
        d(self.c_maskneg, self.i_maskneg)
        d(self.c_mask01, self.i_mask01)
        d(self.c_ones, self.i_onesmat)
        d(self.c_sel16.rearrange("p (c s) -> p c s", c=8),
          self.i_sel16.rearrange("c p s -> p c s"))
        dc(self.cb_ones, self.i_onesmat[:, 0:128])
        dc(self.cb_ident, self.i_identity)
        dc(self.br_b2, self.i_b2row)
        dc(self.br_out, self.i_boutrow)
        dc(self.br_eb, self.i_ebrow)
        dc(self.br_dtb, self.i_dtbias)
        for dch in range(8):
            dc(self.WX[:, dch * 136:(dch + 1) * 136],
               self.i_wxT[dch * 128:(dch + 1) * 128, :])
        dc(self.OM.rearrange("d (h m) -> d h m", h=16),
           self.i_omega.rearrange("h d m -> d h m"))
        self.OM2 = self._sb("OM2", (128, 16 * 64), BF16)
        dc(self.OM2[64:128, :].rearrange("d (h m) -> d h m", h=16),
           self.i_omega.rearrange("h d m -> d h m"))
        self.CBY = self._sb("CBY", (128, 2), BF16)
        self.cb_sel2 = self.CBY[:, :]
        dc(self.cb_sel2, self.i_sel16[0][:, 0:2])
        self.cb_blk2r = self.CB16[0:2, 458:586] if False else None
        self.CBX = self._sb("CBX", (2, 128), BF16)
        dc(self.CBX[:, :], self.i_blk2r)
        self.RWsb = self._sb("RWsb", (128, 64), BF16)
        for dch in range(8):
            dc(self.RWsb[:, dch * 8:(dch + 1) * 8],
               self.i_rwT[dch * 128:(dch + 1) * 128, :])
        d(self.B1C[:, :], self.i_b1col)
        d(self.sm_arow, self.i_alog)
        d(self.sm_qrow, self.i_qnodes)
        d(self.sm_qwrow, self.i_qweights)
        d(self.sm_oddm, self.i_oddmask)
        d(self.DROW[0:1, :], self.i_dvec)
        d(self.BLK[:, :], self.i_blk)
        self.constvals = {}
        for k, val in enumerate([1e-5, 1.0, -1.0]):
            ap = S[:, 257 + k: 258 + k]
            p.op("vector", lambda v, ap=ap, val=val: v.memset(ap, val), w=[ap])
            self.nc.const_aps.register(F32, val, ap)
            self.constvals[val] = ap
        p.op("vector", lambda v: v.memset(self.ROW2A[:, :], 1.0),
             w=[self.ROW2A[:, :]])
        p.op("vector", lambda v: v.memset(self.ROW2B[:, :], -1.0),
             w=[self.ROW2B[:, :]])
        p.op("vector", lambda v: v.memset(self.SINIT[:, :], 0.0),
             w=[self.SINIT[:, :]])
        bD = self.psum[5]
        self.mm(bD[:, 0:8], self.c_ones[0:1, 0:128], self.DROW[0:1, :])
        p.op("vector", lambda v: v.tensor_scalar(
            self.c_dbc1, bD[:, 0:8], 1.0, None, ALU.add),
            r=[bD[:, 0:8]], w=[self.c_dbc1])
        t8 = self.sm_wtmp
        p.op("scalar", lambda s: s.activation(t8[0:1, 0:8], self.sm_arow, AF.Exp),
             r=[self.sm_arow], w=[t8[0:1, 0:8]])
        p.op("vector", lambda v: v.tensor_scalar(
            t8[0:1, 8:16], t8[0:1, 0:8], -1.0, None, ALU.mult),
            r=[t8[0:1, 0:8]], w=[t8[0:1, 8:16]])
        bA = self.psum[6]
        self.tr(bA[0:8, 0:1], t8[0:1, 8:16], 1)
        p.op("vector", lambda v: v.tensor_copy(self.sm_acol, bA[0:8, 0:1]),
             r=[bA[0:8, 0:1]], w=[self.sm_acol])
        q = self.sm_qrow
        t0 = S[0:1, 470:472]; t1 = S[0:1, 472:474]; t2 = S[0:1, 474:476]
        t3 = S[0:1, 476:478]; t4 = S[0:1, 478:480]; t5 = S[0:1, 480:482]
        p.op("vector", lambda v: v.tensor_scalar(t0, q, 0.0, None, ALU.max),
             r=[q], w=[t0])
        p.op("scalar", lambda s: s.activation(t1, t0, AF.Sqrt, scale=2.0),
             r=[t0], w=[t1])
        p.op("vector", lambda v: v.tensor_scalar(t3, self.sm_qwrow, 0.0, None,
             ALU.max), r=[self.sm_qwrow], w=[t3])
        p.op("scalar", lambda s: s.activation(t4, t3, AF.Ln, scale=1.0 / 32.0),
             r=[t3], w=[t4])
        p.op("vector", lambda v: v.tensor_scalar(t5, q, -1.0, None, ALU.mult),
             r=[q], w=[t5])
        p.op("vector", lambda v: v.tensor_tensor(t2, t5, t4, ALU.add),
             r=[t5, t4], w=[t2])
        bB = self.psum[6]
        self.tr(bB[0:2, 0:1], t1, 1)
        self.tr(bB[0:2, 1:2], t2, 1)
        self.tr(bB[0:2, 2:3], t5, 1)
        vals = S[0:2, 482:485]
        p.op("vector", lambda v: v.tensor_copy(vals, bB[0:2, 0:3]),
             r=[bB[0:2, 0:3]], w=[vals])
        bC = self.psum[6]
        self.mm(bC[0:64, 0:3], self.BLK[:, :], vals)
        p.op("vector", lambda v: v.tensor_copy(self.sm_s2col, bC[0:64, 0:1]),
             r=[bC[0:64, 0:1]], w=[self.sm_s2col])
        p.op("vector", lambda v: v.tensor_copy(self.sm_lnccol, bC[0:64, 1:2]),
             r=[bC[0:64, 1:2]], w=[self.sm_lnccol])
        p.op("vector", lambda v: v.tensor_copy(self.sm_negscol, bC[0:64, 2:3]),
             r=[bC[0:64, 2:3]], w=[self.sm_negscol])
        self.p.mark_ro("identity", "maskneg", "mask01", "onesmat", "sel16",
                       "blk232", "b1col", "alog", "qnodes", "qweights",
                       "oddmask", "dvec", "b2row", "boutrow", "ebrow", "dtbias",
                       "wxT", "omega", "w1L", "w2", "qkL", "vL", "bqkvrow",
                       "wout", "rwT", "mw1L", "mw3L", "mw2T", "x", "px",
                       "CONST", "BROWS", "WX", "OM", "B1C", "DROW",
                       "BLK")

    def load_stream(self, src):
        for t in range(NT):
            self.p.dma("sync", self.X[:, t * 1024:(t + 1) * 1024],
                       src[t * 128:(t + 1) * 128, :], self.s_ld)

    def fmconv(self):
        p = self.p
        for k, (t, dc) in enumerate([(t, dc) for t in range(NT) for dc in range(8)]):
            b = self.psum[k % 4]
            xs = self.X[:, t * 1024 + dc * 128: t * 1024 + (dc + 1) * 128]
            self.tr(b[:, 0:128], xs, 128)
            dst = self.XF[:, dc * T + t * 128: dc * T + (t + 1) * 128]
            if k % 2 == 0:
                p.op("vector", lambda v, d=dst, bb=b: v.tensor_copy(d, bb[:, 0:128]),
                     r=[b[:, 0:128]], w=[dst])
            else:
                p.op("scalar", lambda s, d=dst, bb=b: s.copy(d, bb[:, 0:128]),
                     r=[b[:, 0:128]], w=[dst])

    def l0_prep(self, sub=99):
        p = self.p
        S = self.SMALL
        self.B_fm = self.S16[0:64, 0:2048]
        self.C_fm = self.S16[0:64, 2048:4096]
        self.B_tmall = self.S16[:, 4096:5120]
        self.SBF = self.S16[0:64, 5120:6144]
        bB, bC = self.psum[3], self.psum[4]
        for tb in range(4):
            for which, bank, co in (("B", bB, 8), ("C", bC, 72)):
                for dc in range(8):
                    self.mm(bank[0:64, 0:512],
                            self.WX[:, dc * 136 + co: dc * 136 + co + 64],
                            self.XF[:, dc * T + tb * 512: dc * T + (tb + 1) * 512],
                            start=(dc == 0), stop=(dc == 7))
                dst = (self.B_fm if which == "B" else self.C_fm)[:, tb * 512:(tb + 1) * 512]
                p.op("vector", lambda v, d=dst, bb=bank: v.tensor_copy(
                    d, bb[0:64, 0:512]), r=[bank[0:64, 0:512]], w=[dst])
        if sub < 1:
            return
        import os
        nch_lim = int(os.environ.get("MKCH", NCH))
        pieces = os.environ.get("MKPIECE", "abcd")
        bT = self.psum[5]
        bX = self.psum[6]
        sp1 = S[:, 488:496]
        sp2 = S[:, 496:504]
        for c in range(min(NCH, nch_lim)):
            for dc in range(8):
                self.mm(bT[:, 0:72],
                        self.XF[:, dc * T + c * 128: dc * T + (c + 1) * 128],
                        self.WX[:, dc * 136: dc * 136 + 72],
                        start=(dc == 0), stop=False)
            self.mm(bT[:, 0:8], self.cb_ones[0:1, :], self.br_dtb,
                    start=False, stop=True)
            if "a" not in pieces:
                p.op("vector", lambda v: v.tensor_copy(
                    S[:, 504:505], bT[:, 0:1]), r=[bT[:, 0:1]], w=[S[:, 504:505]])
                continue
            p.op("vector", lambda v, c=c: v.tensor_copy(
                self.B_tmall[:, c * 64:(c + 1) * 64], bT[:, 8:72]),
                r=[bT[:, 8:72]], w=[self.B_tmall[:, c * 64:(c + 1) * 64]])
            if "b" not in pieces:
                continue
            lvl = int(os.environ.get("MKB", "5"))
            if lvl >= 1:
                fn_ = {"sq": AF.Square, "abs": AF.Abs, "exp": AF.Exp,
                       "relu": AF.Relu}[os.environ.get("MKABS", "abs")]
                p.op("scalar", lambda s, fn_=fn_: s.activation(
                    sp1, bT[:, 0:8], fn_), r=[bT[:, 0:8]], w=[sp1])
            if lvl >= 2:
                p.op("scalar", lambda s: s.activation(sp1, sp1, AF.Exp, scale=-1.0),
                     r=[sp1], w=[sp1])
            if lvl >= 3:
                p.op("scalar", lambda s: s.activation(sp1, sp1, AF.Ln, bias=1.0),
                     r=[sp1], w=[sp1])
            if lvl >= 4:
                p.op("scalar", lambda s: s.activation(sp2, bT[:, 0:8], AF.Relu),
                     r=[bT[:, 0:8]], w=[sp2])
            dst = self.sm_dtsp[:, c * 8:(c + 1) * 8]
            if lvl >= 5:
                p.op("vector", lambda v, d=dst: v.tensor_tensor(d, sp1, sp2, ALU.add),
                     r=[sp1, sp2], w=[dst])
            else:
                p.op("vector", lambda v, d=dst: v.tensor_copy(d, sp1),
                     r=[sp1], w=[dst])
            if "c" not in pieces:
                continue
            self.tr(bX[0:8, 0:128], dst, 128)
            rd = self.ROWD2[0:8, c * 128:(c + 1) * 128]
            p.op("vector", lambda v, rd=rd: v.tensor_copy(rd, bX[0:8, 0:128]),
                 r=[bX[0:8, 0:128]], w=[rd])
        if sub < 2:
            return
        RD = self.ROWD2[0:8, :]
        p.op("vector", lambda v: v.tensor_scalar(
            RD, RD, self.sm_acol, None, ALU.mult), r=[RD, self.sm_acol], w=[RD])
        for c in range(NCH):
            seg = self.ROWD2[0:8, c * 128:(c + 1) * 128]
            init = 0.0 if c == 0 else self.ROWD2[0:8, c * 128 - 1:c * 128]
            rr = [seg, self.c_ones[0:8, 0:128]] + ([init] if c > 0 else [])
            p.op("vector", lambda v, seg=seg, init=init: v.tensor_tensor_scan(
                seg, seg, self.c_ones[0:8, 0:128], init, ALU.add, ALU.mult),
                r=rr, w=[seg])
        if sub < 3:
            return
        p.dma("sync", self.d_cum, RD, self.s_io)
        if sub < 4:
            return
        for c in range(NCH):
            wt = self.sm_wtmp
            p.op("vector", lambda v, c=c: v.tensor_scalar(
                wt, self.ROWD2[0:8, c * 128:(c + 1) * 128],
                self.ROWD2[0:8, c * 128 + 127:c * 128 + 128], None, ALU.subtract),
                r=[self.ROWD2[0:8, c * 128:(c + 1) * 128]], w=[wt])
            p.op("scalar", lambda s: s.activation(wt, wt, AF.Exp, scale=-1.0),
                 r=[wt], w=[wt])
            bW = self.psum[6]
            self.tr(bW[:, 0:8], wt, 8)
            dst = self.sm_wtm[:, c * 8:(c + 1) * 8]
            p.op("vector", lambda v, d=dst: v.tensor_copy(d, bW[:, 0:8]),
                 r=[bW[:, 0:8]], w=[dst])
        if sub < 5:
            return
        for c in range(NCH - 1, 0, -1):
            seg = self.ROWD2[0:8, c * 128:(c + 1) * 128]
            prev = self.ROWD2[0:8, c * 128 - 1:c * 128]
            p.op("vector", lambda v, seg=seg, prev=prev: v.tensor_scalar(
                seg, seg, prev, None, ALU.subtract), r=[seg, prev], w=[seg])
        p.op("scalar", lambda s: s.activation(RD, RD, AF.Exp), r=[RD], w=[RD])
        p.dma("sync", self.d_z, RD, self.s_io)
        for c in range(NCH):
            bZt = self.psum[(2, 5, 6)[c % 3]]
            self.tr(bZt[:, 0:8], self.ROWD2[0:8, c * 128:(c + 1) * 128], 8)
            dst = self.ZTM[:, c * 8:(c + 1) * 8]
            p.op("vector", lambda v, d=dst, b=bZt: v.tensor_copy(d, b[:, 0:8]),
                 r=[bZt[:, 0:8]], w=[dst])

    def l0_mixer(self):
        p = self.p
        p.op("vector", lambda v: v.tensor_copy(self.STATE[:, :], self.SINIT[:, :]),
             r=[self.SINIT[:, :]], w=[self.STATE[:, :]])
        p.op("vector", lambda v: v.tensor_copy(self.SBF, self.STATE[:, :]),
             r=[self.STATE[:, :]], w=[self.SBF])
        XhP = [self.S16[:, 6144:6272], self.S16[:, 6656:6784]]
        XhwP = [self.S16[:, 6272:6400], self.S16[:, 6784:6912]]
        MtP = [self.S16[:, 6400:6528], self.S16[:, 6912:7040]]
        CzP = [self.S16[0:64, 6528:6656], self.S16[0:64, 7040:7168]]
        EmatP = [self.S32[:, 0:128], self.S32[:, 128:256]]
        bDP = [self.psum[0], self.psum[6]]
        bG, bZ, bS = self.psum[1], self.psum[2], self.psum[5]
        bY = [self.psum[3], self.psum[4]]
        for c in range(NCH):
            p.dma("sync", self.ROW2A[0:1, :].rearrange("p (h t) -> p h t", h=8),
                  self.d_cum[:, c * 128:(c + 1) * 128], self.s_row)
            p.dma("sync", self.ROW2B[1:2, :].rearrange("p (h t) -> p h t", h=8),
                  self.d_cum[:, c * 128:(c + 1) * 128], self.s_row)
            p.dma("sync", self.ROWZ[0:1, :].rearrange("p (h t) -> p h t", h=8),
                  self.d_z[:, c * 128:(c + 1) * 128], self.s_row)
            Bc = self.B_fm[:, c * 128:(c + 1) * 128]
            Cc = self.C_fm[:, c * 128:(c + 1) * 128]
            self.mm(bG[:, 0:128], Bc, Cc)

            def stage_a(h, c=c, Cc=Cc):
                hs = slice(h * 128, (h + 1) * 128)
                Xh, Xhw = XhP[h % 2], XhwP[h % 2]
                Mt, Emat = MtP[h % 2], EmatP[h % 2]
                bD = bDP[h % 2]
                self.mm(bD[:, 0:128], self.ROW2A[:, hs], self.ROW2B[:, hs])
                zbc = bZ[0:64, (h % 2):(h % 2) + 1]
                self.mm(zbc, self.c_ones[0:1, 0:64],
                        self.ROWZ[0:1, h * 128 + 127:h * 128 + 128])
                p.op("vector", lambda v, Emat=Emat, bD=bD: v.scalar_tensor_tensor(
                    Emat, bD[:, 0:128], 0.0, self.c_maskneg, ALU.min, ALU.add),
                    r=[bD[:, 0:128], self.c_maskneg], w=[Emat])
                p.op("scalar", lambda s, Emat=Emat: s.activation(Emat, Emat, AF.Exp),
                     r=[Emat], w=[Emat])
                p.op("vector", lambda v, Mt=Mt, Emat=Emat: v.tensor_tensor(
                    Mt, bG[:, 0:128], Emat, ALU.mult),
                    r=[bG[:, 0:128], Emat], w=[Mt])
                zl = self.SMALL[0:64, 256:257] if h % 2 == 0 else \
                    self.SMALL[0:64, 260:261]
                p.op("vector", lambda v, zl=zl, zbc=zbc: v.tensor_copy(
                    zl, zbc), r=[zbc], w=[zl])
                xsl = self.X[:, c * 1024 + h * 128: c * 1024 + (h + 1) * 128]
                dcol = self.sm_dtsp[:, c * 8 + h: c * 8 + h + 1]
                wcol = self.sm_wtm[:, c * 8 + h: c * 8 + h + 1]
                p.op("vector", lambda v, xsl=xsl, dcol=dcol, Xh=Xh: v.tensor_scalar(
                    Xh, xsl, dcol, None, ALU.mult), r=[xsl, dcol], w=[Xh])
                p.op("vector", lambda v, wcol=wcol, Xh=Xh, Xhw=Xhw: v.tensor_scalar(
                    Xhw, Xh, wcol, None, ALU.mult), r=[Xh, wcol], w=[Xhw])

            def stage_b(h, c=c):
                hs = slice(h * 128, (h + 1) * 128)
                Xh, Xhw = XhP[h % 2], XhwP[h % 2]
                Mt = MtP[h % 2]
                Cc2 = self.C_fm[:, c * 128:(c + 1) * 128]
                zl = self.SMALL[0:64, 256:257] if h % 2 == 0 else \
                    self.SMALL[0:64, 260:261]
                by = bY[h % 2]
                p.op("tensor", lambda t, o=by[:, 0:128], l=Mt, rr=Xh:
                     t.matmul(o, l, rr, start=True, stop=True),
                     r=[Mt, Xh], w=[by[:, 0:128]], pe="start")
                p.op("tensor",
                     lambda t, o=by[:, 128:256], l=Cc2, rr=self.SBF[:, hs]:
                     t.matmul(o, l, rr, start=True, stop=True),
                     r=[Cc2, self.SBF[:, hs]], w=[by[:, 128:256]], pe="stop")
                self.mm(bS[0:64, (h % 2) * 128:(h % 2) * 128 + 128],
                        self.B_tmall[:, c * 64:(c + 1) * 64], Xhw)
                bSs = bS[0:64, (h % 2) * 128:(h % 2) * 128 + 128]
                xsl = self.X[:, c * 1024 + h * 128: c * 1024 + (h + 1) * 128]
                dco = self.c_dbc1[:, h:h + 1]
                zc = self.ZTM[:, c * 8 + h:c * 8 + h + 1]
                p.op("vector", lambda v, xsl=xsl, dco=dco, by=by:
                     v.scalar_tensor_tensor(xsl, xsl, dco, by[:, 0:128],
                                            ALU.mult, ALU.add),
                     r=[xsl, dco, by[:, 0:128]], w=[xsl])
                p.op("vector", lambda v, xsl=xsl, zc=zc, by=by:
                     v.scalar_tensor_tensor(xsl, by[:, 128:256], zc, xsl,
                                            ALU.mult, ALU.add),
                     r=[by[:, 128:256], zc, xsl], w=[xsl])
                st = self.STATE[:, hs]
                p.op("vector", lambda v, st=st, zl=zl, bSs=bSs:
                     v.scalar_tensor_tensor(st, st, zl, bSs, ALU.mult, ALU.add),
                     r=[st, zl, bSs], w=[st])
                p.op("scalar", lambda s, st=st, hs=hs: s.copy(
                    self.SBF[:, hs], st), r=[st], w=[self.SBF[:, hs]])

            stage_a(0)
            for h in range(NH0):
                if h + 1 < NH0:
                    stage_a(h + 1)
                stage_b(h)

    def postln(self):
        p = self.p
        PS = self.PSTAT
        bQP = [self.psum[6], self.psum[0]]
        SUM = PS[:, 0:16]
        SQA = PS[:, 16:32]
        SQB = PS[:, 32:48]
        NM = PS[:, 48:64]
        RS = PS[:, 64:80]
        VAR = PS[:, 80:96]
        for t in range(NT):
            bQ = bQP[t % 2]
            xs = self.X[:, t * 1024:(t + 1) * 1024]
            x0 = self.X[:, t * 1024:t * 1024 + 512]
            x1 = self.X[:, t * 1024 + 512:(t + 1) * 1024]
            sumc = PS[:, t:t + 1]
            sqa = PS[:, 16 + t:17 + t]
            sqb = PS[:, 32 + t:33 + t]
            p.op("vector", lambda v, xs=xs, sumc=sumc: v.reduce_sum(
                sumc, xs, axis=AX.X), r=[xs], w=[sumc])
            p.op("scalar", lambda s, x0=x0, bQ=bQ, sqa=sqa: s.activation(
                bQ[:, 0:512], x0, AF.Square, accum_out=sqa),
                r=[x0], w=[bQ[:, 0:512], sqa])
            p.op("scalar", lambda s, x1=x1, bQ=bQ, sqb=sqb: s.activation(
                bQ[:, 0:512], x1, AF.Square, accum_out=sqb),
                r=[x1], w=[bQ[:, 0:512], sqb])
        p.op("vector", lambda v: v.tensor_tensor(SQA, SQA, SQB, ALU.add),
             r=[SQA, SQB], w=[SQA])
        p.op("vector", lambda v: v.tensor_scalar(
            NM, SUM, -1.0 / 1024.0, None, ALU.mult), r=[SUM], w=[NM])
        p.op("vector", lambda v: v.tensor_tensor(VAR, NM, NM, ALU.mult),
             r=[NM], w=[VAR])
        p.op("vector", lambda v: v.tensor_scalar(
            SQA, SQA, 1.0 / 1024.0, None, ALU.mult), r=[SQA], w=[SQA])
        p.op("vector", lambda v: v.tensor_sub(VAR, SQA, VAR),
             r=[SQA, VAR], w=[VAR])
        p.op("scalar", lambda s: s.activation(VAR, VAR, AF.Ln, bias=1e-5),
             r=[VAR, self.constvals[1e-5]], w=[VAR])
        p.op("scalar", lambda s: s.activation(RS, VAR, AF.Exp, scale=-0.5),
             r=[VAR], w=[RS])
        for t in range(NT):
            xs = self.X[:, t * 1024:(t + 1) * 1024]
            nm = PS[:, 48 + t:49 + t]
            rs = PS[:, 64 + t:65 + t]
            p.op("vector", lambda v, xs=xs, nm=nm, rs=rs: v.tensor_scalar(
                xs, xs, nm[:, 0:1], rs[:, 0:1], ALU.add, ALU.mult),
                r=[xs, nm, rs], w=[xs])

    def l0_ffn(self):
        p = self.p
        W1G = self.WBUF[:, 0:8192]
        W2G = self.WBUF[:, 8192:16384]
        H1 = self.S16[:, 0:8192]
        for g in range(4):
            for j in range(8):
                hc = g * 8 + j
                p.dma("gpsimd", W1G[:, j * 1024:(j + 1) * 1024],
                      self.i_w1L[hc], self.s_w)
                p.dma("gpsimd", W2G[:, j * 1024:(j + 1) * 1024],
                      self.i_w2[hc * 128:(hc + 1) * 128, :], self.s_w2)
            for tp2 in range(2):
                tbs = (2 * tp2, 2 * tp2 + 1)
                for j in range(8):
                    baP = (self.psum[0], self.psum[2]) if j % 2 == 0 else \
                        (self.psum[1], self.psum[3])
                    hc = g * 8 + j
                    for dc in range(8):
                        for ti, tb in enumerate(tbs):
                            self.mm(baP[ti][:, 0:512],
                                    W1G[:, j * 1024 + dc * 128: j * 1024 + (dc + 1) * 128],
                                    self.XF[:, dc * T + tb * 512: dc * T + (tb + 1) * 512],
                                    start=(dc == 0), stop=(dc == 7))
                    for ti in range(2):
                        h1d = H1[:, (ti * 8 + j) * 512:(ti * 8 + j + 1) * 512]
                        bc = self.B1C[:, hc:hc + 1]
                        p.op("scalar", lambda s, h1d=h1d, ba=baP[ti], bc=bc:
                             s.activation(h1d, ba[:, 0:512], AF.Gelu, bias=bc),
                             r=[baP[ti][:, 0:512], bc], w=[h1d])
                for ti, tb in enumerate(tbs):
                    for q in range(4):
                        tt = tb * 4 + q
                        for nb in range(2):
                            bo = self.psum[4 + (q * 2 + nb) % 3]
                            for j in range(8):
                                last = (j == 7) and (g != 0)
                                self.mm(bo[:, 0:512],
                                        H1[:, (ti * 8 + j) * 512 + q * 128:
                                          (ti * 8 + j) * 512 + q * 128 + 128],
                                        W2G[:, j * 1024 + nb * 512: j * 1024 + (nb + 1) * 512],
                                        start=(j == 0), stop=last)
                            if g == 0:
                                self.mm(bo[:, 0:512], self.cb_ones[0:1, 0:128],
                                        self.br_b2[0:1, nb * 512:(nb + 1) * 512],
                                        start=False, stop=True)
                            xs = self.X[:, tt * 1024 + nb * 512:
                                        tt * 1024 + (nb + 1) * 512]
                            p.op("vector", lambda v, xs=xs, bo=bo: v.tensor_tensor(
                                xs, xs, bo[:, 0:512], ALU.add),
                                r=[xs, bo[:, 0:512]], w=[xs])

    def run_l0(self, src, save_sinit=False):
        p = self.p
        self.load_stream(src)
        self.fmconv()
        self.l0_prep()
        self.l0_mixer()
        if save_sinit:
            p.op("vector", lambda v: v.tensor_scalar(
                self.SINIT[:, :], self.STATE[:, :], self.sm_oddm[0:64, 0:1],
                None, ALU.mult),
                r=[self.STATE[:, :], self.sm_oddm], w=[self.SINIT[:, :]])
        self.postln()
        self.fmconv()
        self.l0_ffn()
        self.postln()

    def dump_x(self, name):
        ap = self.dump_dram(name, (T, D))
        for t in range(NT):
            self.p.dma("sync", ap[t * 128:(t + 1) * 128, :],
                       self.X[:, t * 1024:(t + 1) * 1024], self.s_st)

    def dump_dram_copy(self, name, src_ap, shape):
        ap = self.dump_dram(name, shape)
        self.p.dma("sync", ap, src_ap, self.s_st)

    def dump_sbuf(self, name, sb_ap, shape):
        ap = self.dump_dram(name, shape)
        self.p.dma("sync", ap, sb_ap, self.s_st)

    def store_out(self):
        for t in range(NT):
            self.p.dma("sync", self.o_out[t * 128:(t + 1) * 128, :],
                       self.X[:, t * 1024:(t + 1) * 1024], self.s_st)

    def finalize(self):
        if hasattr(self.p, "schedule_and_emit"):
            self.p.schedule_and_emit()
        self.p.join_all()
        self.p.raw_wait("sync", self.s_st, self.s_st.n)
        self.p.build_block(self.ctx)
        self.ctx.close()
        return self.nc

    def l1_attn(self, kv_only=False):
        p = self.p
        S16, S32, S = self.S16, self.S32, self.SMALL
        W = self.WBUF
        AOF = S16[:, 0:4096]
        qfP = [S16[:, 4096:4608], W[:, 12288:12800]]
        kfP = [S16[:, 4608:5120], W[:, 12800:13312]]
        ktP = [S16[:, 5120:5632], W[:, 13312:13824]]
        vtP = [S16[:, 5632:6144], W[:, 13824:14336]]
        qnP = [W[:, 10240:10752], W[:, 10752:11264]]
        q2P = [W[:, 11264:11776], W[:, 11776:12288]]
        MtPl = [S16[:, 7696:7824], W[:, 14848:14976]]
        aoP = [S16[:, 7824:7888], W[:, 14976:15040]]
        rnbP = [W[0:2, 15040:15552], W[0:2, 15552:16064]]
        argP = [S32[:, 256:768], S32[:, 0:512]]
        ssP = [S32[0:2, 256:768], S32[0:2, 0:512]]
        bqP = [S[:, 490:491], S[:, 470:471]]
        bkP = [S[:, 491:492], S[:, 471:472]]
        rdP = [S[:, 492:493], S[:, 476:477]]
        dmP = [S[:, 493:494], S[:, 477:478]]
        KVBF = S16[:, 6656:7680]
        KSBF = S16[:, 7680:7696]
        WQP = [W[:, 16384:18432], W[:, 18432:20480]]
        WVP = [W[:, 8192:9216], W[:, 9216:10240]]
        BVP = [self.CB16[0:1, 256:384], self.CB16[0:1, 384:512]]
        WO = W[:, 0:8192]
        b0, b1, b2, b3, b4, b5, b6 = self.psum
        bb = self.psumb
        featB = [b0, b2]
        YB = [b4, b6]
        if kv_only:
            p.op("vector", lambda v: v.memset(self.KV[:, :], 0.0),
                 w=[self.KV[:, :]])
            p.op("vector", lambda v: v.memset(self.KS[:, :], 0.0),
                 w=[self.KS[:, :]])
        else:
            p.op("vector", lambda v: v.tensor_copy(self.KV[:, :], self.KVI[:, :]),
                 r=[self.KVI[:, :]], w=[self.KV[:, :]])
            p.op("vector", lambda v: v.tensor_copy(self.KS[:, :], self.KSI[:, :]),
                 r=[self.KSI[:, :]], w=[self.KS[:, :]])
            for g in range(8):
                p.dma("gpsimd", WO[:, g * 1024:(g + 1) * 1024],
                      self.i_wout[g * 128:(g + 1) * 128, :], self.s_w)
        for hi in range(2):
            rb = slice(hi * 64, hi * 64 + 64)
            cv = [c for c in range(16) if c % 2 == hi]
            for h in cv:
                p.op("vector", lambda v, rb=rb, h=h: v.tensor_copy(
                    KVBF[rb, h * 64:(h + 1) * 64],
                    self.KV[:, h * 64:(h + 1) * 64]),
                    r=[self.KV[:, h * 64:(h + 1) * 64]],
                    w=[KVBF[rb, h * 64:(h + 1) * 64]])
                p.op("vector", lambda v, rb=rb, h=h: v.tensor_copy(
                    KSBF[rb, h:h + 1], self.KS[:, h:h + 1]),
                    r=[self.KS[:, h:h + 1]], w=[KSBF[rb, h:h + 1]])

        pairs = [(tb, hp) for tb in range(4) for hp in range(8)]

        def loads(idx):
            tb, hp = pairs[idx]
            par = idx % 2
            if not kv_only:
                p.dma("gpsimd", WQP[par][:, 0:1024], self.i_qprL[hp], self.s_w)
            p.dma("gpsimd", WQP[par][:, 1024:2048], self.i_kprL[hp], self.s_w)
            p.dma("gpsimd", WVP[par][:, 0:512], self.i_vL[2 * hp], self.s_w)
            p.dma("gpsimd", WVP[par][:, 512:1024], self.i_vL[2 * hp + 1], self.s_w)
            p.dma("sync", bqP[par],
                  self.i_bqkvrow[0:1, hp * 128:(hp + 1) * 128]
                  .rearrange("o f -> (o f)")[:, None], self.s_ld)
            p.dma("sync", bkP[par],
                  self.i_bqkvrow[0:1, 1024 + hp * 128: 1024 + (hp + 1) * 128]
                  .rearrange("o f -> (o f)")[:, None], self.s_ld)
            p.dma("gpsimd", BVP[par],
                  self.i_bqkvrow[0:1, 2048 + hp * 128: 2048 + (hp + 1) * 128],
                  self.s_w)

        def emit_features(idx):
            tb, hp = pairs[idx]
            par = idx % 2
            qn, q2 = qnP[par], q2P[par]
            rnb, argsb, ssb = rnbP[par], argP[par], ssP[par]
            bf = featB[par]
            WQ, WV, BV = WQP[par], WVP[par], BVP[par]
            for which in (("q",) if not kv_only else ()) + ("k",):
                wt = WQ[:, 0:1024] if which == "q" else WQ[:, 1024:2048]
                bcol = bqP[par] if which == "q" else bkP[par]
                featbuf = qfP[par] if which == "q" else kfP[par]
                bias = self.sm_lnccol if which == "q" else 0.0
                for dc in range(8):
                    self.mm(bf[:, 0:512], wt[:, dc * 128:(dc + 1) * 128],
                            self.XF[:, dc * T + tb * 512: dc * T + (tb + 1) * 512],
                            start=(dc == 0), stop=(dc == 7))
                p.op("vector", lambda v, bcol=bcol, qn=qn, bf=bf:
                     v.tensor_scalar(qn, bf[:, 0:512], bcol, None, ALU.add),
                     r=[bf[:, 0:512], bcol], w=[qn])
                p.op("scalar", lambda s, q2=q2, qn=qn: s.activation(
                    q2, qn, AF.Square), r=[qn], w=[q2])
                self.mm(bf[0:2, 0:512], self.cb_sel2, q2)
                p.op("vector", lambda v, ssb=ssb, bf=bf: v.tensor_scalar(
                    ssb, bf[0:2, 0:512], 1e-24, None, ALU.max),
                    r=[bf[0:2, 0:512]], w=[ssb])
                p.op("scalar", lambda s, ssb=ssb: s.activation(ssb, ssb, AF.Ln),
                     r=[ssb], w=[ssb])
                p.op("scalar", lambda s, rnb=rnb, ssb=ssb: s.activation(
                    rnb, ssb, AF.Exp, scale=-0.5), r=[ssb], w=[rnb])
                self.mm(b3[:, 0:512], self.CBX[:, :], rnb)
                p.op("vector", lambda v, qn=qn: v.tensor_tensor(
                    qn, qn, b3[:, 0:512], ALU.mult),
                    r=[qn, b3[:, 0:512]], w=[qn])
                for hi in range(2):
                    h = 2 * hp + hi
                    rb = slice(hi * 64, hi * 64 + 64)
                    om = self.OM[:, h * 64:(h + 1) * 64] if hi == 0 else \
                        self.OM2[64:128, h * 64:(h + 1) * 64]
                    bk = bf if hi == 0 else b3
                    self.mm(bk[0:64, 0:512], om, qn[rb, :])
                    arg = argsb[rb, :]
                    p.op("vector", lambda v, arg=arg, bk=bk: v.tensor_scalar(
                        arg, bk[0:64, 0:512], self.sm_s2col, self.sm_negscol,
                        ALU.mult, ALU.add),
                        r=[bk[0:64, 0:512], self.sm_s2col, self.sm_negscol],
                        w=[arg])
                    p.op("vector", lambda v, arg=arg: v.tensor_scalar(
                        arg, arg, -20.0, 20.0, ALU.max, ALU.min),
                        r=[arg], w=[arg])
                    extr = [] if isinstance(bias, float) else [bias]
                    feat = featbuf[rb, :]
                    p.op("scalar", lambda s, feat=feat, bias=bias, arg=arg:
                         s.activation(feat, arg, AF.Exp, bias=bias),
                         r=[arg] + extr, w=[feat])
            for hi in range(2):
                h = 2 * hp + hi
                rb = slice(hi * 64, hi * 64 + 64)
                idn = self.cb_ident[rb, rb]
                for cc in range(4):
                    sl = bb[0:128, par * 256 + cc * 64: par * 256 + (cc + 1) * 64]
                    kslice = kfP[par][rb, cc * 128:(cc + 1) * 128]
                    self.p.op("tensor",
                              lambda t, o=sl, i=kslice, idn=idn:
                              t.transpose(o, i, idn),
                              r=[kslice, idn], w=[sl], pe="single")
                    dst = ktP[par][:, hi * 256 + cc * 64: hi * 256 + (cc + 1) * 64]
                    p.op("scalar", lambda s, d=dst, sl=sl: s.copy(d, sl),
                         r=[sl], w=[dst])
                for cc in range(4):
                    for dc in range(8):
                        self.mm(bf[:, cc * 64:(cc + 1) * 64],
                                self.XF[:, dc * T + tb * 512 + cc * 128:
                                        dc * T + tb * 512 + (cc + 1) * 128],
                                WV[:, hi * 512 + dc * 64: hi * 512 + (dc + 1) * 64],
                                start=(dc == 0), stop=False)
                    self.mm(bf[:, cc * 64:(cc + 1) * 64], self.cb_ones[0:1, :],
                            BV[0:1, hi * 64:(hi + 1) * 64], start=False, stop=True)
                    dst = vtP[par][:, hi * 256 + cc * 64: hi * 256 + (cc + 1) * 64]
                    p.op("vector", lambda v, d=dst, bf=bf, cc=cc: v.tensor_copy(
                        d, bf[:, cc * 64:(cc + 1) * 64]),
                        r=[bf[:, cc * 64:(cc + 1) * 64]], w=[dst])

        def emit_scan(idx, hi):
            tb, hp = pairs[idx]
            par = idx % 2
            h = 2 * hp + hi
            rb = slice(hi * 64, hi * 64 + 64)
            Mt0, ao0 = MtPl[par], aoP[par]
            rden, dmax = rdP[par], dmP[par]
            bY = YB[par]
            hv = slice(h * 64, (h + 1) * 64)
            hk = slice(h, h + 1)
            for cc in range(4):
                kfs = kfP[par][rb, cc * 128:(cc + 1) * 128]
                qfs = qfP[par][rb, cc * 128:(cc + 1) * 128]
                kts = ktP[par][:, hi * 256 + cc * 64: hi * 256 + (cc + 1) * 64]
                vts = vtP[par][:, hi * 256 + cc * 64: hi * 256 + (cc + 1) * 64]
                if not kv_only:
                    self.mm(b1[:, par * 128:par * 128 + 128], kfs, qfs)
                    p.op("vector", lambda v, Mt0=Mt0, par=par: v.tensor_tensor(
                        Mt0, b1[:, par * 128:par * 128 + 128], self.c_mask01,
                        ALU.mult),
                        r=[b1[:, par * 128:par * 128 + 128], self.c_mask01],
                        w=[Mt0])
                    self.mm(bY[:, 0:64], Mt0, vts, start=True, stop=False)
                    self.mm(bY[:, 0:64], qfs, KVBF[rb, hv], start=False,
                            stop=False)
                    self.mm(bY[:, 64:65], Mt0, self.cb_ones[:, 0:1],
                            start=False, stop=False)
                    self.mm(bY[:, 64:65], qfs, KSBF[rb, hk], start=False,
                            stop=True)
                    p.op("vector", lambda v, dmax=dmax, bY=bY: v.tensor_scalar(
                        dmax, bY[:, 64:65], 1e-6, None, ALU.max),
                        r=[bY[:, 64:65]], w=[dmax])
                    p.op("vector", lambda v, rden=rden, dmax=dmax: v.reciprocal(
                        rden, dmax), r=[dmax], w=[rden])
                    p.op("vector", lambda v, ao0=ao0, rden=rden, bY=bY:
                         v.tensor_scalar(ao0, bY[:, 0:64], rden[:, 0:1], None,
                                         ALU.mult),
                         r=[bY[:, 0:64], rden], w=[ao0])
                    sl = bb[0:64, 512 + par * 128: 512 + par * 128 + 128]
                    self.p.op("tensor",
                              lambda t, o=sl, i=ao0, idn=self.cb_ident[0:128, 0:128]:
                              t.transpose(o, i, idn),
                              r=[ao0, self.cb_ident[0:128, 0:128]], w=[sl],
                              pe="single")
                    dst = AOF[(h % 2) * 64:(h % 2) * 64 + 64,
                              cc * 1024 + (h // 2) * 128:
                              cc * 1024 + (h // 2) * 128 + 128]
                    p.op("vector", lambda v, d=dst, sl=sl: v.tensor_copy(d, sl),
                         r=[sl], w=[dst])
                self.mm(b5[0:64, par * 80:par * 80 + 64], kts, vts,
                        start=True, stop=False)
                self.mm(b5[0:64, par * 80 + 64:par * 80 + 65], kts,
                        self.cb_ones[:, 0:1], start=False, stop=True)
                kv = self.KV[:, hv]
                ks = self.KS[:, hk]
                p.op("vector", lambda v, kv=kv, par=par: v.tensor_tensor(
                    kv, kv, b5[0:64, par * 80:par * 80 + 64], ALU.add),
                    r=[kv, b5[0:64, par * 80:par * 80 + 64]], w=[kv])
                p.op("vector", lambda v, ks=ks, par=par: v.tensor_tensor(
                    ks, ks, b5[0:64, par * 80 + 64:par * 80 + 65], ALU.add),
                    r=[ks, b5[0:64, par * 80 + 64:par * 80 + 65]], w=[ks])
                p.op("scalar", lambda s, kv=kv, hv=hv, rb=rb: s.copy(
                    KVBF[rb, hv], kv), r=[kv], w=[KVBF[rb, hv]])
                p.op("scalar", lambda s, ks=ks, hk=hk, rb=rb: s.copy(
                    KSBF[rb, hk], ks), r=[ks], w=[KSBF[rb, hk]])

        def emit_wout(tb):
            bN = [b0, b1, b2, b3]
            for cc in range(4):
                tt = tb * 4 + cc
                for nb in range(2):
                    bo = bN[cc % 2 * 2 + nb]
                    for g in range(8):
                        self.mm(bo[:, 0:512],
                                AOF[:, cc * 1024 + g * 128: cc * 1024 + (g + 1) * 128],
                                WO[:, g * 1024 + nb * 512: g * 1024 + (nb + 1) * 512],
                                start=(g == 0), stop=False)
                    self.mm(bo[:, 0:512], self.cb_ones[0:1, 0:128],
                            self.br_out[0:1, nb * 512:(nb + 1) * 512],
                            start=False, stop=True)
                    xs = self.X[:, tt * 1024 + nb * 512:
                                tt * 1024 + (nb + 1) * 512]
                    p.op("vector", lambda v, xs=xs, bo=bo: v.tensor_tensor(
                        xs, xs, bo[:, 0:512], ALU.add),
                        r=[xs, bo[:, 0:512]], w=[xs])

        loads(0)
        emit_features(0)
        for idx, (tb, hp) in enumerate(pairs):
            if idx + 1 < len(pairs):
                loads(idx + 1)
                emit_features(idx + 1)
            emit_scan(idx, 0)
            emit_scan(idx, 1)
            if not kv_only and hp == 7:
                emit_wout(tb)
        if kv_only:
            p.op("vector", lambda v: v.tensor_scalar(
                self.KVI[:, :], self.KV[:, :], self.sm_oddm[0:64, 0:1], None,
                ALU.mult), r=[self.KV[:, :], self.sm_oddm], w=[self.KVI[:, :]])
            p.op("vector", lambda v: v.tensor_scalar(
                self.KSI[:, :], self.KS[:, :], self.sm_oddm[0:64, 0:1], None,
                ALU.mult), r=[self.KS[:, :], self.sm_oddm], w=[self.KSI[:, :]])

    def l1_moe(self):
        p = self.p
        S16, S32 = self.S16, self.S32
        c10 = S32[:, 0:160]
        W2G = self.WBUF[:, 8192:12288]
        SA = S16[:, 4096:6144]
        H = S16[:, 0:4096]
        b0, b1 = self.psum[0], self.psum[1]
        bRP = [self.psum[6], self.psum[5]]
        p.op("vector", lambda v: v.memset(c10, 1.0), w=[c10])
        RT = self.RSTAT
        for t in range(NT):
            bR = bRP[t % 2]
            base = t * 40
            lg = RT[:, base:base + 8]
            mska = RT[:, base + 8:base + 16]
            mskb = RT[:, base + 16:base + 24]
            lmask = RT[:, base + 24:base + 32]
            m1c = RT[:, base + 32:base + 33]
            m2c = RT[:, base + 33:base + 34]
            w1c = RT[:, base + 34:base + 35]
            w2c = RT[:, base + 35:base + 36]
            for dc in range(8):
                self.mm(bR[:, 0:8], self.XF[:, dc * T + t * 128: dc * T + (t + 1) * 128],
                        self.RWsb[:, dc * 8:(dc + 1) * 8],
                        start=(dc == 0), stop=False)
            self.mm(bR[:, 0:8], self.cb_ones[0:1, :], self.br_eb,
                    start=False, stop=True)
            p.op("vector", lambda v, lg=lg, bR=bR: v.tensor_copy(lg, bR[:, 0:8]),
                 r=[bR[:, 0:8]], w=[lg])
            p.op("vector", lambda v, m1c=m1c, lg=lg: v.reduce_max(
                m1c, lg, axis=AX.X), r=[lg], w=[m1c])
            p.op("vector", lambda v, mska=mska, lg=lg, m1c=m1c: v.tensor_scalar(
                mska, lg, m1c[:, 0:1], None, ALU.is_equal),
                r=[lg, m1c], w=[mska])
            p.op("vector", lambda v, lmask=lmask, mska=mska, lg=lg:
                 v.scalar_tensor_tensor(lmask, mska, -1e30, lg, ALU.mult, ALU.add),
                 r=[mska, lg], w=[lmask])
            p.op("vector", lambda v, m2c=m2c, lmask=lmask: v.reduce_max(
                m2c, lmask, axis=AX.X), r=[lmask], w=[m2c])
            p.op("vector", lambda v, mskb=mskb, lg=lg, m2c=m2c: v.tensor_scalar(
                mskb, lg, m2c[:, 0:1], None, ALU.is_equal),
                r=[lg, m2c], w=[mskb])
            p.op("vector", lambda v, w1c=w1c, m1c=m1c, m2c=m2c: v.tensor_sub(
                w1c, m1c, m2c), r=[m1c, m2c], w=[w1c])
            p.op("scalar", lambda s, w1c=w1c: s.activation(w1c, w1c, AF.Sigmoid),
                 r=[w1c], w=[w1c])
            p.op("vector", lambda v, w2c=w2c, m1c=m1c, m2c=m2c: v.tensor_sub(
                w2c, m2c, m1c), r=[m1c, m2c], w=[w2c])
            p.op("scalar", lambda s, w2c=w2c: s.activation(w2c, w2c, AF.Sigmoid),
                 r=[w2c], w=[w2c])
            dst = c10[:, t * 10: t * 10 + 8]
            p.op("vector", lambda v, d=dst, mska=mska, w1c=w1c: v.tensor_scalar(
                d, mska, w1c[:, 0:1], None, ALU.mult), r=[mska, w1c], w=[dst])
            p.op("vector", lambda v, d=dst, mskb=mskb, w2c=w2c:
                 v.scalar_tensor_tensor(d, mskb, w2c[:, 0:1], d, ALU.mult, ALU.add),
                 r=[mskb, w2c, dst], w=[dst])
        W1P = [self.WBUF[:, 0:4096], self.WBUF[:, 16384:20480]]
        W3P = [self.WBUF[:, 4096:8192], self.WBUF[:, 12288:16384]]
        groups = [(e, g) for e in range(NE) for g in range(4)]

        def load13(idx, par):
            e, g = groups[idx]
            for j in range(4):
                hc = g * 4 + j
                p.dma("gpsimd", W1P[par][:, j * 1024:(j + 1) * 1024],
                      self.i_mw1L[e, hc], self.s_w)
                p.dma("gpsimd", W3P[par][:, j * 1024:(j + 1) * 1024],
                      self.i_mw3L[e, hc], self.s_w)

        def load2(idx):
            e, g = groups[idx]
            for j in range(4):
                hc = g * 4 + j
                p.dma("gpsimd", W2G[:, j * 1024:(j + 1) * 1024],
                      self.i_mw2T[e, hc * 128:(hc + 1) * 128, :], self.s_w)

        load13(0, 0)
        load2(0)
        for gi, (e, g) in enumerate(groups):
            par = gi % 2
            if gi + 1 < len(groups):
                load13(gi + 1, 1 - par)
            W1G = W1P[par]
            W3G = W3P[par]
            for tp2 in range(2):
                tbs = (2 * tp2, 2 * tp2 + 1)
                for j in range(4):
                    baP = (b0, self.psum[2]) if j % 2 == 0 else (self.psum[1], self.psum[3])
                    saP = (SA[:, 0:512], SA[:, 512:1024]) if j % 2 == 0 else \
                        (SA[:, 1024:1536], SA[:, 1536:2048])
                    for dc in range(8):
                        for ti, tb in enumerate(tbs):
                            self.mm(baP[ti][:, 0:512],
                                    W1G[:, j * 1024 + dc * 128: j * 1024 + (dc + 1) * 128],
                                    self.XF[:, dc * T + tb * 512: dc * T + (tb + 1) * 512],
                                    start=(dc == 0), stop=(dc == 7))
                    for ti in range(2):
                        p.op("scalar", lambda s, ba=baP[ti], sa=saP[ti]: s.activation(
                            sa, ba[:, 0:512], AF.Silu), r=[baP[ti][:, 0:512]], w=[saP[ti]])
                    for dc in range(8):
                        for ti, tb in enumerate(tbs):
                            self.mm(baP[ti][:, 0:512],
                                    W3G[:, j * 1024 + dc * 128: j * 1024 + (dc + 1) * 128],
                                    self.XF[:, dc * T + tb * 512: dc * T + (tb + 1) * 512],
                                    start=(dc == 0), stop=(dc == 7))
                    for ti, tb in enumerate(tbs):
                        hd = H[:, (ti * 4 + j) * 512:(ti * 4 + j + 1) * 512]
                        p.op("vector", lambda v, hd=hd, ba=baP[ti], sa=saP[ti]:
                             v.tensor_tensor(hd, sa, ba[:, 0:512], ALU.mult),
                             r=[saP[ti], baP[ti][:, 0:512]], w=[hd])
                for ti, tb in enumerate(tbs):
                    for q in range(4):
                        tt = tb * 4 + q
                        for nb in range(2):
                            bo = self.psum[4 + (q * 2 + nb) % 3]
                            for j in range(4):
                                self.mm(bo[:, 0:512],
                                        H[:, (ti * 4 + j) * 512 + q * 128:
                                          (ti * 4 + j) * 512 + q * 128 + 128],
                                        W2G[:, j * 1024 + nb * 512: j * 1024 + (nb + 1) * 512],
                                        start=(j == 0), stop=(j == 3))
                            xs = self.X[:, tt * 1024 + nb * 512:
                                        tt * 1024 + (nb + 1) * 512]
                            ccol = c10[:, tt * 10 + e: tt * 10 + e + 1]
                            p.op("vector", lambda v, xs=xs, bo=bo, ccol=ccol:
                                 v.scalar_tensor_tensor(xs, bo[:, 0:512], ccol,
                                                        xs, ALU.mult, ALU.add),
                                 r=[bo[:, 0:512], ccol, xs], w=[xs])
            if gi + 1 < len(groups):
                load2(gi + 1)


_NC_CACHE = None


def _build_program():
    global _NC_CACHE
    if _NC_CACHE is None:
        m = MK()
        m.setup()
        m.run_l0(m.i_px, save_sinit=True)
        m.fmconv()
        m.l1_attn(kv_only=True)
        m.run_l0(m.i_x)
        m.fmconv()
        m.l1_attn(kv_only=False)
        m.postln()
        m.fmconv()
        m.l1_moe()
        m.postln()
        m.store_out()
        _NC_CACHE = m.finalize()
    return _NC_CACHE


def kernel(**inputs):
    from concourse.bass_utils import run_bass_kernel_spmd
    per_core = host_prep(inputs)
    nc = _build_program()
    res = run_bass_kernel_spmd(nc, per_core, core_ids=list(range(8)))
    out = np.zeros((4, 4096, 1024), np.float32)
    for c in range(8):
        b, h = c // 2, c % 2
        out[b, h * 2048:(h + 1) * 2048] = res.results[c]["out"]
    return out


# version 14
# speedup vs baseline: 1.0619x; 1.0619x over previous
import numpy as np

import concourse.bass as bass
import concourse.mybir as mybir

F32 = mybir.dt.float32
BF16 = mybir.dt.bfloat16
AF = mybir.ActivationFunctionType
ALU = mybir.AluOpType
AX = mybir.AxisListType


class Sem:
    def __init__(self, name):
        self.name = name
        self.n = 0
        self.h = None
        self.is_dma = False
        self.batches = []
        self.open_start = None
        self.close_pending = False


class Prog:
    ENGINES = ("sync", "scalar", "vector", "tensor", "gpsimd")

    def __init__(self, nc):
        self.nc = nc
        self.q = {e: [] for e in self.ENGINES}
        self.sems = []
        self.ninst = 0
        self.auto = {e: self.sem("a_" + e) for e in self.ENGINES}
        self.mem = {}
        self.sat = {e: {} for e in self.ENGINES}
        self.ro = set()
        self.psum_names = set()
        self._pegs = {}

    def sem(self, name):
        s = Sem(name)
        self.sems.append(s)
        return s

    def emit(self, engine, fn):
        self.q[engine].append(fn)
        self.ninst += 1

    def raw_wait(self, engine, sem, val):
        if self.sat[engine].get(sem, 0) >= val or val <= 0:
            return
        self.sat[engine][sem] = val
        self.emit(engine, lambda eng, s=sem, v=val: eng.wait_ge(s.h, v))

    @staticmethod
    def rect(ap):
        t = ap.tensor
        name = t.name
        kind = type(t).__name__
        shape = list(t.shape)
        dims = list(ap.ap)
        off = ap.offset
        if kind == "DRamTensorHandle":
            span = 1
            for st, cnt in dims:
                span += (cnt - 1) * st
            return (name, 0, 1, off, off + span)
        if kind == "PSumTensorHandle":
            return (name, 0, 128, 0, 99999)
        W = shape[1] if len(shape) == 2 else int(np.prod(shape[1:]))
        p0 = off // W
        c0 = off % W
        pn = dims[0][1] if dims else 1
        span = 1
        for st, cnt in dims[1:]:
            span += (cnt - 1) * st
        return (name, p0, p0 + pn, c0, c0 + span)

    @staticmethod
    def _overlap(a, b):
        return a[1] < b[2] and b[1] < a[2] and a[3] < b[4] and b[3] < a[4]

    @staticmethod
    def _covers(a, b):
        return a[1] <= b[1] and a[2] >= b[2] and a[3] <= b[3] and a[4] >= b[4]

    def _dep_val(self, sem, val):
        if not sem.is_dma:
            return val
        for st, en in sem.batches:
            if st < val <= en:
                return en
        if sem.open_start is not None and val > sem.open_start:
            sem.batches.append((sem.open_start, sem.n))
            sem.open_start = None
            sem.close_pending = True
            return sem.n
        return val

    def _dep_wait(self, engine, sem, val):
        self.raw_wait(engine, sem, self._dep_val(sem, val))

    def _resolve(self, engine, reads, writes):
        deps = {}
        rrects, wrects = [], []
        own = self.auto.get(engine)
        for ap in reads:
            r = self.rect(ap)
            rrects.append(r)
            excl = r[0] in self.psum_names
            for rec in self.mem.get(r[0], ()):
                if self._overlap(r, rec) and (
                        rec[5] == "W" or (excl and rec[6] is not own)):
                    deps[rec[6]] = max(deps.get(rec[6], 0), rec[7])
        for ap in writes:
            r = self.rect(ap)
            wrects.append(r)
            for rec in self.mem.get(r[0], ()):
                if self._overlap(r, rec):
                    deps[rec[6]] = max(deps.get(rec[6], 0), rec[7])
        for sem, val in deps.items():
            self._dep_wait(engine, sem, val)
        return rrects, wrects

    def _record(self, engine, rrects, wrects, sem, val):
        for r in rrects:
            if r[0] in self.ro:
                continue
            recs = self.mem.setdefault(r[0], [])
            recs[:] = [q for q in recs
                       if not (q[5] == "R" and q[6] is sem and q[:5] == r[:5])]
            recs.append(r + ("R", sem, val))
        for r in wrects:
            recs = self.mem.setdefault(r[0], [])
            recs[:] = [q for q in recs if not self._covers(r, q)]
            recs.append(r + ("W", sem, val))

    def op(self, engine, fn, r=(), w=(), pe=None):
        sem = self.auto[engine]
        rrects, wrects = self._resolve(engine, r, w)
        if pe is None or pe == "single":
            val = sem.n + 1
            sem.n += 1
            self.emit(engine, lambda eng, f=fn, s=sem: f(eng).then_inc(s.h, 1))
            self._record(engine, rrects, wrects, sem, val)
            return
        key = wrects[0][0]
        if pe == "start":
            assert key not in self._pegs, f"group already open on {key}"
            self._pegs[key] = {"reads": list(rrects), "writes": list(wrects)}
            self.emit(engine, fn)
        elif pe == "mid":
            g = self._pegs[key]
            g["reads"] += rrects
            g["writes"] += wrects
            self.emit(engine, fn)
        elif pe == "stop":
            g = self._pegs.pop(key)
            val = sem.n + 1
            sem.n += 1
            self.emit(engine, lambda eng, f=fn, s=sem: f(eng).then_inc(s.h, 1))
            self._record(engine, g["reads"] + rrects, g["writes"] + wrects,
                         sem, val)
        else:
            raise ValueError(pe)

    def dma(self, engine, out, in_, sem, **kw):
        sem.is_dma = True
        rrects, wrects = self._resolve(engine, [in_], [out])
        if sem.close_pending:
            self.raw_wait(engine, sem, sem.n)
            sem.close_pending = False
        if sem.open_start is None:
            sem.open_start = sem.n
        sem.n += 16
        val = sem.n
        self.emit(engine, lambda eng, o=out, i=in_, s=sem, kw=kw:
                  eng.dma_start(out=o, in_=i, **kw).then_inc(s.h, 16))
        self._record(engine, rrects, wrects, sem, val)
        return val

    def mark_ro(self, *names):
        for n in names:
            self.ro.add(n)
            if n in self.mem:
                self.mem[n] = [q for q in self.mem[n] if q[5] == "W"]

    def join_all(self):
        targets = []
        for e in self.ENGINES:
            s = self.auto[e]
            self.raw_wait(e, s, s.n)
            s.n += 1
            self.emit(e, lambda eng, ss=s: eng.nop().then_inc(ss.h, 1))
            targets.append((s, s.n))
        for e in self.ENGINES:
            for s, v in targets:
                self.raw_wait(e, s, v)

    def build_block(self, ctx):
        for s in self.sems:
            s.h = ctx.enter_context(self.nc.semaphore(s.name))
        with self.nc.Block() as block:
            for name in self.ENGINES:
                cmds = self.q[name]
                def runner(eng, cmds=cmds):
                    for fn in cmds:
                        fn(eng)
                getattr(block, name)(runner)


class SchedProg(Prog):

    def __init__(self, nc):
        super().__init__(nc)
        self.cmds = []
        self.cmem = {}
        self._open_groups = {}
        self.buffering = True

    def _cdeps(self, rrects, wrects):
        deps = set()
        for r in rrects:
            for rec in self.cmem.get(r[0], ()):
                if rec[5] == "W" and self._overlap(r, rec):
                    deps.add(rec[6])
                elif r[0] in self.psum_names and self._overlap(r, rec):
                    deps.add(rec[6])
        for r in wrects:
            for rec in self.cmem.get(r[0], ()):
                if self._overlap(r, rec):
                    deps.add(rec[6])
        return deps

    def _crecord(self, rrects, wrects, cid):
        for r in rrects:
            if r[0] in self.ro:
                continue
            recs = self.cmem.setdefault(r[0], [])
            recs.append(r + ("R", cid))
        for r in wrects:
            recs = self.cmem.setdefault(r[0], [])
            recs[:] = [q for q in recs if not self._covers(r, q)]
            recs.append(r + ("W", cid))

    def op(self, engine, fn, r=(), w=(), pe=None):
        if not self.buffering:
            return super().op(engine, fn, r=r, w=w, pe=pe)
        rrects = [self.rect(ap) for ap in r]
        wrects = [self.rect(ap) for ap in w]
        if pe in (None, "single"):
            cid = len(self.cmds)
            cmd = {"engine": engine, "parts": [(fn, r, w, "single" if pe else None)],
                   "deps": self._cdeps(rrects, wrects), "id": cid}
            self.cmds.append(cmd)
            self._crecord(rrects, wrects, cid)
            return
        key = wrects[0][0]
        if pe == "start":
            cid = len(self.cmds)
            cmd = {"engine": engine, "parts": [(fn, r, w, "start")],
                   "deps": self._cdeps(rrects, wrects), "id": cid,
                   "rr": list(rrects), "ww": list(wrects)}
            self.cmds.append(cmd)
            self._open_groups[key] = cmd
        elif pe in ("mid", "stop"):
            cmd = self._open_groups[key]
            cmd["parts"].append((fn, r, w, pe))
            cmd["deps"] |= self._cdeps(rrects, wrects)
            cmd["rr"] += rrects
            cmd["ww"] += wrects
            if pe == "stop":
                del self._open_groups[key]
                self._crecord(cmd["rr"], cmd["ww"], cmd["id"])

    def dma(self, engine, out, in_, sem, **kw):
        if not self.buffering:
            return super().dma(engine, out, in_, sem, **kw)
        rrects = [self.rect(in_)]
        wrects = [self.rect(out)]
        cid = len(self.cmds)
        cmd = {"engine": engine, "dma": (out, in_, sem, kw),
               "deps": self._cdeps(rrects, wrects), "id": cid}
        self.cmds.append(cmd)
        self._crecord(rrects, wrects, cid)

    def join_all(self):
        if not self.buffering:
            return super().join_all()
        return

    def schedule_and_emit(self):
        import heapq
        n = len(self.cmds)
        fwd = [[] for _ in range(n)]
        indeg = [0] * n
        for c in self.cmds:
            i = c["id"]
            for d in c["deps"]:
                if d != i:
                    fwd[d].append(i)
                    indeg[i] += 1
        level = [0] * n
        ready = [i for i in range(n) if indeg[i] == 0]
        heapq.heapify(ready)
        order = []
        while ready:
            i = heapq.heappop(ready)
            order.append(i)
            for j in fwd[i]:
                if level[i] + 1 > level[j]:
                    level[j] = level[i] + 1
                indeg[j] -= 1
                if indeg[j] == 0:
                    heapq.heappush(ready, j)
        assert len(order) == n, f"cycle in dep graph: {len(order)} != {n}"
        order = sorted(order, key=lambda i: (level[i], i))
        self.buffering = False
        for i in order:
            c = self.cmds[i]
            if "dma" in c:
                out, in_, sem, kw = c["dma"]
                super().dma(c["engine"], out, in_, sem, **kw)
            else:
                for fn, r, w, pe in c["parts"]:
                    super().op(c["engine"], fn, r=r, w=w, pe=pe)


"""Builder for the mk_seq_postln_d2 megakernel (auto-dependency version).

Layout (per core, 2048 tokens = 16 tiles of 128):
- X  : residual stream, token-major fp32 [128, 16*1024]; tile t cols [t*1024,+1024)
- XF : feature-major bf16 [128, 8*2048]; d-chunk dc cols [dc*2048,+2048)
- matmul operands at partition base 0; PSUM: 8 banks [128, 512] f32.
"""
import numpy as np
import ml_dtypes
from contextlib import ExitStack



NT = 16
T = 2048
D = 1024
NCH = 16
NH0 = 8
HD0 = 128
NST = 64
NH1 = 16
HD1 = 64
FD = 64
NE = 10
HID = 2048


def host_prep(inputs):
    f32 = np.float32
    x0 = np.asarray(inputs["L0_attn_x"], f32)
    shared = {}
    shared["wxT"] = np.ascontiguousarray(np.asarray(inputs["L0_attn_W_xproj"], f32).T)
    shared["dtbias"] = np.asarray(inputs["L0_attn_dt_bias"], f32).reshape(1, 8)
    shared["alog"] = np.asarray(inputs["L0_attn_A_log"], f32).reshape(1, 8)
    shared["dvec"] = np.asarray(inputs["L0_attn_D"], f32).reshape(1, 8)
    w1 = np.asarray(inputs["L0_ffn_w1"], f32)
    shared["w1L"] = np.ascontiguousarray(
        w1.reshape(8, 128, 32, 128).transpose(2, 1, 0, 3).reshape(32, 128, 1024))
    shared["b1col"] = np.ascontiguousarray(
        np.asarray(inputs["L0_ffn_b1"], f32).reshape(32, 128).T)
    shared["w2"] = np.asarray(inputs["L0_ffn_w2"], f32)
    shared["b2row"] = np.asarray(inputs["L0_ffn_b2"], f32).reshape(1, 1024)
    wqkv = np.asarray(inputs["L1_attn_Wqkv"], f32)
    shared["qkL"] = np.ascontiguousarray(
        wqkv[:, 0:2048].reshape(8, 128, 32, 64).transpose(2, 1, 0, 3)
        .reshape(32, 128, 512))
    shared["qprL"] = np.ascontiguousarray(
        wqkv[:, 0:1024].reshape(8, 128, 8, 128).transpose(2, 1, 0, 3)
        .reshape(8, 128, 1024))
    shared["kprL"] = np.ascontiguousarray(
        wqkv[:, 1024:2048].reshape(8, 128, 8, 128).transpose(2, 1, 0, 3)
        .reshape(8, 128, 1024))
    blk2r = np.zeros((2, 128), np.float32)
    blk2r[0, 0:64] = 1.0
    blk2r[1, 64:128] = 1.0
    shared["blk2r"] = blk2r
    shared["vL2"] = np.ascontiguousarray(
        wqkv[:, 2048:3072].reshape(8, 128, 8, 128).transpose(2, 1, 0, 3)
        .reshape(8, 128, 1024))
    shared["bqkvrow"] = np.asarray(inputs["L1_attn_bqkv"], f32).reshape(1, 3072)
    shared["wout"] = np.asarray(inputs["L1_attn_Wout"], f32)
    shared["boutrow"] = np.asarray(inputs["L1_attn_bout"], f32).reshape(1, 1024)
    om = np.asarray(inputs["L1_attn_omega"], f32)
    shared["omega"] = np.ascontiguousarray(om.transpose(1, 2, 0, 3).reshape(16, 64, 64))
    shared["qnodes"] = np.asarray(inputs["L1_attn_quad_nodes"], f32).reshape(1, 2)
    shared["qweights"] = np.asarray(inputs["L1_attn_quad_weights"], f32).reshape(1, 2)
    shared["rwT"] = np.ascontiguousarray(np.asarray(inputs["L1_ffn_router_w"], f32).T)
    shared["ebrow"] = np.asarray(inputs["L1_ffn_expert_bias"], f32).reshape(1, 8)
    mw1 = np.concatenate([np.asarray(inputs["L1_ffn_w1"], f32),
                          np.asarray(inputs["L1_ffn_sw1"], f32)], 0)
    mw3 = np.concatenate([np.asarray(inputs["L1_ffn_w3"], f32),
                          np.asarray(inputs["L1_ffn_sw3"], f32)], 0)
    mw2 = np.concatenate([np.asarray(inputs["L1_ffn_w2"], f32),
                          np.asarray(inputs["L1_ffn_sw2"], f32)], 0)
    shared["mw1L"] = np.ascontiguousarray(
        mw1.transpose(0, 2, 1).reshape(10, 8, 128, 16, 128)
        .transpose(0, 3, 2, 1, 4).reshape(10, 16, 128, 1024))
    shared["mw3L"] = np.ascontiguousarray(
        mw3.transpose(0, 2, 1).reshape(10, 8, 128, 16, 128)
        .transpose(0, 3, 2, 1, 4).reshape(10, 16, 128, 1024))
    shared["mw2T"] = np.ascontiguousarray(mw2.transpose(0, 2, 1))
    tri = np.triu(np.ones((128, 128), f32))
    shared["maskneg"] = np.where(tri > 0, 0.0, -1e30).astype(f32)
    shared["mask01"] = tri
    shared["identity"] = np.eye(128, dtype=f32)
    shared["onesmat"] = np.ones((128, 128), f32)
    sel = np.zeros((8, 128, 16), f32)
    for c in range(8):
        for f in range(128):
            sel[c, f, (c * 128 + f) // 64] = 1.0
    shared["sel16"] = sel
    blk = np.zeros((2, 64), f32)
    blk[0, 0:32] = 1.0
    blk[1, 32:64] = 1.0
    shared["blk232"] = blk
    per_core = []
    for c in range(8):
        b, h = c // 2, c % 2
        m = dict(shared)
        m["x"] = np.ascontiguousarray(x0[b, h * 2048:(h + 1) * 2048])
        m["px"] = np.ascontiguousarray(x0[b, 0:2048])
        m["oddmask"] = np.full((128, 1), float(h), f32)
        per_core.append(m)
    return per_core


class MK:
    def __init__(self, sched=True):
        nc = bass.Bass("TRN2", target_bir_lowering=False)
        self.nc = nc
        self.p = SchedProg(nc) if sched else Prog(nc)
        self.ctx = ExitStack()
        self._mk_io()
        self._mk_sbuf()
        p = self.p
        self.s_ld = p.sem("ld")
        self.s_w = p.sem("w")
        self.s_w2 = p.sem("w2")
        self.s_io = p.sem("io")
        self.s_row = p.sem("rowq")
        self.s_st = p.sem("st")

    def _in(self, name, shape):
        return self.nc.dram_tensor(name, list(shape), F32, kind="ExternalInput").ap()

    def _mk_io(self):
        i = self._in
        self.i_x = i("x", (T, D))
        self.i_px = i("px", (T, D))
        self.i_oddmask = i("oddmask", (128, 1))
        self.i_wxT = i("wxT", (D, 136))
        self.i_dtbias = i("dtbias", (1, 8))
        self.i_alog = i("alog", (1, 8))
        self.i_dvec = i("dvec", (1, 8))
        self.i_w1L = i("w1L", (32, 128, 1024))
        self.i_b1col = i("b1col", (128, 32))
        self.i_w2 = i("w2", (4096, D))
        self.i_b2row = i("b2row", (1, D))
        self.i_qkL = i("qkL", (32, 128, 512))
        self.i_qprL = i("qprL", (8, 128, 1024))
        self.i_kprL = i("kprL", (8, 128, 1024))
        self.i_blk2r = i("blk2r", (2, 128))
        self.i_vL2 = i("vL2", (8, 128, 1024))
        self.i_bqkvrow = i("bqkvrow", (1, 3072))
        self.i_wout = i("wout", (D, D))
        self.i_boutrow = i("boutrow", (1, D))
        self.i_omega = i("omega", (16, 64, 64))
        self.i_qnodes = i("qnodes", (1, 2))
        self.i_qweights = i("qweights", (1, 2))
        self.i_rwT = i("rwT", (D, 8))
        self.i_ebrow = i("ebrow", (1, 8))
        self.i_mw1L = i("mw1L", (NE, 16, 128, 1024))
        self.i_mw3L = i("mw3L", (NE, 16, 128, 1024))
        self.i_mw2T = i("mw2T", (NE, HID, D))
        self.i_maskneg = i("maskneg", (128, 128))
        self.i_mask01 = i("mask01", (128, 128))
        self.i_identity = i("identity", (128, 128))
        self.i_onesmat = i("onesmat", (128, 128))
        self.i_sel16 = i("sel16", (8, 128, 16))
        self.i_blk = i("blk232", (2, 64))
        self.o_out = self.nc.dram_tensor("out", [T, D], F32, kind="ExternalOutput").ap()
        self.d_cum = self.nc.dram_tensor("d_cum", [8, T], F32).ap()
        self.d_z = self.nc.dram_tensor("d_z", [8, T], F32).ap()
        self.d_rn = self.nc.dram_tensor("d_rn", [16, T], F32).ap()
        self.d_cT = self.nc.dram_tensor("d_cT", [16, 160], F32).ap()
        self.dumps = {}

    def dump_dram(self, name, shape):
        if name not in self.dumps:
            self.dumps[name] = self.nc.dram_tensor(
                "dump_" + name, list(shape), F32, kind="ExternalOutput").ap()
        return self.dumps[name]

    def _sb(self, name, shape, dt):
        return self.ctx.enter_context(self.nc.sbuf_tensor(name, list(shape), dt))

    def _mk_sbuf(self):
        sb = self._sb
        self.X = sb("X", (128, NT * 1024), F32)
        self.XF = sb("XF", (128, 8 * T), BF16)
        self.S16 = sb("S16", (128, 8192), BF16)
        self.S32 = sb("S32", (128, 768), F32)
        self.WBUF = sb("WBUF", (128, 20480), BF16)
        self.CONST = sb("CONST", (128, 768), F32)
        self.CB16 = sb("CB16", (128, 512), BF16)
        self.SMALL = sb("SMALL", (128, 512), F32)
        self.ROW2A = sb("ROW2A", (2, 1024), F32)
        self.ROW2B = sb("ROW2B", (2, 1024), F32)
        self.ROWZ = sb("ROWZ", (1, 1024), F32)
        self.ZTM = sb("ZTM", (128, 128), F32)
        self.ROWD2 = sb("ROWD2", (8, T), F32)
        self.STATE = sb("STATE", (64, NH0 * HD0), F32)
        self.SINIT = sb("SINIT", (64, NH0 * HD0), F32)
        self.KVI = sb("KVI", (64, NH1 * HD1), F32)
        self.KS = sb("KS", (64, NH1), F32)
        self.KSI = sb("KSI", (64, NH1), F32)
        self.KV = self.STATE
        self.BROWS = sb("BROWS", (1, 2080), BF16)
        self.WX = sb("WX", (128, 8 * 136), BF16)
        self.OM = sb("OM", (64, 16 * 64), BF16)
        self.B1C = sb("B1C", (128, 32), F32)
        self.DROW = sb("DROW", (1, 8), F32)
        self.BLK = sb("BLK", (2, 64), F32)
        self.PSTAT = sb("PSTAT", (128, 96), F32)
        self.BTM2 = sb("BTM2", (128, 1024), BF16)
        self.psum = [self.ctx.enter_context(
            self.nc.psum_tensor(f"ps{i}", [128, 512], F32)) for i in range(7)]
        self.psumb = self.ctx.enter_context(
            self.nc.psum_tensor("psb", [128, 1024], BF16))
        self.p.psum_names = {f"ps{i}" for i in range(7)} | {"psb"}

    def mm(self, out, lhsT, rhs, start=True, stop=True):
        pe = "single" if (start and stop) else (
            "start" if start else ("stop" if stop else "mid"))
        self.p.op("tensor",
                  lambda t, o=out, l=lhsT, r=rhs, st=start, sp=stop:
                  t.matmul(o, l, r, start=st, stop=sp),
                  r=[lhsT, rhs], w=[out], pe=pe)

    def tr(self, out, in_, n):
        if in_.tensor.dtype == BF16:
            ident = self.cb_ident[0:n, 0:n]
        else:
            ident = self.c_ident[0:n, 0:n]
        self.p.op("tensor",
                  lambda t, o=out, i=in_, idn=ident: t.transpose(o, i, idn),
                  r=[in_, ident], w=[out], pe="single")

    def setup(self):
        p = self.p
        C, S = self.CONST, self.SMALL
        self.c_ident = C[:, 0:128]
        self.c_maskneg = C[:, 128:256]
        self.c_mask01 = C[:, 256:384]
        self.c_ones = C[:, 384:512]
        self.c_dbc1 = C[:, 512:520]
        self.c_sel16 = C[:, 520:648]
        self.cb_ones = self.CB16[:, 0:128]
        self.cb_ident = self.CB16[:, 128:256]
        self.br_b2 = self.BROWS[0:1, 0:1024]
        self.br_out = self.BROWS[0:1, 1024:2048]
        self.br_eb = self.BROWS[0:1, 2048:2056]
        self.br_dtb = self.BROWS[0:1, 2056:2064]
        self.sm_dtsp = S[:, 0:128]
        self.sm_wtm = S[:, 128:256]
        self.sm_zlast = S[0:64, 256:257]
        self.sm_wtmp = S[0:8, 260:388]
        self.sm_arow = S[0:1, 388:396]
        self.sm_acol = S[0:8, 396:397]
        self.sm_qrow = S[0:1, 400:402]
        self.sm_qwrow = S[0:1, 402:404]
        self.sm_s2col = S[0:64, 404:405]
        self.sm_negscol = S[0:64, 405:406]
        self.sm_lnccol = S[0:64, 406:407]
        self.sm_oddm = S[:, 408:409]
        self.sm_rn = S[:, 412:428]
        self.sm_den = S[:, 428:444]
        self.sm_c10 = S[:, 444:454]
        self.sm_m1 = S[:, 456:457]
        self.sm_m2 = S[:, 457:458]
        self.sm_lg = S[:, 460:470]
        ld = self.s_ld
        w = self.s_w
        d = lambda out, in_: p.dma("sync", out, in_, ld)
        dc = lambda out, in_: p.dma("gpsimd", out, in_, w)
        d(self.c_ident, self.i_identity)
        d(self.c_maskneg, self.i_maskneg)
        d(self.c_mask01, self.i_mask01)
        d(self.c_ones, self.i_onesmat)
        d(self.c_sel16.rearrange("p (c s) -> p c s", c=8),
          self.i_sel16.rearrange("c p s -> p c s"))
        dc(self.cb_ones, self.i_onesmat[:, 0:128])
        dc(self.cb_ident, self.i_identity)
        dc(self.br_b2, self.i_b2row)
        dc(self.br_out, self.i_boutrow)
        dc(self.br_eb, self.i_ebrow)
        dc(self.br_dtb, self.i_dtbias)
        for dch in range(8):
            dc(self.WX[:, dch * 136:(dch + 1) * 136],
               self.i_wxT[dch * 128:(dch + 1) * 128, :])
        dc(self.OM.rearrange("d (h m) -> d h m", h=16),
           self.i_omega.rearrange("h d m -> d h m"))
        self.OM2 = self._sb("OM2", (128, 16 * 64), BF16)
        dc(self.OM2[64:128, :].rearrange("d (h m) -> d h m", h=16),
           self.i_omega.rearrange("h d m -> d h m"))
        self.CBY = self._sb("CBY", (128, 2), BF16)
        self.cb_sel2 = self.CBY[:, :]
        dc(self.cb_sel2, self.i_sel16[0][:, 0:2])
        self.cb_blk2r = self.CB16[0:2, 458:586] if False else None
        self.CBX = self._sb("CBX", (2, 128), BF16)
        dc(self.CBX[:, :], self.i_blk2r)
        self.RWsb = self._sb("RWsb", (128, 64), BF16)
        for dch in range(8):
            dc(self.RWsb[:, dch * 8:(dch + 1) * 8],
               self.i_rwT[dch * 128:(dch + 1) * 128, :])
        d(self.B1C[:, :], self.i_b1col)
        d(self.sm_arow, self.i_alog)
        d(self.sm_qrow, self.i_qnodes)
        d(self.sm_qwrow, self.i_qweights)
        d(self.sm_oddm, self.i_oddmask)
        d(self.DROW[0:1, :], self.i_dvec)
        d(self.BLK[:, :], self.i_blk)
        self.constvals = {}
        for k, val in enumerate([1e-5, 1.0, -1.0]):
            ap = S[:, 257 + k: 258 + k]
            p.op("vector", lambda v, ap=ap, val=val: v.memset(ap, val), w=[ap])
            self.nc.const_aps.register(F32, val, ap)
            self.constvals[val] = ap
        p.op("vector", lambda v: v.memset(self.ROW2A[:, :], 1.0),
             w=[self.ROW2A[:, :]])
        p.op("vector", lambda v: v.memset(self.ROW2B[:, :], -1.0),
             w=[self.ROW2B[:, :]])
        p.op("vector", lambda v: v.memset(self.SINIT[:, :], 0.0),
             w=[self.SINIT[:, :]])
        bD = self.psum[5]
        self.mm(bD[:, 0:8], self.c_ones[0:1, 0:128], self.DROW[0:1, :])
        p.op("vector", lambda v: v.tensor_scalar(
            self.c_dbc1, bD[:, 0:8], 1.0, None, ALU.add),
            r=[bD[:, 0:8]], w=[self.c_dbc1])
        t8 = self.sm_wtmp
        p.op("scalar", lambda s: s.activation(t8[0:1, 0:8], self.sm_arow, AF.Exp),
             r=[self.sm_arow], w=[t8[0:1, 0:8]])
        p.op("vector", lambda v: v.tensor_scalar(
            t8[0:1, 8:16], t8[0:1, 0:8], -1.0, None, ALU.mult),
            r=[t8[0:1, 0:8]], w=[t8[0:1, 8:16]])
        bA = self.psum[6]
        self.tr(bA[0:8, 0:1], t8[0:1, 8:16], 1)
        p.op("vector", lambda v: v.tensor_copy(self.sm_acol, bA[0:8, 0:1]),
             r=[bA[0:8, 0:1]], w=[self.sm_acol])
        q = self.sm_qrow
        t0 = S[0:1, 470:472]; t1 = S[0:1, 472:474]; t2 = S[0:1, 474:476]
        t3 = S[0:1, 476:478]; t4 = S[0:1, 478:480]; t5 = S[0:1, 480:482]
        p.op("vector", lambda v: v.tensor_scalar(t0, q, 0.0, None, ALU.max),
             r=[q], w=[t0])
        p.op("scalar", lambda s: s.activation(t1, t0, AF.Sqrt, scale=2.0),
             r=[t0], w=[t1])
        p.op("vector", lambda v: v.tensor_scalar(t3, self.sm_qwrow, 0.0, None,
             ALU.max), r=[self.sm_qwrow], w=[t3])
        p.op("scalar", lambda s: s.activation(t4, t3, AF.Ln, scale=1.0 / 32.0),
             r=[t3], w=[t4])
        p.op("vector", lambda v: v.tensor_scalar(t5, q, -1.0, None, ALU.mult),
             r=[q], w=[t5])
        p.op("vector", lambda v: v.tensor_tensor(t2, t5, t4, ALU.add),
             r=[t5, t4], w=[t2])
        bB = self.psum[6]
        self.tr(bB[0:2, 0:1], t1, 1)
        self.tr(bB[0:2, 1:2], t2, 1)
        self.tr(bB[0:2, 2:3], t5, 1)
        vals = S[0:2, 482:485]
        p.op("vector", lambda v: v.tensor_copy(vals, bB[0:2, 0:3]),
             r=[bB[0:2, 0:3]], w=[vals])
        bC = self.psum[6]
        self.mm(bC[0:64, 0:3], self.BLK[:, :], vals)
        p.op("vector", lambda v: v.tensor_copy(self.sm_s2col, bC[0:64, 0:1]),
             r=[bC[0:64, 0:1]], w=[self.sm_s2col])
        p.op("vector", lambda v: v.tensor_copy(self.sm_lnccol, bC[0:64, 1:2]),
             r=[bC[0:64, 1:2]], w=[self.sm_lnccol])
        p.op("vector", lambda v: v.tensor_copy(self.sm_negscol, bC[0:64, 2:3]),
             r=[bC[0:64, 2:3]], w=[self.sm_negscol])
        self.p.mark_ro("identity", "maskneg", "mask01", "onesmat", "sel16",
                       "blk232", "b1col", "alog", "qnodes", "qweights",
                       "oddmask", "dvec", "b2row", "boutrow", "ebrow", "dtbias",
                       "wxT", "omega", "w1L", "w2", "qkL", "vL2", "bqkvrow",
                       "wout", "rwT", "mw1L", "mw3L", "mw2T", "x", "px",
                       "CONST", "BROWS", "WX", "OM", "B1C", "DROW",
                       "BLK")

    def load_stream(self, src, split=False):
        for t in range(NT):
            eng, sem = (("gpsimd", self.s_w) if (split and t % 2 == 1)
                        else ("sync", self.s_ld))
            self.p.dma(eng, self.X[:, t * 1024:(t + 1) * 1024],
                       src[t * 128:(t + 1) * 128, :], sem)

    def fmconv(self):
        p = self.p
        for k, (t, dc) in enumerate([(t, dc) for t in range(NT) for dc in range(8)]):
            b = self.psum[k % 4]
            xs = self.X[:, t * 1024 + dc * 128: t * 1024 + (dc + 1) * 128]
            self.tr(b[:, 0:128], xs, 128)
            dst = self.XF[:, dc * T + t * 128: dc * T + (t + 1) * 128]
            if k % 2 == 0:
                p.op("vector", lambda v, d=dst, bb=b: v.tensor_copy(d, bb[:, 0:128]),
                     r=[b[:, 0:128]], w=[dst])
            else:
                p.op("scalar", lambda s, d=dst, bb=b: s.copy(d, bb[:, 0:128]),
                     r=[b[:, 0:128]], w=[dst])

    def l0_prep(self, sub=99):
        p = self.p
        S = self.SMALL
        self.B_fm = self.S16[0:64, 0:2048]
        self.C_fm = self.S16[0:64, 2048:4096]
        self.B_tmall = self.BTM2[:, 0:1024]
        self.SBF = self.S16[0:64, 5120:6144]
        bB, bC = self.psum[3], self.psum[4]
        for tb in range(4):
            for which, bank, co in (("B", bB, 8), ("C", bC, 72)):
                for dc in range(8):
                    self.mm(bank[0:64, 0:512],
                            self.WX[:, dc * 136 + co: dc * 136 + co + 64],
                            self.XF[:, dc * T + tb * 512: dc * T + (tb + 1) * 512],
                            start=(dc == 0), stop=(dc == 7))
                dst = (self.B_fm if which == "B" else self.C_fm)[:, tb * 512:(tb + 1) * 512]
                p.op("vector", lambda v, d=dst, bb=bank: v.tensor_copy(
                    d, bb[0:64, 0:512]), r=[bank[0:64, 0:512]], w=[dst])
        if sub < 1:
            return
        import os
        nch_lim = int(os.environ.get("MKCH", NCH))
        pieces = os.environ.get("MKPIECE", "abcd")
        bT = self.psum[5]
        bX = self.psum[6]
        sp1 = S[:, 504:512]
        sp2 = S[:, 496:504]
        for c in range(min(NCH, nch_lim)):
            for dc in range(8):
                self.mm(bT[:, 0:72],
                        self.XF[:, dc * T + c * 128: dc * T + (c + 1) * 128],
                        self.WX[:, dc * 136: dc * 136 + 72],
                        start=(dc == 0), stop=False)
            self.mm(bT[:, 0:8], self.cb_ones[0:1, :], self.br_dtb,
                    start=False, stop=True)
            if "a" not in pieces:
                p.op("vector", lambda v: v.tensor_copy(
                    S[:, 504:505], bT[:, 0:1]), r=[bT[:, 0:1]], w=[S[:, 504:505]])
                continue
            p.op("vector", lambda v, c=c: v.tensor_copy(
                self.B_tmall[:, c * 64:(c + 1) * 64], bT[:, 8:72]),
                r=[bT[:, 8:72]], w=[self.B_tmall[:, c * 64:(c + 1) * 64]])
            if "b" not in pieces:
                continue
            lvl = int(os.environ.get("MKB", "5"))
            if lvl >= 1:
                fn_ = {"sq": AF.Square, "abs": AF.Abs, "exp": AF.Exp,
                       "relu": AF.Relu}[os.environ.get("MKABS", "abs")]
                p.op("scalar", lambda s, fn_=fn_: s.activation(
                    sp1, bT[:, 0:8], fn_), r=[bT[:, 0:8]], w=[sp1])
            if lvl >= 2:
                p.op("scalar", lambda s: s.activation(sp1, sp1, AF.Exp, scale=-1.0),
                     r=[sp1], w=[sp1])
            if lvl >= 3:
                p.op("scalar", lambda s: s.activation(sp1, sp1, AF.Ln, bias=1.0),
                     r=[sp1], w=[sp1])
            if lvl >= 4:
                p.op("scalar", lambda s: s.activation(sp2, bT[:, 0:8], AF.Relu),
                     r=[bT[:, 0:8]], w=[sp2])
            dst = self.sm_dtsp[:, c * 8:(c + 1) * 8]
            if lvl >= 5:
                p.op("vector", lambda v, d=dst: v.tensor_tensor(d, sp1, sp2, ALU.add),
                     r=[sp1, sp2], w=[dst])
            else:
                p.op("vector", lambda v, d=dst: v.tensor_copy(d, sp1),
                     r=[sp1], w=[dst])
            if "c" not in pieces:
                continue
            self.tr(bX[0:8, 0:128], dst, 128)
            rd = self.ROWD2[0:8, c * 128:(c + 1) * 128]
            p.op("vector", lambda v, rd=rd: v.tensor_copy(rd, bX[0:8, 0:128]),
                 r=[bX[0:8, 0:128]], w=[rd])
        if sub < 2:
            return
        RD = self.ROWD2[0:8, :]
        p.op("vector", lambda v: v.tensor_scalar(
            RD, RD, self.sm_acol, None, ALU.mult), r=[RD, self.sm_acol], w=[RD])
        for c in range(NCH):
            seg = self.ROWD2[0:8, c * 128:(c + 1) * 128]
            init = 0.0 if c == 0 else self.ROWD2[0:8, c * 128 - 1:c * 128]
            rr = [seg, self.c_ones[0:8, 0:128]] + ([init] if c > 0 else [])
            p.op("vector", lambda v, seg=seg, init=init: v.tensor_tensor_scan(
                seg, seg, self.c_ones[0:8, 0:128], init, ALU.add, ALU.mult),
                r=rr, w=[seg])
        if sub < 3:
            return
        p.dma("sync", self.d_cum, RD, self.s_io)
        if sub < 4:
            return
        for c in range(NCH):
            wt = self.sm_wtmp
            p.op("vector", lambda v, c=c: v.tensor_scalar(
                wt, self.ROWD2[0:8, c * 128:(c + 1) * 128],
                self.ROWD2[0:8, c * 128 + 127:c * 128 + 128], None, ALU.subtract),
                r=[self.ROWD2[0:8, c * 128:(c + 1) * 128]], w=[wt])
            p.op("scalar", lambda s: s.activation(wt, wt, AF.Exp, scale=-1.0),
                 r=[wt], w=[wt])
            bW = self.psum[6]
            self.tr(bW[:, 0:8], wt, 8)
            dst = self.sm_wtm[:, c * 8:(c + 1) * 8]
            p.op("vector", lambda v, d=dst: v.tensor_copy(d, bW[:, 0:8]),
                 r=[bW[:, 0:8]], w=[dst])
        if sub < 5:
            return
        for c in range(NCH - 1, 0, -1):
            seg = self.ROWD2[0:8, c * 128:(c + 1) * 128]
            prev = self.ROWD2[0:8, c * 128 - 1:c * 128]
            p.op("vector", lambda v, seg=seg, prev=prev: v.tensor_scalar(
                seg, seg, prev, None, ALU.subtract), r=[seg, prev], w=[seg])
        p.op("scalar", lambda s: s.activation(RD, RD, AF.Exp), r=[RD], w=[RD])
        p.dma("sync", self.d_z, RD, self.s_io)
        for c in range(NCH):
            bZt = self.psum[(2, 5, 6)[c % 3]]
            self.tr(bZt[:, 0:8], self.ROWD2[0:8, c * 128:(c + 1) * 128], 8)
            dst = self.ZTM[:, c * 8:(c + 1) * 8]
            p.op("vector", lambda v, d=dst, b=bZt: v.tensor_copy(d, b[:, 0:8]),
                 r=[bZt[:, 0:8]], w=[dst])

    def l0_mixer(self):
        p = self.p
        p.op("vector", lambda v: v.tensor_copy(self.STATE[:, :], self.SINIT[:, :]),
             r=[self.SINIT[:, :]], w=[self.STATE[:, :]])
        p.op("vector", lambda v: v.tensor_copy(self.SBF, self.STATE[:, :]),
             r=[self.STATE[:, :]], w=[self.SBF])
        XhP = [self.S16[:, 6144:6272], self.S16[:, 6656:6784]]
        XhwP = [self.S16[:, 6272:6400], self.S16[:, 6784:6912]]
        MtP = [self.S16[:, 6400:6528], self.S16[:, 6912:7040]]
        CzP = [self.S16[0:64, 6528:6656], self.S16[0:64, 7040:7168]]
        EmatP = [self.S32[:, 0:128], self.S32[:, 128:256]]
        bDP = [self.psum[0], self.psum[6]]
        bG, bZ, bS = self.psum[1], self.psum[2], self.psum[5]
        bY = [self.psum[3], self.psum[4]]
        for c in range(NCH):
            p.dma("sync", self.ROW2A[0:1, :].rearrange("p (h t) -> p h t", h=8),
                  self.d_cum[:, c * 128:(c + 1) * 128], self.s_row)
            p.dma("sync", self.ROW2B[1:2, :].rearrange("p (h t) -> p h t", h=8),
                  self.d_cum[:, c * 128:(c + 1) * 128], self.s_row)
            p.dma("sync", self.ROWZ[0:1, :].rearrange("p (h t) -> p h t", h=8),
                  self.d_z[:, c * 128:(c + 1) * 128], self.s_row)
            Bc = self.B_fm[:, c * 128:(c + 1) * 128]
            Cc = self.C_fm[:, c * 128:(c + 1) * 128]
            self.mm(bG[:, 0:128], Bc, Cc)

            def stage_a(h, c=c, Cc=Cc):
                hs = slice(h * 128, (h + 1) * 128)
                Xh, Xhw = XhP[h % 2], XhwP[h % 2]
                Mt, Emat = MtP[h % 2], EmatP[h % 2]
                bD = bDP[h % 2]
                self.mm(bD[:, 0:128], self.ROW2A[:, hs], self.ROW2B[:, hs])
                zbc = bZ[0:64, (h % 2):(h % 2) + 1]
                self.mm(zbc, self.c_ones[0:1, 0:64],
                        self.ROWZ[0:1, h * 128 + 127:h * 128 + 128])
                p.op("vector", lambda v, Emat=Emat, bD=bD: v.scalar_tensor_tensor(
                    Emat, bD[:, 0:128], 0.0, self.c_maskneg, ALU.min, ALU.add),
                    r=[bD[:, 0:128], self.c_maskneg], w=[Emat])
                p.op("scalar", lambda s, Emat=Emat: s.activation(Emat, Emat, AF.Exp),
                     r=[Emat], w=[Emat])
                p.op("vector", lambda v, Mt=Mt, Emat=Emat: v.tensor_tensor(
                    Mt, bG[:, 0:128], Emat, ALU.mult),
                    r=[bG[:, 0:128], Emat], w=[Mt])
                zl = self.SMALL[0:64, 256:257] if h % 2 == 0 else \
                    self.SMALL[0:64, 260:261]
                p.op("vector", lambda v, zl=zl, zbc=zbc: v.tensor_copy(
                    zl, zbc), r=[zbc], w=[zl])
                xsl = self.X[:, c * 1024 + h * 128: c * 1024 + (h + 1) * 128]
                dcol = self.sm_dtsp[:, c * 8 + h: c * 8 + h + 1]
                wcol = self.sm_wtm[:, c * 8 + h: c * 8 + h + 1]
                p.op("vector", lambda v, xsl=xsl, dcol=dcol, Xh=Xh: v.tensor_scalar(
                    Xh, xsl, dcol, None, ALU.mult), r=[xsl, dcol], w=[Xh])
                p.op("vector", lambda v, wcol=wcol, Xh=Xh, Xhw=Xhw: v.tensor_scalar(
                    Xhw, Xh, wcol, None, ALU.mult), r=[Xh, wcol], w=[Xhw])

            def stage_b(h, c=c):
                hs = slice(h * 128, (h + 1) * 128)
                Xh, Xhw = XhP[h % 2], XhwP[h % 2]
                Mt = MtP[h % 2]
                Cc2 = self.C_fm[:, c * 128:(c + 1) * 128]
                zl = self.SMALL[0:64, 256:257] if h % 2 == 0 else \
                    self.SMALL[0:64, 260:261]
                by = bY[h % 2]
                p.op("tensor", lambda t, o=by[:, 0:128], l=Mt, rr=Xh:
                     t.matmul(o, l, rr, start=True, stop=True),
                     r=[Mt, Xh], w=[by[:, 0:128]], pe="start")
                p.op("tensor",
                     lambda t, o=by[:, 128:256], l=Cc2, rr=self.SBF[:, hs]:
                     t.matmul(o, l, rr, start=True, stop=True),
                     r=[Cc2, self.SBF[:, hs]], w=[by[:, 128:256]], pe="stop")
                self.mm(bS[0:64, (h % 2) * 128:(h % 2) * 128 + 128],
                        self.B_tmall[:, c * 64:(c + 1) * 64], Xhw)
                bSs = bS[0:64, (h % 2) * 128:(h % 2) * 128 + 128]
                xsl = self.X[:, c * 1024 + h * 128: c * 1024 + (h + 1) * 128]
                dco = self.c_dbc1[:, h:h + 1]
                zc = self.ZTM[:, c * 8 + h:c * 8 + h + 1]
                p.op("vector", lambda v, xsl=xsl, dco=dco, by=by:
                     v.scalar_tensor_tensor(xsl, xsl, dco, by[:, 0:128],
                                            ALU.mult, ALU.add),
                     r=[xsl, dco, by[:, 0:128]], w=[xsl])
                p.op("vector", lambda v, xsl=xsl, zc=zc, by=by:
                     v.scalar_tensor_tensor(xsl, by[:, 128:256], zc, xsl,
                                            ALU.mult, ALU.add),
                     r=[by[:, 128:256], zc, xsl], w=[xsl])
                st = self.STATE[:, hs]
                p.op("vector", lambda v, st=st, zl=zl, bSs=bSs:
                     v.scalar_tensor_tensor(st, st, zl, bSs, ALU.mult, ALU.add),
                     r=[st, zl, bSs], w=[st])
                p.op("scalar", lambda s, st=st, hs=hs: s.copy(
                    self.SBF[:, hs], st), r=[st], w=[self.SBF[:, hs]])

            stage_a(0)
            for h in range(NH0):
                if h + 1 < NH0:
                    stage_a(h + 1)
                stage_b(h)

    def postln(self):
        p = self.p
        PS = self.PSTAT
        bQP = [self.psum[0], self.psum[0]]
        SUM = PS[:, 0:16]
        SQA = PS[:, 16:32]
        SQB = PS[:, 32:48]
        NM = PS[:, 48:64]
        RS = PS[:, 64:80]
        VAR = PS[:, 80:96]
        for t in range(NT):
            bQ = bQP[t % 2]
            xs = self.X[:, t * 1024:(t + 1) * 1024]
            x0 = self.X[:, t * 1024:t * 1024 + 512]
            x1 = self.X[:, t * 1024 + 512:(t + 1) * 1024]
            sumc = PS[:, t:t + 1]
            sqa = PS[:, 16 + t:17 + t]
            sqb = PS[:, 32 + t:33 + t]
            p.op("vector", lambda v, xs=xs, sumc=sumc: v.reduce_sum(
                sumc, xs, axis=AX.X), r=[xs], w=[sumc])
            p.op("scalar", lambda s, x0=x0, bQ=bQ, sqa=sqa: s.activation(
                bQ[:, 0:512], x0, AF.Square, accum_out=sqa),
                r=[x0], w=[bQ[:, 0:512], sqa])
            p.op("scalar", lambda s, x1=x1, bQ=bQ, sqb=sqb: s.activation(
                bQ[:, 0:512], x1, AF.Square, accum_out=sqb),
                r=[x1], w=[bQ[:, 0:512], sqb])
        p.op("vector", lambda v: v.tensor_tensor(SQA, SQA, SQB, ALU.add),
             r=[SQA, SQB], w=[SQA])
        p.op("vector", lambda v: v.tensor_scalar(
            NM, SUM, -1.0 / 1024.0, None, ALU.mult), r=[SUM], w=[NM])
        p.op("vector", lambda v: v.tensor_tensor(VAR, NM, NM, ALU.mult),
             r=[NM], w=[VAR])
        p.op("vector", lambda v: v.tensor_scalar(
            SQA, SQA, 1.0 / 1024.0, None, ALU.mult), r=[SQA], w=[SQA])
        p.op("vector", lambda v: v.tensor_sub(VAR, SQA, VAR),
             r=[SQA, VAR], w=[VAR])
        p.op("scalar", lambda s: s.activation(VAR, VAR, AF.Ln, bias=1e-5),
             r=[VAR, self.constvals[1e-5]], w=[VAR])
        p.op("scalar", lambda s: s.activation(RS, VAR, AF.Exp, scale=-0.5),
             r=[VAR], w=[RS])
        for t in range(NT):
            xs = self.X[:, t * 1024:(t + 1) * 1024]
            nm = PS[:, 48 + t:49 + t]
            rs = PS[:, 64 + t:65 + t]
            p.op("vector", lambda v, xs=xs, nm=nm, rs=rs: v.tensor_scalar(
                xs, xs, nm[:, 0:1], rs[:, 0:1], ALU.add, ALU.mult),
                r=[xs, nm, rs], w=[xs])

    def l0_ffn(self):
        p = self.p
        W1G = self.WBUF[:, 0:8192]
        W2G = self.WBUF[:, 8192:16384]
        H1 = self.S16[:, 0:8192]
        for g in range(4):
            for j in range(8):
                hc = g * 8 + j
                p.dma("gpsimd", W1G[:, j * 1024:(j + 1) * 1024],
                      self.i_w1L[hc], self.s_w)
                p.dma("gpsimd", W2G[:, j * 1024:(j + 1) * 1024],
                      self.i_w2[hc * 128:(hc + 1) * 128, :], self.s_w2)
            for tp2 in range(2):
                tbs = (2 * tp2, 2 * tp2 + 1)
                for j in range(8):
                    baP = (self.psum[0], self.psum[2]) if j % 2 == 0 else \
                        (self.psum[1], self.psum[3])
                    hc = g * 8 + j
                    for dc in range(8):
                        for ti, tb in enumerate(tbs):
                            self.mm(baP[ti][:, 0:512],
                                    W1G[:, j * 1024 + dc * 128: j * 1024 + (dc + 1) * 128],
                                    self.XF[:, dc * T + tb * 512: dc * T + (tb + 1) * 512],
                                    start=(dc == 0), stop=(dc == 7))
                    for ti in range(2):
                        h1d = H1[:, (ti * 8 + j) * 512:(ti * 8 + j + 1) * 512]
                        bc = self.B1C[:, hc:hc + 1]
                        p.op("scalar", lambda s, h1d=h1d, ba=baP[ti], bc=bc:
                             s.activation(h1d, ba[:, 0:512], AF.Gelu, bias=bc),
                             r=[baP[ti][:, 0:512], bc], w=[h1d])
                for ti, tb in enumerate(tbs):
                    for q in range(4):
                        tt = tb * 4 + q
                        for nb in range(2):
                            bo = self.psum[4 + (q * 2 + nb) % 3]
                            for j in range(8):
                                last = (j == 7) and (g != 0)
                                self.mm(bo[:, 0:512],
                                        H1[:, (ti * 8 + j) * 512 + q * 128:
                                          (ti * 8 + j) * 512 + q * 128 + 128],
                                        W2G[:, j * 1024 + nb * 512: j * 1024 + (nb + 1) * 512],
                                        start=(j == 0), stop=last)
                            if g == 0:
                                self.mm(bo[:, 0:512], self.cb_ones[0:1, 0:128],
                                        self.br_b2[0:1, nb * 512:(nb + 1) * 512],
                                        start=False, stop=True)
                            xs = self.X[:, tt * 1024 + nb * 512:
                                        tt * 1024 + (nb + 1) * 512]
                            p.op("vector", lambda v, xs=xs, bo=bo: v.tensor_tensor(
                                xs, xs, bo[:, 0:512], ALU.add),
                                r=[xs, bo[:, 0:512]], w=[xs])

    def run_l0(self, src, save_sinit=False):
        p = self.p
        self.load_stream(src, split=save_sinit)
        self.fmconv()
        self.l0_prep()
        self.l0_mixer()
        if save_sinit:
            p.op("vector", lambda v: v.tensor_scalar(
                self.SINIT[:, :], self.STATE[:, :], self.sm_oddm[0:64, 0:1],
                None, ALU.mult),
                r=[self.STATE[:, :], self.sm_oddm], w=[self.SINIT[:, :]])
        self.postln()
        self.fmconv()
        self.l0_ffn()
        self.postln()

    def dump_x(self, name):
        ap = self.dump_dram(name, (T, D))
        for t in range(NT):
            self.p.dma("sync", ap[t * 128:(t + 1) * 128, :],
                       self.X[:, t * 1024:(t + 1) * 1024], self.s_st)

    def dump_dram_copy(self, name, src_ap, shape):
        ap = self.dump_dram(name, shape)
        self.p.dma("sync", ap, src_ap, self.s_st)

    def dump_sbuf(self, name, sb_ap, shape):
        ap = self.dump_dram(name, shape)
        self.p.dma("sync", ap, sb_ap, self.s_st)

    def store_out(self):
        for t in range(NT):
            self.p.dma("sync", self.o_out[t * 128:(t + 1) * 128, :],
                       self.X[:, t * 1024:(t + 1) * 1024], self.s_st)

    def finalize(self):
        if hasattr(self.p, "schedule_and_emit"):
            self.p.schedule_and_emit()
        self.p.join_all()
        self.p.raw_wait("sync", self.s_st, self.s_st.n)
        self.p.build_block(self.ctx)
        self.ctx.close()
        return self.nc

    def l1_attn(self, kv_only=False):
        p = self.p
        S16, S32, S = self.S16, self.S32, self.SMALL
        W = self.WBUF
        AOF = S16[:, 0:4096]
        qfP = [S16[:, 4096:4608], W[:, 12288:12800]]
        kfP = [S16[:, 4608:5120], W[:, 12800:13312]]
        ktP = [S16[:, 5120:5632], W[:, 13312:13824]]
        vtP = [S16[:, 5632:6144], W[:, 13824:14336]]
        qnP = [W[:, 10240:10752], W[:, 10752:11264]]
        q2P = [W[:, 11264:11776], W[:, 11776:12288]]
        MtPl = [S16[:, 7696:7824], W[:, 14848:14976]]
        aoP = [S16[:, 7824:7888], W[:, 14976:15040]]
        rnbP = [W[0:2, 15040:15552], W[0:2, 15552:16064]]
        argP = [S32[:, 256:768], S32[:, 0:512]]
        ssP = [S32[0:2, 256:768], S32[0:2, 0:512]]
        bqP = [S[:, 490:491], S[:, 470:471]]
        bkP = [S[:, 491:492], S[:, 471:472]]
        rdP = [S[:, 492:493], S[:, 476:477]]
        dmP = [S[:, 493:494], S[:, 477:478]]
        KVBF = S16[:, 6656:7680]
        KSBF = S16[:, 7680:7696]
        WQP = [W[:, 16384:18432], W[:, 18432:20480]]
        WVP = [W[:, 8192:9216], W[:, 9216:10240]]
        BVP = [self.CB16[0:1, 256:384], self.CB16[0:1, 384:512]]
        WO = W[:, 0:8192]
        b0, b1, b2, b3, b4, b5, b6 = self.psum
        bb = self.psumb
        featB = [b0, b2]
        YB = [b4, b6]
        if kv_only:
            p.op("vector", lambda v: v.memset(self.KV[:, :], 0.0),
                 w=[self.KV[:, :]])
            p.op("vector", lambda v: v.memset(self.KS[:, :], 0.0),
                 w=[self.KS[:, :]])
        else:
            p.op("vector", lambda v: v.tensor_copy(self.KV[:, :], self.KVI[:, :]),
                 r=[self.KVI[:, :]], w=[self.KV[:, :]])
            p.op("vector", lambda v: v.tensor_copy(self.KS[:, :], self.KSI[:, :]),
                 r=[self.KSI[:, :]], w=[self.KS[:, :]])
            for g in range(8):
                p.dma("gpsimd", WO[:, g * 1024:(g + 1) * 1024],
                      self.i_wout[g * 128:(g + 1) * 128, :], self.s_w)
        for hi in range(2):
            rb = slice(hi * 64, hi * 64 + 64)
            cv = [c for c in range(16) if c % 2 == hi]
            for h in cv:
                p.op("vector", lambda v, rb=rb, h=h: v.tensor_copy(
                    KVBF[rb, h * 64:(h + 1) * 64],
                    self.KV[:, h * 64:(h + 1) * 64]),
                    r=[self.KV[:, h * 64:(h + 1) * 64]],
                    w=[KVBF[rb, h * 64:(h + 1) * 64]])
                p.op("vector", lambda v, rb=rb, h=h: v.tensor_copy(
                    KSBF[rb, h:h + 1], self.KS[:, h:h + 1]),
                    r=[self.KS[:, h:h + 1]], w=[KSBF[rb, h:h + 1]])

        pairs = [(tb, hp) for tb in range(4) for hp in range(8)]

        def loads(idx):
            tb, hp = pairs[idx]
            par = idx % 2
            if not kv_only:
                p.dma("gpsimd", WQP[par][:, 0:1024], self.i_qprL[hp], self.s_w)
            p.dma("gpsimd", WQP[par][:, 1024:2048], self.i_kprL[hp], self.s_w)
            p.dma("gpsimd", WVP[par][:, 0:1024], self.i_vL2[hp], self.s_w)
            p.dma("gpsimd", bqP[par],
                  self.i_bqkvrow[0:1, hp * 128:(hp + 1) * 128]
                  .rearrange("o f -> (o f)")[:, None], self.s_w)
            p.dma("gpsimd", bkP[par],
                  self.i_bqkvrow[0:1, 1024 + hp * 128: 1024 + (hp + 1) * 128]
                  .rearrange("o f -> (o f)")[:, None], self.s_w)
            p.dma("gpsimd", BVP[par],
                  self.i_bqkvrow[0:1, 2048 + hp * 128: 2048 + (hp + 1) * 128],
                  self.s_w)

        def emit_features(idx):
            tb, hp = pairs[idx]
            par = idx % 2
            qn, q2 = qnP[par], q2P[par]
            rnb, argsb, ssb = rnbP[par], argP[par], ssP[par]
            bf = featB[par]
            WQ, WV, BV = WQP[par], WVP[par], BVP[par]
            for which in (("q",) if not kv_only else ()) + ("k",):
                wt = WQ[:, 0:1024] if which == "q" else WQ[:, 1024:2048]
                bcol = bqP[par] if which == "q" else bkP[par]
                featbuf = qfP[par] if which == "q" else kfP[par]
                bias = self.sm_lnccol if which == "q" else 0.0
                for dc in range(8):
                    self.mm(bf[:, 0:512], wt[:, dc * 128:(dc + 1) * 128],
                            self.XF[:, dc * T + tb * 512: dc * T + (tb + 1) * 512],
                            start=(dc == 0), stop=(dc == 7))
                p.op("vector", lambda v, bcol=bcol, qn=qn, bf=bf:
                     v.tensor_scalar(qn, bf[:, 0:512], bcol, None, ALU.add),
                     r=[bf[:, 0:512], bcol], w=[qn])
                p.op("scalar", lambda s, q2=q2, qn=qn: s.activation(
                    q2, qn, AF.Square), r=[qn], w=[q2])
                self.mm(bf[0:2, 0:512], self.cb_sel2, q2)
                p.op("vector", lambda v, ssb=ssb, bf=bf: v.tensor_scalar(
                    ssb, bf[0:2, 0:512], 1e-24, None, ALU.max),
                    r=[bf[0:2, 0:512]], w=[ssb])
                p.op("scalar", lambda s, ssb=ssb: s.activation(ssb, ssb, AF.Ln),
                     r=[ssb], w=[ssb])
                p.op("scalar", lambda s, rnb=rnb, ssb=ssb: s.activation(
                    rnb, ssb, AF.Exp, scale=-0.5), r=[ssb], w=[rnb])
                self.mm(b3[:, 0:512], self.CBX[:, :], rnb)
                p.op("vector", lambda v, qn=qn: v.tensor_tensor(
                    qn, qn, b3[:, 0:512], ALU.mult),
                    r=[qn, b3[:, 0:512]], w=[qn])
                for hi in range(2):
                    h = 2 * hp + hi
                    rb = slice(hi * 64, hi * 64 + 64)
                    om = self.OM[:, h * 64:(h + 1) * 64] if hi == 0 else \
                        self.OM2[64:128, h * 64:(h + 1) * 64]
                    bk = bf if hi == 0 else b3
                    self.mm(bk[0:64, 0:512], om, qn[rb, :])
                    arg = argsb[rb, :]
                    p.op("vector", lambda v, arg=arg, bk=bk: v.tensor_scalar(
                        arg, bk[0:64, 0:512], self.sm_s2col, self.sm_negscol,
                        ALU.mult, ALU.add),
                        r=[bk[0:64, 0:512], self.sm_s2col, self.sm_negscol],
                        w=[arg])
                    p.op("vector", lambda v, arg=arg: v.tensor_scalar(
                        arg, arg, -20.0, 20.0, ALU.max, ALU.min),
                        r=[arg], w=[arg])
                    extr = [] if isinstance(bias, float) else [bias]
                    feat = featbuf[rb, :]
                    p.op("scalar", lambda s, feat=feat, bias=bias, arg=arg:
                         s.activation(feat, arg, AF.Exp, bias=bias),
                         r=[arg] + extr, w=[feat])
            for hi in range(2):
                h = 2 * hp + hi
                rb = slice(hi * 64, hi * 64 + 64)
                idn = self.cb_ident[rb, rb]
                for cc in range(4):
                    sl = bb[0:128, par * 256 + cc * 64: par * 256 + (cc + 1) * 64]
                    kslice = kfP[par][rb, cc * 128:(cc + 1) * 128]
                    self.p.op("tensor",
                              lambda t, o=sl, i=kslice, idn=idn:
                              t.transpose(o, i, idn),
                              r=[kslice, idn], w=[sl], pe="single")
                    dst = ktP[par][:, hi * 256 + cc * 64: hi * 256 + (cc + 1) * 64]
                    p.op("scalar", lambda s, d=dst, sl=sl: s.copy(d, sl),
                         r=[sl], w=[dst])
            for cc in range(4):
                bvs = bf[:, cc * 128:(cc + 1) * 128]
                for dc in range(8):
                    self.mm(bvs,
                            self.XF[:, dc * T + tb * 512 + cc * 128:
                                    dc * T + tb * 512 + (cc + 1) * 128],
                            WV[:, dc * 128:(dc + 1) * 128],
                            start=(dc == 0), stop=False)
                self.mm(bvs, self.cb_ones[0:1, :], BV[0:1, 0:128],
                        start=False, stop=True)
                for hi in range(2):
                    src = bf[:, cc * 128 + hi * 64: cc * 128 + (hi + 1) * 64]
                    dst = vtP[par][:, hi * 256 + cc * 64: hi * 256 + (cc + 1) * 64]
                    p.op("vector", lambda v, d=dst, s_=src: v.tensor_copy(d, s_),
                         r=[src], w=[dst])

        def emit_scan(idx, hi):
            tb, hp = pairs[idx]
            par = idx % 2
            h = 2 * hp + hi
            rb = slice(hi * 64, hi * 64 + 64)
            Mt0, ao0 = MtPl[par], aoP[par]
            rden, dmax = rdP[par], dmP[par]
            bY = YB[par]
            hv = slice(h * 64, (h + 1) * 64)
            hk = slice(h, h + 1)
            for cc in range(4):
                kfs = kfP[par][rb, cc * 128:(cc + 1) * 128]
                qfs = qfP[par][rb, cc * 128:(cc + 1) * 128]
                kts = ktP[par][:, hi * 256 + cc * 64: hi * 256 + (cc + 1) * 64]
                vts = vtP[par][:, hi * 256 + cc * 64: hi * 256 + (cc + 1) * 64]
                if not kv_only:
                    self.mm(b1[:, par * 128:par * 128 + 128], kfs, qfs)
                    p.op("vector", lambda v, Mt0=Mt0, par=par: v.tensor_tensor(
                        Mt0, b1[:, par * 128:par * 128 + 128], self.c_mask01,
                        ALU.mult),
                        r=[b1[:, par * 128:par * 128 + 128], self.c_mask01],
                        w=[Mt0])
                    self.mm(bY[:, 0:64], Mt0, vts, start=True, stop=False)
                    self.mm(bY[:, 0:64], qfs, KVBF[rb, hv], start=False,
                            stop=False)
                    self.mm(bY[:, 64:65], Mt0, self.cb_ones[:, 0:1],
                            start=False, stop=False)
                    self.mm(bY[:, 64:65], qfs, KSBF[rb, hk], start=False,
                            stop=True)
                    p.op("vector", lambda v, dmax=dmax, bY=bY: v.tensor_scalar(
                        dmax, bY[:, 64:65], 1e-6, None, ALU.max),
                        r=[bY[:, 64:65]], w=[dmax])
                    p.op("vector", lambda v, rden=rden, dmax=dmax: v.reciprocal(
                        rden, dmax), r=[dmax], w=[rden])
                    p.op("vector", lambda v, ao0=ao0, rden=rden, bY=bY:
                         v.tensor_scalar(ao0, bY[:, 0:64], rden[:, 0:1], None,
                                         ALU.mult),
                         r=[bY[:, 0:64], rden], w=[ao0])
                    sl = bb[0:64, 512 + par * 128: 512 + par * 128 + 128]
                    self.p.op("tensor",
                              lambda t, o=sl, i=ao0, idn=self.cb_ident[0:128, 0:128]:
                              t.transpose(o, i, idn),
                              r=[ao0, self.cb_ident[0:128, 0:128]], w=[sl],
                              pe="single")
                    dst = AOF[(h % 2) * 64:(h % 2) * 64 + 64,
                              cc * 1024 + (h // 2) * 128:
                              cc * 1024 + (h // 2) * 128 + 128]
                    p.op("vector", lambda v, d=dst, sl=sl: v.tensor_copy(d, sl),
                         r=[sl], w=[dst])
                self.mm(b5[0:64, par * 80:par * 80 + 64], kts, vts,
                        start=True, stop=False)
                self.mm(b5[0:64, par * 80 + 64:par * 80 + 65], kts,
                        self.cb_ones[:, 0:1], start=False, stop=True)
                kv = self.KV[:, hv]
                ks = self.KS[:, hk]
                p.op("vector", lambda v, kv=kv, par=par: v.tensor_tensor(
                    kv, kv, b5[0:64, par * 80:par * 80 + 64], ALU.add),
                    r=[kv, b5[0:64, par * 80:par * 80 + 64]], w=[kv])
                p.op("vector", lambda v, ks=ks, par=par: v.tensor_tensor(
                    ks, ks, b5[0:64, par * 80 + 64:par * 80 + 65], ALU.add),
                    r=[ks, b5[0:64, par * 80 + 64:par * 80 + 65]], w=[ks])
                p.op("scalar", lambda s, kv=kv, hv=hv, rb=rb: s.copy(
                    KVBF[rb, hv], kv), r=[kv], w=[KVBF[rb, hv]])
                p.op("scalar", lambda s, ks=ks, hk=hk, rb=rb: s.copy(
                    KSBF[rb, hk], ks), r=[ks], w=[KSBF[rb, hk]])

        def emit_wout(tb):
            bN = [b0, b1, b2, b3]
            for cc in range(4):
                tt = tb * 4 + cc
                for nb in range(2):
                    bo = bN[cc % 2 * 2 + nb]
                    for g in range(8):
                        self.mm(bo[:, 0:512],
                                AOF[:, cc * 1024 + g * 128: cc * 1024 + (g + 1) * 128],
                                WO[:, g * 1024 + nb * 512: g * 1024 + (nb + 1) * 512],
                                start=(g == 0), stop=False)
                    self.mm(bo[:, 0:512], self.cb_ones[0:1, 0:128],
                            self.br_out[0:1, nb * 512:(nb + 1) * 512],
                            start=False, stop=True)
                    xs = self.X[:, tt * 1024 + nb * 512:
                                tt * 1024 + (nb + 1) * 512]
                    p.op("vector", lambda v, xs=xs, bo=bo: v.tensor_tensor(
                        xs, xs, bo[:, 0:512], ALU.add),
                        r=[xs, bo[:, 0:512]], w=[xs])

        loads(0)
        emit_features(0)
        for idx, (tb, hp) in enumerate(pairs):
            if idx + 1 < len(pairs):
                loads(idx + 1)
                emit_features(idx + 1)
            emit_scan(idx, 0)
            emit_scan(idx, 1)
            if not kv_only and hp == 7:
                emit_wout(tb)
        if kv_only:
            p.op("vector", lambda v: v.tensor_scalar(
                self.KVI[:, :], self.KV[:, :], self.sm_oddm[0:64, 0:1], None,
                ALU.mult), r=[self.KV[:, :], self.sm_oddm], w=[self.KVI[:, :]])
            p.op("vector", lambda v: v.tensor_scalar(
                self.KSI[:, :], self.KS[:, :], self.sm_oddm[0:64, 0:1], None,
                ALU.mult), r=[self.KS[:, :], self.sm_oddm], w=[self.KSI[:, :]])

    def l1_moe(self):
        p = self.p
        S16, S32 = self.S16, self.S32
        c10 = S32[:, 0:160]
        W2G = self.WBUF[:, 8192:12288]
        SA = S16[:, 4096:6144]
        H = S16[:, 0:4096]
        b0, b1 = self.psum[0], self.psum[1]
        bRP = [self.psum[6], self.psum[5]]
        p.op("vector", lambda v: v.memset(c10, 1.0), w=[c10])
        RT = S32
        for t in range(NT):
            bR = bRP[t % 2]
            base = 176 + t * 37
            lg = RT[:, base:base + 8]
            mska = RT[:, base + 8:base + 16]
            mskb = RT[:, base + 16:base + 24]
            lmask = RT[:, base + 24:base + 32]
            m1c = RT[:, base + 32:base + 33]
            m2c = RT[:, base + 33:base + 34]
            w1c = RT[:, base + 34:base + 35]
            w2c = RT[:, base + 35:base + 36]
            for dc in range(8):
                self.mm(bR[:, 0:8], self.XF[:, dc * T + t * 128: dc * T + (t + 1) * 128],
                        self.RWsb[:, dc * 8:(dc + 1) * 8],
                        start=(dc == 0), stop=False)
            self.mm(bR[:, 0:8], self.cb_ones[0:1, :], self.br_eb,
                    start=False, stop=True)
            p.op("vector", lambda v, lg=lg, bR=bR: v.tensor_copy(lg, bR[:, 0:8]),
                 r=[bR[:, 0:8]], w=[lg])
            p.op("vector", lambda v, m1c=m1c, lg=lg: v.reduce_max(
                m1c, lg, axis=AX.X), r=[lg], w=[m1c])
            p.op("vector", lambda v, mska=mska, lg=lg, m1c=m1c: v.tensor_scalar(
                mska, lg, m1c[:, 0:1], None, ALU.is_equal),
                r=[lg, m1c], w=[mska])
            p.op("vector", lambda v, lmask=lmask, mska=mska, lg=lg:
                 v.scalar_tensor_tensor(lmask, mska, -1e30, lg, ALU.mult, ALU.add),
                 r=[mska, lg], w=[lmask])
            p.op("vector", lambda v, m2c=m2c, lmask=lmask: v.reduce_max(
                m2c, lmask, axis=AX.X), r=[lmask], w=[m2c])
            p.op("vector", lambda v, mskb=mskb, lg=lg, m2c=m2c: v.tensor_scalar(
                mskb, lg, m2c[:, 0:1], None, ALU.is_equal),
                r=[lg, m2c], w=[mskb])
            p.op("vector", lambda v, w1c=w1c, m1c=m1c, m2c=m2c: v.tensor_sub(
                w1c, m1c, m2c), r=[m1c, m2c], w=[w1c])
            p.op("scalar", lambda s, w1c=w1c: s.activation(w1c, w1c, AF.Sigmoid),
                 r=[w1c], w=[w1c])
            p.op("vector", lambda v, w2c=w2c, m1c=m1c, m2c=m2c: v.tensor_sub(
                w2c, m2c, m1c), r=[m1c, m2c], w=[w2c])
            p.op("scalar", lambda s, w2c=w2c: s.activation(w2c, w2c, AF.Sigmoid),
                 r=[w2c], w=[w2c])
            dst = c10[:, t * 10: t * 10 + 8]
            p.op("vector", lambda v, d=dst, mska=mska, w1c=w1c: v.tensor_scalar(
                d, mska, w1c[:, 0:1], None, ALU.mult), r=[mska, w1c], w=[dst])
            p.op("vector", lambda v, d=dst, mskb=mskb, w2c=w2c:
                 v.scalar_tensor_tensor(d, mskb, w2c[:, 0:1], d, ALU.mult, ALU.add),
                 r=[mskb, w2c, dst], w=[dst])
        W1P = [self.WBUF[:, 0:4096], self.WBUF[:, 16384:20480]]
        W3P = [self.WBUF[:, 4096:8192], self.WBUF[:, 12288:16384]]
        groups = [(e, g) for e in range(NE) for g in range(4)]

        def load13(idx, par):
            e, g = groups[idx]
            for j in range(4):
                hc = g * 4 + j
                p.dma("gpsimd", W1P[par][:, j * 1024:(j + 1) * 1024],
                      self.i_mw1L[e, hc], self.s_w)
                p.dma("gpsimd", W3P[par][:, j * 1024:(j + 1) * 1024],
                      self.i_mw3L[e, hc], self.s_w)

        def load2(idx):
            e, g = groups[idx]
            for j in range(4):
                hc = g * 4 + j
                p.dma("gpsimd", W2G[:, j * 1024:(j + 1) * 1024],
                      self.i_mw2T[e, hc * 128:(hc + 1) * 128, :], self.s_w)

        load13(0, 0)
        load2(0)
        for gi, (e, g) in enumerate(groups):
            par = gi % 2
            if gi + 1 < len(groups):
                load13(gi + 1, 1 - par)
            W1G = W1P[par]
            W3G = W3P[par]
            for tp2 in range(2):
                tbs = (2 * tp2, 2 * tp2 + 1)
                for j in range(4):
                    baP = (b0, self.psum[2]) if j % 2 == 0 else (self.psum[1], self.psum[3])
                    saP = (SA[:, 0:512], SA[:, 512:1024]) if j % 2 == 0 else \
                        (SA[:, 1024:1536], SA[:, 1536:2048])
                    for dc in range(8):
                        for ti, tb in enumerate(tbs):
                            self.mm(baP[ti][:, 0:512],
                                    W1G[:, j * 1024 + dc * 128: j * 1024 + (dc + 1) * 128],
                                    self.XF[:, dc * T + tb * 512: dc * T + (tb + 1) * 512],
                                    start=(dc == 0), stop=(dc == 7))
                    for ti in range(2):
                        p.op("scalar", lambda s, ba=baP[ti], sa=saP[ti]: s.activation(
                            sa, ba[:, 0:512], AF.Silu), r=[baP[ti][:, 0:512]], w=[saP[ti]])
                    for dc in range(8):
                        for ti, tb in enumerate(tbs):
                            self.mm(baP[ti][:, 0:512],
                                    W3G[:, j * 1024 + dc * 128: j * 1024 + (dc + 1) * 128],
                                    self.XF[:, dc * T + tb * 512: dc * T + (tb + 1) * 512],
                                    start=(dc == 0), stop=(dc == 7))
                    for ti, tb in enumerate(tbs):
                        hd = H[:, (ti * 4 + j) * 512:(ti * 4 + j + 1) * 512]
                        p.op("vector", lambda v, hd=hd, ba=baP[ti], sa=saP[ti]:
                             v.tensor_tensor(hd, sa, ba[:, 0:512], ALU.mult),
                             r=[saP[ti], baP[ti][:, 0:512]], w=[hd])
                for ti, tb in enumerate(tbs):
                    for q in range(4):
                        tt = tb * 4 + q
                        for nb in range(2):
                            bo = self.psum[4 + (q * 2 + nb) % 3]
                            for j in range(4):
                                self.mm(bo[:, 0:512],
                                        H[:, (ti * 4 + j) * 512 + q * 128:
                                          (ti * 4 + j) * 512 + q * 128 + 128],
                                        W2G[:, j * 1024 + nb * 512: j * 1024 + (nb + 1) * 512],
                                        start=(j == 0), stop=(j == 3))
                            xs = self.X[:, tt * 1024 + nb * 512:
                                        tt * 1024 + (nb + 1) * 512]
                            ccol = c10[:, tt * 10 + e: tt * 10 + e + 1]
                            p.op("vector", lambda v, xs=xs, bo=bo, ccol=ccol:
                                 v.scalar_tensor_tensor(xs, bo[:, 0:512], ccol,
                                                        xs, ALU.mult, ALU.add),
                                 r=[bo[:, 0:512], ccol, xs], w=[xs])
            if gi + 1 < len(groups):
                load2(gi + 1)


_NC_CACHE = None


def _build_program():
    global _NC_CACHE
    if _NC_CACHE is None:
        m = MK()
        m.setup()
        m.run_l0(m.i_px, save_sinit=True)
        m.fmconv()
        m.l1_attn(kv_only=True)
        m.run_l0(m.i_x)
        m.fmconv()
        m.l1_attn(kv_only=False)
        m.postln()
        m.fmconv()
        m.l1_moe()
        m.postln()
        m.store_out()
        _NC_CACHE = m.finalize()
    return _NC_CACHE


def kernel(**inputs):
    from concourse.bass_utils import run_bass_kernel_spmd
    per_core = host_prep(inputs)
    nc = _build_program()
    res = run_bass_kernel_spmd(nc, per_core, core_ids=list(range(8)))
    out = np.zeros((4, 4096, 1024), np.float32)
    for c in range(8):
        b, h = c // 2, c % 2
        out[b, h * 2048:(h + 1) * 2048] = res.results[c]["out"]
    return out


# version 15
# speedup vs baseline: 1.0854x; 1.0221x over previous
import numpy as np

import concourse.bass as bass
import concourse.mybir as mybir

F32 = mybir.dt.float32
BF16 = mybir.dt.bfloat16
AF = mybir.ActivationFunctionType
ALU = mybir.AluOpType
AX = mybir.AxisListType


class Sem:
    def __init__(self, name):
        self.name = name
        self.n = 0
        self.h = None
        self.is_dma = False
        self.batches = []
        self.open_start = None
        self.close_pending = False


class Prog:
    ENGINES = ("sync", "scalar", "vector", "tensor", "gpsimd")

    def __init__(self, nc):
        self.nc = nc
        self.q = {e: [] for e in self.ENGINES}
        self.sems = []
        self.ninst = 0
        self.auto = {e: self.sem("a_" + e) for e in self.ENGINES}
        self.mem = {}
        self.sat = {e: {} for e in self.ENGINES}
        self.ro = set()
        self.psum_names = set()
        self._pegs = {}

    def sem(self, name):
        s = Sem(name)
        self.sems.append(s)
        return s

    def emit(self, engine, fn):
        self.q[engine].append(fn)
        self.ninst += 1

    def raw_wait(self, engine, sem, val):
        if self.sat[engine].get(sem, 0) >= val or val <= 0:
            return
        self.sat[engine][sem] = val
        self.emit(engine, lambda eng, s=sem, v=val: eng.wait_ge(s.h, v))

    @staticmethod
    def rect(ap):
        t = ap.tensor
        name = t.name
        kind = type(t).__name__
        shape = list(t.shape)
        dims = list(ap.ap)
        off = ap.offset
        if kind == "DRamTensorHandle":
            span = 1
            for st, cnt in dims:
                span += (cnt - 1) * st
            return (name, 0, 1, off, off + span)
        if kind == "PSumTensorHandle":
            return (name, 0, 128, 0, 99999)
        W = shape[1] if len(shape) == 2 else int(np.prod(shape[1:]))
        p0 = off // W
        c0 = off % W
        pn = dims[0][1] if dims else 1
        span = 1
        for st, cnt in dims[1:]:
            span += (cnt - 1) * st
        return (name, p0, p0 + pn, c0, c0 + span)

    @staticmethod
    def _overlap(a, b):
        return a[1] < b[2] and b[1] < a[2] and a[3] < b[4] and b[3] < a[4]

    @staticmethod
    def _covers(a, b):
        return a[1] <= b[1] and a[2] >= b[2] and a[3] <= b[3] and a[4] >= b[4]

    def _dep_val(self, sem, val):
        if not sem.is_dma:
            return val
        for st, en in sem.batches:
            if st < val <= en:
                return en
        if sem.open_start is not None and val > sem.open_start:
            sem.batches.append((sem.open_start, sem.n))
            sem.open_start = None
            sem.close_pending = True
            return sem.n
        return val

    def _dep_wait(self, engine, sem, val):
        self.raw_wait(engine, sem, self._dep_val(sem, val))

    def _resolve(self, engine, reads, writes):
        deps = {}
        rrects, wrects = [], []
        own = self.auto.get(engine)
        for ap in reads:
            r = self.rect(ap)
            rrects.append(r)
            excl = r[0] in self.psum_names
            for rec in self.mem.get(r[0], ()):
                if self._overlap(r, rec) and (
                        rec[5] == "W" or (excl and rec[6] is not own)):
                    deps[rec[6]] = max(deps.get(rec[6], 0), rec[7])
        for ap in writes:
            r = self.rect(ap)
            wrects.append(r)
            for rec in self.mem.get(r[0], ()):
                if self._overlap(r, rec):
                    deps[rec[6]] = max(deps.get(rec[6], 0), rec[7])
        for sem, val in deps.items():
            self._dep_wait(engine, sem, val)
        return rrects, wrects

    def _record(self, engine, rrects, wrects, sem, val):
        for r in rrects:
            if r[0] in self.ro:
                continue
            recs = self.mem.setdefault(r[0], [])
            recs[:] = [q for q in recs
                       if not (q[5] == "R" and q[6] is sem and q[:5] == r[:5])]
            recs.append(r + ("R", sem, val))
        for r in wrects:
            recs = self.mem.setdefault(r[0], [])
            recs[:] = [q for q in recs if not self._covers(r, q)]
            recs.append(r + ("W", sem, val))

    def op(self, engine, fn, r=(), w=(), pe=None):
        sem = self.auto[engine]
        rrects, wrects = self._resolve(engine, r, w)
        if pe is None or pe == "single":
            val = sem.n + 1
            sem.n += 1
            self.emit(engine, lambda eng, f=fn, s=sem: f(eng).then_inc(s.h, 1))
            self._record(engine, rrects, wrects, sem, val)
            return
        key = wrects[0][0]
        if pe == "start":
            assert key not in self._pegs, f"group already open on {key}"
            self._pegs[key] = {"reads": list(rrects), "writes": list(wrects)}
            self.emit(engine, fn)
        elif pe == "mid":
            g = self._pegs[key]
            g["reads"] += rrects
            g["writes"] += wrects
            self.emit(engine, fn)
        elif pe == "stop":
            g = self._pegs.pop(key)
            val = sem.n + 1
            sem.n += 1
            self.emit(engine, lambda eng, f=fn, s=sem: f(eng).then_inc(s.h, 1))
            self._record(engine, g["reads"] + rrects, g["writes"] + wrects,
                         sem, val)
        else:
            raise ValueError(pe)

    def dma(self, engine, out, in_, sem, **kw):
        sem.is_dma = True
        rrects, wrects = self._resolve(engine, [in_], [out])
        if sem.close_pending:
            self.raw_wait(engine, sem, sem.n)
            sem.close_pending = False
        if sem.open_start is None:
            sem.open_start = sem.n
        sem.n += 16
        val = sem.n
        self.emit(engine, lambda eng, o=out, i=in_, s=sem, kw=kw:
                  eng.dma_start(out=o, in_=i, **kw).then_inc(s.h, 16))
        self._record(engine, rrects, wrects, sem, val)
        return val

    def mark_ro(self, *names):
        for n in names:
            self.ro.add(n)
            if n in self.mem:
                self.mem[n] = [q for q in self.mem[n] if q[5] == "W"]

    def join_all(self):
        targets = []
        for e in self.ENGINES:
            s = self.auto[e]
            self.raw_wait(e, s, s.n)
            s.n += 1
            self.emit(e, lambda eng, ss=s: eng.nop().then_inc(ss.h, 1))
            targets.append((s, s.n))
        for e in self.ENGINES:
            for s, v in targets:
                self.raw_wait(e, s, v)

    def build_block(self, ctx):
        for s in self.sems:
            s.h = ctx.enter_context(self.nc.semaphore(s.name))
        with self.nc.Block() as block:
            for name in self.ENGINES:
                cmds = self.q[name]
                def runner(eng, cmds=cmds):
                    for fn in cmds:
                        fn(eng)
                getattr(block, name)(runner)


class SchedProg(Prog):

    def __init__(self, nc):
        super().__init__(nc)
        self.cmds = []
        self.cmem = {}
        self._open_groups = {}
        self.buffering = True

    def _cdeps(self, rrects, wrects):
        deps = set()
        for r in rrects:
            for rec in self.cmem.get(r[0], ()):
                if rec[5] == "W" and self._overlap(r, rec):
                    deps.add(rec[6])
                elif r[0] in self.psum_names and self._overlap(r, rec):
                    deps.add(rec[6])
        for r in wrects:
            for rec in self.cmem.get(r[0], ()):
                if self._overlap(r, rec):
                    deps.add(rec[6])
        return deps

    def _crecord(self, rrects, wrects, cid):
        for r in rrects:
            if r[0] in self.ro:
                continue
            recs = self.cmem.setdefault(r[0], [])
            recs.append(r + ("R", cid))
        for r in wrects:
            recs = self.cmem.setdefault(r[0], [])
            recs[:] = [q for q in recs if not self._covers(r, q)]
            recs.append(r + ("W", cid))

    def op(self, engine, fn, r=(), w=(), pe=None):
        if not self.buffering:
            return super().op(engine, fn, r=r, w=w, pe=pe)
        rrects = [self.rect(ap) for ap in r]
        wrects = [self.rect(ap) for ap in w]
        if pe in (None, "single"):
            cid = len(self.cmds)
            cmd = {"engine": engine, "parts": [(fn, r, w, "single" if pe else None)],
                   "deps": self._cdeps(rrects, wrects), "id": cid}
            self.cmds.append(cmd)
            self._crecord(rrects, wrects, cid)
            return
        key = wrects[0][0]
        if pe == "start":
            cid = len(self.cmds)
            cmd = {"engine": engine, "parts": [(fn, r, w, "start")],
                   "deps": self._cdeps(rrects, wrects), "id": cid,
                   "rr": list(rrects), "ww": list(wrects)}
            self.cmds.append(cmd)
            self._open_groups[key] = cmd
        elif pe in ("mid", "stop"):
            cmd = self._open_groups[key]
            cmd["parts"].append((fn, r, w, pe))
            cmd["deps"] |= self._cdeps(rrects, wrects)
            cmd["rr"] += rrects
            cmd["ww"] += wrects
            if pe == "stop":
                del self._open_groups[key]
                self._crecord(cmd["rr"], cmd["ww"], cmd["id"])

    def dma(self, engine, out, in_, sem, **kw):
        if not self.buffering:
            return super().dma(engine, out, in_, sem, **kw)
        rrects = [self.rect(in_)]
        wrects = [self.rect(out)]
        cid = len(self.cmds)
        cmd = {"engine": engine, "dma": (out, in_, sem, kw),
               "deps": self._cdeps(rrects, wrects), "id": cid}
        self.cmds.append(cmd)
        self._crecord(rrects, wrects, cid)

    def join_all(self):
        if not self.buffering:
            return super().join_all()
        return

    def schedule_and_emit(self):
        import heapq
        n = len(self.cmds)
        fwd = [[] for _ in range(n)]
        indeg = [0] * n
        for c in self.cmds:
            i = c["id"]
            for d in c["deps"]:
                if d != i:
                    fwd[d].append(i)
                    indeg[i] += 1
        level = [0] * n
        ready = [i for i in range(n) if indeg[i] == 0]
        heapq.heapify(ready)
        order = []
        while ready:
            i = heapq.heappop(ready)
            order.append(i)
            for j in fwd[i]:
                if level[i] + 1 > level[j]:
                    level[j] = level[i] + 1
                indeg[j] -= 1
                if indeg[j] == 0:
                    heapq.heappush(ready, j)
        assert len(order) == n, f"cycle in dep graph: {len(order)} != {n}"
        order = sorted(order, key=lambda i: (level[i], i))
        self.buffering = False
        for i in order:
            c = self.cmds[i]
            if "dma" in c:
                out, in_, sem, kw = c["dma"]
                super().dma(c["engine"], out, in_, sem, **kw)
            else:
                for fn, r, w, pe in c["parts"]:
                    super().op(c["engine"], fn, r=r, w=w, pe=pe)


"""Builder for the mk_seq_postln_d2 megakernel (auto-dependency version).

Layout (per core, 2048 tokens = 16 tiles of 128):
- X  : residual stream, token-major fp32 [128, 16*1024]; tile t cols [t*1024,+1024)
- XF : feature-major bf16 [128, 8*2048]; d-chunk dc cols [dc*2048,+2048)
- matmul operands at partition base 0; PSUM: 8 banks [128, 512] f32.
"""
import numpy as np
import ml_dtypes
from contextlib import ExitStack



NT = 16
T = 2048
D = 1024
NCH = 16
NH0 = 8
HD0 = 128
NST = 64
NH1 = 16
HD1 = 64
FD = 64
NE = 10
HID = 2048


def host_prep(inputs):
    f32 = np.float32
    x0 = np.asarray(inputs["L0_attn_x"], f32)
    shared = {}
    shared["wxT"] = np.ascontiguousarray(np.asarray(inputs["L0_attn_W_xproj"], f32).T)
    shared["dtbias"] = np.asarray(inputs["L0_attn_dt_bias"], f32).reshape(1, 8)
    shared["alog"] = np.asarray(inputs["L0_attn_A_log"], f32).reshape(1, 8)
    shared["dvec"] = np.asarray(inputs["L0_attn_D"], f32).reshape(1, 8)
    w1 = np.asarray(inputs["L0_ffn_w1"], f32)
    shared["w1L"] = np.ascontiguousarray(
        w1.reshape(8, 128, 32, 128).transpose(2, 1, 0, 3).reshape(32, 128, 1024))
    shared["b1col"] = np.ascontiguousarray(
        np.asarray(inputs["L0_ffn_b1"], f32).reshape(32, 128).T)
    shared["w2"] = np.asarray(inputs["L0_ffn_w2"], f32)
    shared["b2row"] = np.asarray(inputs["L0_ffn_b2"], f32).reshape(1, 1024)
    wqkv = np.asarray(inputs["L1_attn_Wqkv"], f32)
    shared["qkL"] = np.ascontiguousarray(
        wqkv[:, 0:2048].reshape(8, 128, 32, 64).transpose(2, 1, 0, 3)
        .reshape(32, 128, 512))
    shared["qprL"] = np.ascontiguousarray(
        wqkv[:, 0:1024].reshape(8, 128, 8, 128).transpose(2, 1, 0, 3)
        .reshape(8, 128, 1024))
    shared["kprL"] = np.ascontiguousarray(
        wqkv[:, 1024:2048].reshape(8, 128, 8, 128).transpose(2, 1, 0, 3)
        .reshape(8, 128, 1024))
    blk2r = np.zeros((2, 128), np.float32)
    blk2r[0, 0:64] = 1.0
    blk2r[1, 64:128] = 1.0
    shared["blk2r"] = blk2r
    shared["vL2"] = np.ascontiguousarray(
        wqkv[:, 2048:3072].reshape(8, 128, 8, 128).transpose(2, 1, 0, 3)
        .reshape(8, 128, 1024))
    shared["bqkvrow"] = np.asarray(inputs["L1_attn_bqkv"], f32).reshape(1, 3072)
    shared["wout"] = np.asarray(inputs["L1_attn_Wout"], f32)
    shared["boutrow"] = np.asarray(inputs["L1_attn_bout"], f32).reshape(1, 1024)
    om = np.asarray(inputs["L1_attn_omega"], f32)
    shared["omega"] = np.ascontiguousarray(om.transpose(1, 2, 0, 3).reshape(16, 64, 64))
    shared["qnodes"] = np.asarray(inputs["L1_attn_quad_nodes"], f32).reshape(1, 2)
    shared["qweights"] = np.asarray(inputs["L1_attn_quad_weights"], f32).reshape(1, 2)
    shared["rwT"] = np.ascontiguousarray(np.asarray(inputs["L1_ffn_router_w"], f32).T)
    shared["ebrow"] = np.asarray(inputs["L1_ffn_expert_bias"], f32).reshape(1, 8)
    mw1 = np.concatenate([np.asarray(inputs["L1_ffn_w1"], f32),
                          np.asarray(inputs["L1_ffn_sw1"], f32)], 0)
    mw3 = np.concatenate([np.asarray(inputs["L1_ffn_w3"], f32),
                          np.asarray(inputs["L1_ffn_sw3"], f32)], 0)
    mw2 = np.concatenate([np.asarray(inputs["L1_ffn_w2"], f32),
                          np.asarray(inputs["L1_ffn_sw2"], f32)], 0)
    shared["mw1L"] = np.ascontiguousarray(
        mw1.transpose(0, 2, 1).reshape(10, 8, 128, 16, 128)
        .transpose(0, 3, 2, 1, 4).reshape(10, 16, 128, 1024))
    shared["mw3L"] = np.ascontiguousarray(
        mw3.transpose(0, 2, 1).reshape(10, 8, 128, 16, 128)
        .transpose(0, 3, 2, 1, 4).reshape(10, 16, 128, 1024))
    shared["mw2T"] = np.ascontiguousarray(mw2.transpose(0, 2, 1))
    tri = np.triu(np.ones((128, 128), f32))
    shared["maskneg"] = np.where(tri > 0, 0.0, -1e30).astype(f32)
    shared["mask01"] = tri
    shared["identity"] = np.eye(128, dtype=f32)
    shared["onesmat"] = np.ones((128, 128), f32)
    sel = np.zeros((8, 128, 16), f32)
    for c in range(8):
        for f in range(128):
            sel[c, f, (c * 128 + f) // 64] = 1.0
    shared["sel16"] = sel
    blk = np.zeros((2, 64), f32)
    blk[0, 0:32] = 1.0
    blk[1, 32:64] = 1.0
    shared["blk232"] = blk
    per_core = []
    for c in range(8):
        b, h = c // 2, c % 2
        m = dict(shared)
        m["x"] = np.ascontiguousarray(x0[b, h * 2048:(h + 1) * 2048])
        m["px"] = np.ascontiguousarray(x0[b, 0:2048])
        m["oddmask"] = np.full((128, 1), float(h), f32)
        per_core.append(m)
    return per_core


class MK:
    def __init__(self, sched=True):
        nc = bass.Bass("TRN2", target_bir_lowering=False)
        self.nc = nc
        self.p = SchedProg(nc) if sched else Prog(nc)
        self.ctx = ExitStack()
        self._mk_io()
        self._mk_sbuf()
        p = self.p
        self.s_ld = p.sem("ld")
        self.s_w = p.sem("w")
        self.s_w2 = p.sem("w2")
        self.s_io = p.sem("io")
        self.s_row = p.sem("rowq")
        self.s_st = p.sem("st")

    def _in(self, name, shape):
        return self.nc.dram_tensor(name, list(shape), F32, kind="ExternalInput").ap()

    def _mk_io(self):
        i = self._in
        self.i_x = i("x", (T, D))
        self.i_px = i("px", (T, D))
        self.i_oddmask = i("oddmask", (128, 1))
        self.i_wxT = i("wxT", (D, 136))
        self.i_dtbias = i("dtbias", (1, 8))
        self.i_alog = i("alog", (1, 8))
        self.i_dvec = i("dvec", (1, 8))
        self.i_w1L = i("w1L", (32, 128, 1024))
        self.i_b1col = i("b1col", (128, 32))
        self.i_w2 = i("w2", (4096, D))
        self.i_b2row = i("b2row", (1, D))
        self.i_qkL = i("qkL", (32, 128, 512))
        self.i_qprL = i("qprL", (8, 128, 1024))
        self.i_kprL = i("kprL", (8, 128, 1024))
        self.i_blk2r = i("blk2r", (2, 128))
        self.i_vL2 = i("vL2", (8, 128, 1024))
        self.i_bqkvrow = i("bqkvrow", (1, 3072))
        self.i_wout = i("wout", (D, D))
        self.i_boutrow = i("boutrow", (1, D))
        self.i_omega = i("omega", (16, 64, 64))
        self.i_qnodes = i("qnodes", (1, 2))
        self.i_qweights = i("qweights", (1, 2))
        self.i_rwT = i("rwT", (D, 8))
        self.i_ebrow = i("ebrow", (1, 8))
        self.i_mw1L = i("mw1L", (NE, 16, 128, 1024))
        self.i_mw3L = i("mw3L", (NE, 16, 128, 1024))
        self.i_mw2T = i("mw2T", (NE, HID, D))
        self.i_maskneg = i("maskneg", (128, 128))
        self.i_mask01 = i("mask01", (128, 128))
        self.i_identity = i("identity", (128, 128))
        self.i_onesmat = i("onesmat", (128, 128))
        self.i_sel16 = i("sel16", (8, 128, 16))
        self.i_blk = i("blk232", (2, 64))
        self.o_out = self.nc.dram_tensor("out", [T, D], F32, kind="ExternalOutput").ap()
        self.d_cum = self.nc.dram_tensor("d_cum", [8, T], F32).ap()
        self.d_z = self.nc.dram_tensor("d_z", [8, T], F32).ap()
        self.d_rn = self.nc.dram_tensor("d_rn", [16, T], F32).ap()
        self.d_cT = self.nc.dram_tensor("d_cT", [16, 160], F32).ap()
        self.dumps = {}

    def dump_dram(self, name, shape):
        if name not in self.dumps:
            self.dumps[name] = self.nc.dram_tensor(
                "dump_" + name, list(shape), F32, kind="ExternalOutput").ap()
        return self.dumps[name]

    def _sb(self, name, shape, dt):
        return self.ctx.enter_context(self.nc.sbuf_tensor(name, list(shape), dt))

    def _mk_sbuf(self):
        sb = self._sb
        self.X = sb("X", (128, NT * 1024), F32)
        self.XF = sb("XF", (128, 8 * T), BF16)
        self.S16 = sb("S16", (128, 8192), BF16)
        self.S32 = sb("S32", (128, 768), F32)
        self.WBUF = sb("WBUF", (128, 20480), BF16)
        self.CONST = sb("CONST", (128, 768), F32)
        self.CB16 = sb("CB16", (128, 512), BF16)
        self.SMALL = sb("SMALL", (128, 512), F32)
        self.ROW2A = sb("ROW2A", (2, 1024), F32)
        self.ROW2B = sb("ROW2B", (2, 1024), F32)
        self.ROWZ = sb("ROWZ", (1, 1024), F32)
        self.ZTM = sb("ZTM", (128, 128), F32)
        self.ROWD2 = sb("ROWD2", (8, T), F32)
        self.STATE = sb("STATE", (64, NH0 * HD0), F32)
        self.SINIT = sb("SINIT", (64, NH0 * HD0), F32)
        self.KVI = sb("KVI", (64, NH1 * HD1), F32)
        self.KS = sb("KS", (64, NH1), F32)
        self.KSI = sb("KSI", (64, NH1), F32)
        self.KV = self.STATE
        self.BROWS = sb("BROWS", (1, 2080), BF16)
        self.WX = sb("WX", (128, 8 * 136), BF16)
        self.OM = sb("OM", (64, 16 * 64), BF16)
        self.B1C = sb("B1C", (128, 32), F32)
        self.DROW = sb("DROW", (1, 8), F32)
        self.BLK = sb("BLK", (2, 64), F32)
        self.PSTAT = sb("PSTAT", (128, 96), F32)
        self.BTM2 = sb("BTM2", (128, 1024), BF16)
        self.psum = [self.ctx.enter_context(
            self.nc.psum_tensor(f"ps{i}", [128, 512], F32)) for i in range(7)]
        self.psumb = self.ctx.enter_context(
            self.nc.psum_tensor("psb", [128, 1024], BF16))
        self.p.psum_names = {f"ps{i}" for i in range(7)} | {"psb"}

    def mm(self, out, lhsT, rhs, start=True, stop=True):
        pe = "single" if (start and stop) else (
            "start" if start else ("stop" if stop else "mid"))
        self.p.op("tensor",
                  lambda t, o=out, l=lhsT, r=rhs, st=start, sp=stop:
                  t.matmul(o, l, r, start=st, stop=sp),
                  r=[lhsT, rhs], w=[out], pe=pe)

    def tr(self, out, in_, n):
        if in_.tensor.dtype == BF16:
            ident = self.cb_ident[0:n, 0:n]
        else:
            ident = self.c_ident[0:n, 0:n]
        self.p.op("tensor",
                  lambda t, o=out, i=in_, idn=ident: t.transpose(o, i, idn),
                  r=[in_, ident], w=[out], pe="single")

    def setup(self):
        p = self.p
        C, S = self.CONST, self.SMALL
        self.c_ident = C[:, 0:128]
        self.c_maskneg = C[:, 128:256]
        self.c_mask01 = C[:, 256:384]
        self.c_ones = C[:, 384:512]
        self.c_dbc1 = C[:, 512:520]
        self.c_sel16 = C[:, 520:648]
        self.cb_ones = self.CB16[:, 0:128]
        self.cb_ident = self.CB16[:, 128:256]
        self.br_b2 = self.BROWS[0:1, 0:1024]
        self.br_out = self.BROWS[0:1, 1024:2048]
        self.br_eb = self.BROWS[0:1, 2048:2056]
        self.br_dtb = self.BROWS[0:1, 2056:2064]
        self.sm_dtsp = S[:, 0:128]
        self.sm_wtm = S[:, 128:256]
        self.sm_zlast = S[0:64, 256:257]
        self.sm_wtmp = S[0:8, 260:388]
        self.sm_arow = S[0:1, 388:396]
        self.sm_acol = S[0:8, 396:397]
        self.sm_qrow = S[0:1, 400:402]
        self.sm_qwrow = S[0:1, 402:404]
        self.sm_s2col = S[0:64, 404:405]
        self.sm_negscol = S[0:64, 405:406]
        self.sm_lnccol = S[0:64, 406:407]
        self.sm_oddm = S[:, 408:409]
        self.sm_rn = S[:, 412:428]
        self.sm_den = S[:, 428:444]
        self.sm_c10 = S[:, 444:454]
        self.sm_m1 = S[:, 456:457]
        self.sm_m2 = S[:, 457:458]
        self.sm_lg = S[:, 460:470]
        ld = self.s_ld
        w = self.s_w
        d = lambda out, in_: p.dma("sync", out, in_, ld)
        dc = lambda out, in_: p.dma("gpsimd", out, in_, w)
        d(self.c_ident, self.i_identity)
        d(self.c_maskneg, self.i_maskneg)
        d(self.c_mask01, self.i_mask01)
        d(self.c_ones, self.i_onesmat)
        d(self.c_sel16.rearrange("p (c s) -> p c s", c=8),
          self.i_sel16.rearrange("c p s -> p c s"))
        dc(self.cb_ones, self.i_onesmat[:, 0:128])
        dc(self.cb_ident, self.i_identity)
        dc(self.br_b2, self.i_b2row)
        dc(self.br_out, self.i_boutrow)
        dc(self.br_eb, self.i_ebrow)
        dc(self.br_dtb, self.i_dtbias)
        for dch in range(8):
            dc(self.WX[:, dch * 136:(dch + 1) * 136],
               self.i_wxT[dch * 128:(dch + 1) * 128, :])
        dc(self.OM.rearrange("d (h m) -> d h m", h=16),
           self.i_omega.rearrange("h d m -> d h m"))
        self.OM2 = self._sb("OM2", (128, 16 * 64), BF16)
        dc(self.OM2[64:128, :].rearrange("d (h m) -> d h m", h=16),
           self.i_omega.rearrange("h d m -> d h m"))
        self.CBY = self._sb("CBY", (128, 2), BF16)
        self.cb_sel2 = self.CBY[:, :]
        dc(self.cb_sel2, self.i_sel16[0][:, 0:2])
        self.cb_blk2r = self.CB16[0:2, 458:586] if False else None
        self.CBX = self._sb("CBX", (2, 128), BF16)
        dc(self.CBX[:, :], self.i_blk2r)
        self.RWsb = self._sb("RWsb", (128, 64), BF16)
        for dch in range(8):
            dc(self.RWsb[:, dch * 8:(dch + 1) * 8],
               self.i_rwT[dch * 128:(dch + 1) * 128, :])
        d(self.B1C[:, :], self.i_b1col)
        d(self.sm_arow, self.i_alog)
        d(self.sm_qrow, self.i_qnodes)
        d(self.sm_qwrow, self.i_qweights)
        d(self.sm_oddm, self.i_oddmask)
        d(self.DROW[0:1, :], self.i_dvec)
        d(self.BLK[:, :], self.i_blk)
        self.constvals = {}
        for k, val in enumerate([1e-5, 1.0, -1.0]):
            ap = S[:, 257 + k: 258 + k]
            p.op("vector", lambda v, ap=ap, val=val: v.memset(ap, val), w=[ap])
            self.nc.const_aps.register(F32, val, ap)
            self.constvals[val] = ap
        p.op("vector", lambda v: v.memset(self.ROW2A[:, :], 1.0),
             w=[self.ROW2A[:, :]])
        p.op("vector", lambda v: v.memset(self.ROW2B[:, :], -1.0),
             w=[self.ROW2B[:, :]])
        p.op("vector", lambda v: v.memset(self.SINIT[:, :], 0.0),
             w=[self.SINIT[:, :]])
        bD = self.psum[5]
        self.mm(bD[:, 0:8], self.c_ones[0:1, 0:128], self.DROW[0:1, :])
        p.op("vector", lambda v: v.tensor_scalar(
            self.c_dbc1, bD[:, 0:8], 1.0, None, ALU.add),
            r=[bD[:, 0:8]], w=[self.c_dbc1])
        t8 = self.sm_wtmp
        p.op("scalar", lambda s: s.activation(t8[0:1, 0:8], self.sm_arow, AF.Exp),
             r=[self.sm_arow], w=[t8[0:1, 0:8]])
        p.op("vector", lambda v: v.tensor_scalar(
            t8[0:1, 8:16], t8[0:1, 0:8], -1.0, None, ALU.mult),
            r=[t8[0:1, 0:8]], w=[t8[0:1, 8:16]])
        bA = self.psum[6]
        self.tr(bA[0:8, 0:1], t8[0:1, 8:16], 1)
        p.op("vector", lambda v: v.tensor_copy(self.sm_acol, bA[0:8, 0:1]),
             r=[bA[0:8, 0:1]], w=[self.sm_acol])
        q = self.sm_qrow
        t0 = S[0:1, 470:472]; t1 = S[0:1, 472:474]; t2 = S[0:1, 474:476]
        t3 = S[0:1, 476:478]; t4 = S[0:1, 478:480]; t5 = S[0:1, 480:482]
        p.op("vector", lambda v: v.tensor_scalar(t0, q, 0.0, None, ALU.max),
             r=[q], w=[t0])
        p.op("scalar", lambda s: s.activation(t1, t0, AF.Sqrt, scale=2.0),
             r=[t0], w=[t1])
        p.op("vector", lambda v: v.tensor_scalar(t3, self.sm_qwrow, 0.0, None,
             ALU.max), r=[self.sm_qwrow], w=[t3])
        p.op("scalar", lambda s: s.activation(t4, t3, AF.Ln, scale=1.0 / 32.0),
             r=[t3], w=[t4])
        p.op("vector", lambda v: v.tensor_scalar(t5, q, -1.0, None, ALU.mult),
             r=[q], w=[t5])
        p.op("vector", lambda v: v.tensor_tensor(t2, t5, t4, ALU.add),
             r=[t5, t4], w=[t2])
        bB = self.psum[6]
        self.tr(bB[0:2, 0:1], t1, 1)
        self.tr(bB[0:2, 1:2], t2, 1)
        self.tr(bB[0:2, 2:3], t5, 1)
        vals = S[0:2, 482:485]
        p.op("vector", lambda v: v.tensor_copy(vals, bB[0:2, 0:3]),
             r=[bB[0:2, 0:3]], w=[vals])
        bC = self.psum[6]
        self.mm(bC[0:64, 0:3], self.BLK[:, :], vals)
        p.op("vector", lambda v: v.tensor_copy(self.sm_s2col, bC[0:64, 0:1]),
             r=[bC[0:64, 0:1]], w=[self.sm_s2col])
        p.op("vector", lambda v: v.tensor_copy(self.sm_lnccol, bC[0:64, 1:2]),
             r=[bC[0:64, 1:2]], w=[self.sm_lnccol])
        p.op("vector", lambda v: v.tensor_copy(self.sm_negscol, bC[0:64, 2:3]),
             r=[bC[0:64, 2:3]], w=[self.sm_negscol])
        self.p.mark_ro("identity", "maskneg", "mask01", "onesmat", "sel16",
                       "blk232", "b1col", "alog", "qnodes", "qweights",
                       "oddmask", "dvec", "b2row", "boutrow", "ebrow", "dtbias",
                       "wxT", "omega", "w1L", "w2", "qkL", "vL2", "bqkvrow",
                       "wout", "rwT", "mw1L", "mw3L", "mw2T", "x", "px",
                       "CONST", "BROWS", "WX", "OM", "B1C", "DROW",
                       "BLK")

    def load_stream(self, src, split=False):
        for t in range(NT):
            eng, sem = (("gpsimd", self.s_w) if (split and t % 2 == 1)
                        else ("sync", self.s_ld))
            self.p.dma(eng, self.X[:, t * 1024:(t + 1) * 1024],
                       src[t * 128:(t + 1) * 128, :], sem)

    def fmconv(self):
        p = self.p
        for k, (t, dc) in enumerate([(t, dc) for t in range(NT) for dc in range(8)]):
            b = self.psum[k % 4]
            xs = self.X[:, t * 1024 + dc * 128: t * 1024 + (dc + 1) * 128]
            self.tr(b[:, 0:128], xs, 128)
            dst = self.XF[:, dc * T + t * 128: dc * T + (t + 1) * 128]
            if k % 2 == 0:
                p.op("vector", lambda v, d=dst, bb=b: v.tensor_copy(d, bb[:, 0:128]),
                     r=[b[:, 0:128]], w=[dst])
            else:
                p.op("scalar", lambda s, d=dst, bb=b: s.copy(d, bb[:, 0:128]),
                     r=[b[:, 0:128]], w=[dst])

    def l0_prep(self, sub=99):
        p = self.p
        S = self.SMALL
        self.B_fm = self.S16[0:64, 0:2048]
        self.C_fm = self.S16[0:64, 2048:4096]
        self.B_tmall = self.BTM2[:, 0:1024]
        self.SBF = self.S16[0:64, 5120:6144]
        bB, bC = self.psum[3], self.psum[4]
        for tb in range(4):
            for which, bank, co in (("B", bB, 8), ("C", bC, 72)):
                for dc in range(8):
                    self.mm(bank[0:64, 0:512],
                            self.WX[:, dc * 136 + co: dc * 136 + co + 64],
                            self.XF[:, dc * T + tb * 512: dc * T + (tb + 1) * 512],
                            start=(dc == 0), stop=(dc == 7))
                dst = (self.B_fm if which == "B" else self.C_fm)[:, tb * 512:(tb + 1) * 512]
                p.op("vector", lambda v, d=dst, bb=bank: v.tensor_copy(
                    d, bb[0:64, 0:512]), r=[bank[0:64, 0:512]], w=[dst])
        if sub < 1:
            return
        import os
        nch_lim = int(os.environ.get("MKCH", NCH))
        pieces = os.environ.get("MKPIECE", "abcd")
        bT = self.psum[5]
        bX = self.psum[6]
        sp1 = S[:, 504:512]
        sp2 = S[:, 496:504]
        for c in range(min(NCH, nch_lim)):
            for dc in range(8):
                self.mm(bT[:, 0:72],
                        self.XF[:, dc * T + c * 128: dc * T + (c + 1) * 128],
                        self.WX[:, dc * 136: dc * 136 + 72],
                        start=(dc == 0), stop=False)
            self.mm(bT[:, 0:8], self.cb_ones[0:1, :], self.br_dtb,
                    start=False, stop=True)
            if "a" not in pieces:
                p.op("vector", lambda v: v.tensor_copy(
                    S[:, 504:505], bT[:, 0:1]), r=[bT[:, 0:1]], w=[S[:, 504:505]])
                continue
            p.op("vector", lambda v, c=c: v.tensor_copy(
                self.B_tmall[:, c * 64:(c + 1) * 64], bT[:, 8:72]),
                r=[bT[:, 8:72]], w=[self.B_tmall[:, c * 64:(c + 1) * 64]])
            if "b" not in pieces:
                continue
            lvl = int(os.environ.get("MKB", "5"))
            if lvl >= 1:
                fn_ = {"sq": AF.Square, "abs": AF.Abs, "exp": AF.Exp,
                       "relu": AF.Relu}[os.environ.get("MKABS", "abs")]
                p.op("scalar", lambda s, fn_=fn_: s.activation(
                    sp1, bT[:, 0:8], fn_), r=[bT[:, 0:8]], w=[sp1])
            if lvl >= 2:
                p.op("scalar", lambda s: s.activation(sp1, sp1, AF.Exp, scale=-1.0),
                     r=[sp1], w=[sp1])
            if lvl >= 3:
                p.op("scalar", lambda s: s.activation(sp1, sp1, AF.Ln, bias=1.0),
                     r=[sp1], w=[sp1])
            if lvl >= 4:
                p.op("scalar", lambda s: s.activation(sp2, bT[:, 0:8], AF.Relu),
                     r=[bT[:, 0:8]], w=[sp2])
            dst = self.sm_dtsp[:, c * 8:(c + 1) * 8]
            if lvl >= 5:
                p.op("vector", lambda v, d=dst: v.tensor_tensor(d, sp1, sp2, ALU.add),
                     r=[sp1, sp2], w=[dst])
            else:
                p.op("vector", lambda v, d=dst: v.tensor_copy(d, sp1),
                     r=[sp1], w=[dst])
            if "c" not in pieces:
                continue
            self.tr(bX[0:8, 0:128], dst, 128)
            rd = self.ROWD2[0:8, c * 128:(c + 1) * 128]
            p.op("vector", lambda v, rd=rd: v.tensor_copy(rd, bX[0:8, 0:128]),
                 r=[bX[0:8, 0:128]], w=[rd])
        if sub < 2:
            return
        RD = self.ROWD2[0:8, :]
        p.op("vector", lambda v: v.tensor_scalar(
            RD, RD, self.sm_acol, None, ALU.mult), r=[RD, self.sm_acol], w=[RD])
        for c in range(NCH):
            seg = self.ROWD2[0:8, c * 128:(c + 1) * 128]
            init = 0.0 if c == 0 else self.ROWD2[0:8, c * 128 - 1:c * 128]
            rr = [seg, self.c_ones[0:8, 0:128]] + ([init] if c > 0 else [])
            p.op("vector", lambda v, seg=seg, init=init: v.tensor_tensor_scan(
                seg, seg, self.c_ones[0:8, 0:128], init, ALU.add, ALU.mult),
                r=rr, w=[seg])
        if sub < 3:
            return
        p.dma("sync", self.d_cum, RD, self.s_io)
        if sub < 4:
            return
        for c in range(NCH):
            wt = self.sm_wtmp
            p.op("vector", lambda v, c=c: v.tensor_scalar(
                wt, self.ROWD2[0:8, c * 128:(c + 1) * 128],
                self.ROWD2[0:8, c * 128 + 127:c * 128 + 128], None, ALU.subtract),
                r=[self.ROWD2[0:8, c * 128:(c + 1) * 128]], w=[wt])
            p.op("scalar", lambda s: s.activation(wt, wt, AF.Exp, scale=-1.0),
                 r=[wt], w=[wt])
            bW = self.psum[6]
            self.tr(bW[:, 0:8], wt, 8)
            dst = self.sm_wtm[:, c * 8:(c + 1) * 8]
            p.op("vector", lambda v, d=dst: v.tensor_copy(d, bW[:, 0:8]),
                 r=[bW[:, 0:8]], w=[dst])
        if sub < 5:
            return
        for c in range(NCH - 1, 0, -1):
            seg = self.ROWD2[0:8, c * 128:(c + 1) * 128]
            prev = self.ROWD2[0:8, c * 128 - 1:c * 128]
            p.op("vector", lambda v, seg=seg, prev=prev: v.tensor_scalar(
                seg, seg, prev, None, ALU.subtract), r=[seg, prev], w=[seg])
        p.op("scalar", lambda s: s.activation(RD, RD, AF.Exp), r=[RD], w=[RD])
        p.dma("sync", self.d_z, RD, self.s_io)
        for c in range(NCH):
            bZt = self.psum[(2, 5, 6)[c % 3]]
            self.tr(bZt[:, 0:8], self.ROWD2[0:8, c * 128:(c + 1) * 128], 8)
            dst = self.ZTM[:, c * 8:(c + 1) * 8]
            p.op("vector", lambda v, d=dst, b=bZt: v.tensor_copy(d, b[:, 0:8]),
                 r=[bZt[:, 0:8]], w=[dst])

    def l0_mixer(self):
        p = self.p
        p.op("vector", lambda v: v.tensor_copy(self.STATE[:, :], self.SINIT[:, :]),
             r=[self.SINIT[:, :]], w=[self.STATE[:, :]])
        p.op("vector", lambda v: v.tensor_copy(self.SBF, self.STATE[:, :]),
             r=[self.STATE[:, :]], w=[self.SBF])
        XhP = [self.S16[:, 6144:6272], self.S16[:, 6656:6784]]
        XhwP = [self.S16[:, 6272:6400], self.S16[:, 6784:6912]]
        MtP = [self.S16[:, 6400:6528], self.S16[:, 6912:7040]]
        CzP = [self.S16[0:64, 6528:6656], self.S16[0:64, 7040:7168]]
        EmatP = [self.S32[:, 0:128], self.S32[:, 128:256]]
        bDP = [self.psum[0], self.psum[6]]
        bG, bZ, bS = self.psum[1], self.psum[2], self.psum[5]
        bY = [self.psum[3], self.psum[4]]
        for c in range(NCH):
            p.dma("sync", self.ROW2A[0:1, :].rearrange("p (h t) -> p h t", h=8),
                  self.d_cum[:, c * 128:(c + 1) * 128], self.s_row)
            p.dma("sync", self.ROW2B[1:2, :].rearrange("p (h t) -> p h t", h=8),
                  self.d_cum[:, c * 128:(c + 1) * 128], self.s_row)
            p.dma("sync", self.ROWZ[0:1, :].rearrange("p (h t) -> p h t", h=8),
                  self.d_z[:, c * 128:(c + 1) * 128], self.s_row)
            Bc = self.B_fm[:, c * 128:(c + 1) * 128]
            Cc = self.C_fm[:, c * 128:(c + 1) * 128]
            self.mm(bG[:, 0:128], Bc, Cc)

            def stage_a(h, c=c, Cc=Cc):
                hs = slice(h * 128, (h + 1) * 128)
                Xh, Xhw = XhP[h % 2], XhwP[h % 2]
                Mt, Emat = MtP[h % 2], EmatP[h % 2]
                bD = bDP[h % 2]
                self.mm(bD[:, 0:128], self.ROW2A[:, hs], self.ROW2B[:, hs])
                zbc = bZ[0:64, (h % 2):(h % 2) + 1]
                self.mm(zbc, self.c_ones[0:1, 0:64],
                        self.ROWZ[0:1, h * 128 + 127:h * 128 + 128])
                p.op("vector", lambda v, Emat=Emat, bD=bD: v.scalar_tensor_tensor(
                    Emat, bD[:, 0:128], 0.0, self.c_maskneg, ALU.min, ALU.add),
                    r=[bD[:, 0:128], self.c_maskneg], w=[Emat])
                p.op("scalar", lambda s, Emat=Emat: s.activation(Emat, Emat, AF.Exp),
                     r=[Emat], w=[Emat])
                p.op("vector", lambda v, Mt=Mt, Emat=Emat: v.tensor_tensor(
                    Mt, bG[:, 0:128], Emat, ALU.mult),
                    r=[bG[:, 0:128], Emat], w=[Mt])
                zl = self.SMALL[0:64, 256:257] if h % 2 == 0 else \
                    self.SMALL[0:64, 260:261]
                p.op("vector", lambda v, zl=zl, zbc=zbc: v.tensor_copy(
                    zl, zbc), r=[zbc], w=[zl])
                xsl = self.X[:, c * 1024 + h * 128: c * 1024 + (h + 1) * 128]
                dcol = self.sm_dtsp[:, c * 8 + h: c * 8 + h + 1]
                wcol = self.sm_wtm[:, c * 8 + h: c * 8 + h + 1]
                p.op("vector", lambda v, xsl=xsl, dcol=dcol, Xh=Xh: v.tensor_scalar(
                    Xh, xsl, dcol, None, ALU.mult), r=[xsl, dcol], w=[Xh])
                p.op("vector", lambda v, wcol=wcol, Xh=Xh, Xhw=Xhw: v.tensor_scalar(
                    Xhw, Xh, wcol, None, ALU.mult), r=[Xh, wcol], w=[Xhw])

            def stage_b(h, c=c):
                hs = slice(h * 128, (h + 1) * 128)
                Xh, Xhw = XhP[h % 2], XhwP[h % 2]
                Mt = MtP[h % 2]
                Cc2 = self.C_fm[:, c * 128:(c + 1) * 128]
                zl = self.SMALL[0:64, 256:257] if h % 2 == 0 else \
                    self.SMALL[0:64, 260:261]
                by = bY[h % 2]
                p.op("tensor", lambda t, o=by[:, 0:128], l=Mt, rr=Xh:
                     t.matmul(o, l, rr, start=True, stop=True),
                     r=[Mt, Xh], w=[by[:, 0:128]], pe="start")
                p.op("tensor",
                     lambda t, o=by[:, 128:256], l=Cc2, rr=self.SBF[:, hs]:
                     t.matmul(o, l, rr, start=True, stop=True),
                     r=[Cc2, self.SBF[:, hs]], w=[by[:, 128:256]], pe="stop")
                self.mm(bS[0:64, (h % 2) * 128:(h % 2) * 128 + 128],
                        self.B_tmall[:, c * 64:(c + 1) * 64], Xhw)
                bSs = bS[0:64, (h % 2) * 128:(h % 2) * 128 + 128]
                xsl = self.X[:, c * 1024 + h * 128: c * 1024 + (h + 1) * 128]
                dco = self.c_dbc1[:, h:h + 1]
                zc = self.ZTM[:, c * 8 + h:c * 8 + h + 1]
                p.op("vector", lambda v, xsl=xsl, dco=dco, by=by:
                     v.scalar_tensor_tensor(xsl, xsl, dco, by[:, 0:128],
                                            ALU.mult, ALU.add),
                     r=[xsl, dco, by[:, 0:128]], w=[xsl])
                p.op("vector", lambda v, xsl=xsl, zc=zc, by=by:
                     v.scalar_tensor_tensor(xsl, by[:, 128:256], zc, xsl,
                                            ALU.mult, ALU.add),
                     r=[by[:, 128:256], zc, xsl], w=[xsl])
                st = self.STATE[:, hs]
                p.op("vector", lambda v, st=st, zl=zl, bSs=bSs:
                     v.scalar_tensor_tensor(st, st, zl, bSs, ALU.mult, ALU.add),
                     r=[st, zl, bSs], w=[st])
                p.op("scalar", lambda s, st=st, hs=hs: s.copy(
                    self.SBF[:, hs], st), r=[st], w=[self.SBF[:, hs]])

            stage_a(0)
            for h in range(NH0):
                if h + 1 < NH0:
                    stage_a(h + 1)
                stage_b(h)

    def postln(self):
        p = self.p
        PS = self.PSTAT
        bQP = [self.psum[6], self.psum[0]]
        SUM = PS[:, 0:16]
        SQA = PS[:, 16:32]
        SQB = PS[:, 32:48]
        NM = PS[:, 48:64]
        RS = PS[:, 64:80]
        VAR = PS[:, 80:96]
        for t in range(NT):
            bQ = bQP[t % 2]
            xs = self.X[:, t * 1024:(t + 1) * 1024]
            x0 = self.X[:, t * 1024:t * 1024 + 512]
            x1 = self.X[:, t * 1024 + 512:(t + 1) * 1024]
            sumc = PS[:, t:t + 1]
            sqa = PS[:, 16 + t:17 + t]
            sqb = PS[:, 32 + t:33 + t]
            p.op("vector", lambda v, xs=xs, sumc=sumc: v.reduce_sum(
                sumc, xs, axis=AX.X), r=[xs], w=[sumc])
            p.op("scalar", lambda s, x0=x0, bQ=bQ, sqa=sqa: s.activation(
                bQ[:, 0:512], x0, AF.Square, accum_out=sqa),
                r=[x0], w=[bQ[:, 0:512], sqa])
            p.op("scalar", lambda s, x1=x1, bQ=bQ, sqb=sqb: s.activation(
                bQ[:, 0:512], x1, AF.Square, accum_out=sqb),
                r=[x1], w=[bQ[:, 0:512], sqb])
        p.op("vector", lambda v: v.tensor_tensor(SQA, SQA, SQB, ALU.add),
             r=[SQA, SQB], w=[SQA])
        p.op("vector", lambda v: v.tensor_scalar(
            NM, SUM, -1.0 / 1024.0, None, ALU.mult), r=[SUM], w=[NM])
        p.op("vector", lambda v: v.tensor_tensor(VAR, NM, NM, ALU.mult),
             r=[NM], w=[VAR])
        p.op("vector", lambda v: v.tensor_scalar(
            SQA, SQA, 1.0 / 1024.0, None, ALU.mult), r=[SQA], w=[SQA])
        p.op("vector", lambda v: v.tensor_sub(VAR, SQA, VAR),
             r=[SQA, VAR], w=[VAR])
        p.op("scalar", lambda s: s.activation(VAR, VAR, AF.Ln, bias=1e-5),
             r=[VAR, self.constvals[1e-5]], w=[VAR])
        p.op("scalar", lambda s: s.activation(RS, VAR, AF.Exp, scale=-0.5),
             r=[VAR], w=[RS])
        for t in range(NT):
            xs = self.X[:, t * 1024:(t + 1) * 1024]
            nm = PS[:, 48 + t:49 + t]
            rs = PS[:, 64 + t:65 + t]
            p.op("vector", lambda v, xs=xs, nm=nm, rs=rs: v.tensor_scalar(
                xs, xs, nm[:, 0:1], rs[:, 0:1], ALU.add, ALU.mult),
                r=[xs, nm, rs], w=[xs])

    def l0_ffn(self):
        p = self.p
        W1G = self.WBUF[:, 0:8192]
        W2G = self.WBUF[:, 8192:16384]
        H1 = self.S16[:, 0:8192]
        for g in range(4):
            for j in range(8):
                hc = g * 8 + j
                p.dma("gpsimd", W1G[:, j * 1024:(j + 1) * 1024],
                      self.i_w1L[hc], self.s_w)
                p.dma("gpsimd", W2G[:, j * 1024:(j + 1) * 1024],
                      self.i_w2[hc * 128:(hc + 1) * 128, :], self.s_w2)
            for tp2 in range(2):
                tbs = (2 * tp2, 2 * tp2 + 1)
                for j in range(8):
                    baP = (self.psum[0], self.psum[2]) if j % 2 == 0 else \
                        (self.psum[1], self.psum[3])
                    hc = g * 8 + j
                    for dc in range(8):
                        for ti, tb in enumerate(tbs):
                            self.mm(baP[ti][:, 0:512],
                                    W1G[:, j * 1024 + dc * 128: j * 1024 + (dc + 1) * 128],
                                    self.XF[:, dc * T + tb * 512: dc * T + (tb + 1) * 512],
                                    start=(dc == 0), stop=(dc == 7))
                    for ti in range(2):
                        h1d = H1[:, (ti * 8 + j) * 512:(ti * 8 + j + 1) * 512]
                        bc = self.B1C[:, hc:hc + 1]
                        p.op("scalar", lambda s, h1d=h1d, ba=baP[ti], bc=bc:
                             s.activation(h1d, ba[:, 0:512], AF.Gelu, bias=bc),
                             r=[baP[ti][:, 0:512], bc], w=[h1d])
                for ti, tb in enumerate(tbs):
                    for q in range(4):
                        tt = tb * 4 + q
                        for nb in range(2):
                            bo = self.psum[4 + (q * 2 + nb) % 3]
                            for j in range(8):
                                last = (j == 7) and (g != 0)
                                self.mm(bo[:, 0:512],
                                        H1[:, (ti * 8 + j) * 512 + q * 128:
                                          (ti * 8 + j) * 512 + q * 128 + 128],
                                        W2G[:, j * 1024 + nb * 512: j * 1024 + (nb + 1) * 512],
                                        start=(j == 0), stop=last)
                            if g == 0:
                                self.mm(bo[:, 0:512], self.cb_ones[0:1, 0:128],
                                        self.br_b2[0:1, nb * 512:(nb + 1) * 512],
                                        start=False, stop=True)
                            xs = self.X[:, tt * 1024 + nb * 512:
                                        tt * 1024 + (nb + 1) * 512]
                            p.op("vector", lambda v, xs=xs, bo=bo: v.tensor_tensor(
                                xs, xs, bo[:, 0:512], ALU.add),
                                r=[xs, bo[:, 0:512]], w=[xs])

    def run_l0(self, src, save_sinit=False):
        p = self.p
        self.load_stream(src, split=save_sinit)
        self.fmconv()
        self.l0_prep()
        self.l0_mixer()
        if save_sinit:
            p.op("vector", lambda v: v.tensor_scalar(
                self.SINIT[:, :], self.STATE[:, :], self.sm_oddm[0:64, 0:1],
                None, ALU.mult),
                r=[self.STATE[:, :], self.sm_oddm], w=[self.SINIT[:, :]])
        self.postln()
        self.fmconv()
        self.l0_ffn()
        self.postln()

    def dump_x(self, name):
        ap = self.dump_dram(name, (T, D))
        for t in range(NT):
            self.p.dma("sync", ap[t * 128:(t + 1) * 128, :],
                       self.X[:, t * 1024:(t + 1) * 1024], self.s_st)

    def dump_dram_copy(self, name, src_ap, shape):
        ap = self.dump_dram(name, shape)
        self.p.dma("sync", ap, src_ap, self.s_st)

    def dump_sbuf(self, name, sb_ap, shape):
        ap = self.dump_dram(name, shape)
        self.p.dma("sync", ap, sb_ap, self.s_st)

    def store_out(self):
        for t in range(NT):
            self.p.dma("sync", self.o_out[t * 128:(t + 1) * 128, :],
                       self.X[:, t * 1024:(t + 1) * 1024], self.s_st)

    def finalize(self):
        if hasattr(self.p, "schedule_and_emit"):
            self.p.schedule_and_emit()
        self.p.join_all()
        self.p.raw_wait("sync", self.s_st, self.s_st.n)
        self.p.build_block(self.ctx)
        self.ctx.close()
        return self.nc

    def l1_attn(self, kv_only=False):
        p = self.p
        S16, S32, S = self.S16, self.S32, self.SMALL
        W = self.WBUF
        AOF = S16[:, 0:4096]
        qfP = [S16[:, 4096:4608], W[:, 12288:12800]]
        kfP = [S16[:, 4608:5120], W[:, 12800:13312]]
        ktP = [S16[:, 5120:5632], W[:, 13312:13824]]
        vtP = [S16[:, 5632:6144], W[:, 13824:14336]]
        qnP = [W[:, 10240:10752], W[:, 10752:11264]]
        q2P = [W[:, 11264:11776], W[:, 11776:12288]]
        MtPl = [S16[:, 7696:7824], W[:, 14848:14976]]
        aoP = [S16[:, 7824:7888], W[:, 14976:15040]]
        rnbP = [W[0:2, 15040:15552], W[0:2, 15552:16064]]
        argP = [S32[:, 256:768], S32[:, 0:512]]
        ssP = [S32[0:2, 256:768], S32[0:2, 0:512]]
        bqP = [S[:, 490:491], S[:, 470:471]]
        bkP = [S[:, 491:492], S[:, 471:472]]
        rdP = [S[:, 492:493], S[:, 476:477]]
        dmP = [S[:, 493:494], S[:, 477:478]]
        KVBF = S16[:, 6656:7680]
        KSBF = S16[:, 7680:7696]
        WQP = [W[:, 16384:18432], W[:, 18432:20480]]
        WVP = [W[:, 8192:9216], W[:, 9216:10240]]
        BVP = [self.CB16[0:1, 256:384], self.CB16[0:1, 384:512]]
        WO = W[:, 0:8192]
        b0, b1, b2, b3, b4, b5, b6 = self.psum
        bb = self.psumb
        featB = [b0, b2]
        YB = [b4, b6]
        if kv_only:
            p.op("vector", lambda v: v.memset(self.KV[:, :], 0.0),
                 w=[self.KV[:, :]])
            p.op("vector", lambda v: v.memset(self.KS[:, :], 0.0),
                 w=[self.KS[:, :]])
        else:
            p.op("vector", lambda v: v.tensor_copy(self.KV[:, :], self.KVI[:, :]),
                 r=[self.KVI[:, :]], w=[self.KV[:, :]])
            p.op("vector", lambda v: v.tensor_copy(self.KS[:, :], self.KSI[:, :]),
                 r=[self.KSI[:, :]], w=[self.KS[:, :]])
            for g in range(8):
                p.dma("gpsimd", WO[:, g * 1024:(g + 1) * 1024],
                      self.i_wout[g * 128:(g + 1) * 128, :], self.s_w)
        for hi in range(2):
            rb = slice(hi * 64, hi * 64 + 64)
            cv = [c for c in range(16) if c % 2 == hi]
            for h in cv:
                p.op("vector", lambda v, rb=rb, h=h: v.tensor_copy(
                    KVBF[rb, h * 64:(h + 1) * 64],
                    self.KV[:, h * 64:(h + 1) * 64]),
                    r=[self.KV[:, h * 64:(h + 1) * 64]],
                    w=[KVBF[rb, h * 64:(h + 1) * 64]])
                p.op("vector", lambda v, rb=rb, h=h: v.tensor_copy(
                    KSBF[rb, h:h + 1], self.KS[:, h:h + 1]),
                    r=[self.KS[:, h:h + 1]], w=[KSBF[rb, h:h + 1]])

        pairs = [(tb, hp) for tb in range(4) for hp in range(8)]

        def loads(idx):
            tb, hp = pairs[idx]
            par = idx % 2
            if not kv_only:
                p.dma("gpsimd", WQP[par][:, 0:1024], self.i_qprL[hp], self.s_w)
            p.dma("gpsimd", WQP[par][:, 1024:2048], self.i_kprL[hp], self.s_w)
            p.dma("gpsimd", WVP[par][:, 0:1024], self.i_vL2[hp], self.s_w)
            p.dma("gpsimd", bqP[par],
                  self.i_bqkvrow[0:1, hp * 128:(hp + 1) * 128]
                  .rearrange("o f -> (o f)")[:, None], self.s_w)
            p.dma("gpsimd", bkP[par],
                  self.i_bqkvrow[0:1, 1024 + hp * 128: 1024 + (hp + 1) * 128]
                  .rearrange("o f -> (o f)")[:, None], self.s_w)
            p.dma("gpsimd", BVP[par],
                  self.i_bqkvrow[0:1, 2048 + hp * 128: 2048 + (hp + 1) * 128],
                  self.s_w)

        def emit_features(idx):
            tb, hp = pairs[idx]
            par = idx % 2
            qn, q2 = qnP[par], q2P[par]
            rnb, argsb, ssb = rnbP[par], argP[par], ssP[par]
            bf = featB[par]
            WQ, WV, BV = WQP[par], WVP[par], BVP[par]
            for which in (("q",) if not kv_only else ()) + ("k",):
                wt = WQ[:, 0:1024] if which == "q" else WQ[:, 1024:2048]
                bcol = bqP[par] if which == "q" else bkP[par]
                featbuf = qfP[par] if which == "q" else kfP[par]
                bias = self.sm_lnccol if which == "q" else 0.0
                for dc in range(8):
                    self.mm(bf[:, 0:512], wt[:, dc * 128:(dc + 1) * 128],
                            self.XF[:, dc * T + tb * 512: dc * T + (tb + 1) * 512],
                            start=(dc == 0), stop=(dc == 7))
                p.op("vector", lambda v, bcol=bcol, qn=qn, bf=bf:
                     v.tensor_scalar(qn, bf[:, 0:512], bcol, None, ALU.add),
                     r=[bf[:, 0:512], bcol], w=[qn])
                p.op("scalar", lambda s, q2=q2, qn=qn: s.activation(
                    q2, qn, AF.Square), r=[qn], w=[q2])
                self.mm(bf[0:2, 0:512], self.cb_sel2, q2)
                p.op("vector", lambda v, ssb=ssb, bf=bf: v.tensor_scalar(
                    ssb, bf[0:2, 0:512], 1e-24, None, ALU.max),
                    r=[bf[0:2, 0:512]], w=[ssb])
                p.op("scalar", lambda s, ssb=ssb: s.activation(ssb, ssb, AF.Ln),
                     r=[ssb], w=[ssb])
                p.op("scalar", lambda s, rnb=rnb, ssb=ssb: s.activation(
                    rnb, ssb, AF.Exp, scale=-0.5), r=[ssb], w=[rnb])
                self.mm(b3[:, 0:512], self.CBX[:, :], rnb)
                p.op("vector", lambda v, qn=qn: v.tensor_tensor(
                    qn, qn, b3[:, 0:512], ALU.mult),
                    r=[qn, b3[:, 0:512]], w=[qn])
                for hi in range(2):
                    h = 2 * hp + hi
                    rb = slice(hi * 64, hi * 64 + 64)
                    om = self.OM[:, h * 64:(h + 1) * 64] if hi == 0 else \
                        self.OM2[64:128, h * 64:(h + 1) * 64]
                    bk = bf if hi == 0 else b3
                    self.mm(bk[0:64, 0:512], om, qn[rb, :])
                    arg = argsb[rb, :]
                    p.op("vector", lambda v, arg=arg, bk=bk: v.tensor_scalar(
                        arg, bk[0:64, 0:512], self.sm_s2col, self.sm_negscol,
                        ALU.mult, ALU.add),
                        r=[bk[0:64, 0:512], self.sm_s2col, self.sm_negscol],
                        w=[arg])
                    p.op("vector", lambda v, arg=arg: v.tensor_scalar(
                        arg, arg, -20.0, 20.0, ALU.max, ALU.min),
                        r=[arg], w=[arg])
                    extr = [] if isinstance(bias, float) else [bias]
                    feat = featbuf[rb, :]
                    p.op("scalar", lambda s, feat=feat, bias=bias, arg=arg:
                         s.activation(feat, arg, AF.Exp, bias=bias),
                         r=[arg] + extr, w=[feat])
            for hi in range(2):
                h = 2 * hp + hi
                rb = slice(hi * 64, hi * 64 + 64)
                idn = self.cb_ident[rb, rb]
                for cc in range(4):
                    sl = bb[0:128, par * 256 + cc * 64: par * 256 + (cc + 1) * 64]
                    kslice = kfP[par][rb, cc * 128:(cc + 1) * 128]
                    self.p.op("tensor",
                              lambda t, o=sl, i=kslice, idn=idn:
                              t.transpose(o, i, idn),
                              r=[kslice, idn], w=[sl], pe="single")
                    dst = ktP[par][:, hi * 256 + cc * 64: hi * 256 + (cc + 1) * 64]
                    p.op("scalar", lambda s, d=dst, sl=sl: s.copy(d, sl),
                         r=[sl], w=[dst])
            for cc in range(4):
                bvs = bf[:, cc * 128:(cc + 1) * 128]
                for dc in range(8):
                    self.mm(bvs,
                            self.XF[:, dc * T + tb * 512 + cc * 128:
                                    dc * T + tb * 512 + (cc + 1) * 128],
                            WV[:, dc * 128:(dc + 1) * 128],
                            start=(dc == 0), stop=False)
                self.mm(bvs, self.cb_ones[0:1, :], BV[0:1, 0:128],
                        start=False, stop=True)
                for hi in range(2):
                    src = bf[:, cc * 128 + hi * 64: cc * 128 + (hi + 1) * 64]
                    dst = vtP[par][:, hi * 256 + cc * 64: hi * 256 + (cc + 1) * 64]
                    p.op("vector", lambda v, d=dst, s_=src: v.tensor_copy(d, s_),
                         r=[src], w=[dst])

        def emit_scan(idx, hi):
            tb, hp = pairs[idx]
            par = idx % 2
            h = 2 * hp + hi
            rb = slice(hi * 64, hi * 64 + 64)
            Mt0, ao0 = MtPl[par], aoP[par]
            rden, dmax = rdP[par], dmP[par]
            bY = YB[par]
            hv = slice(h * 64, (h + 1) * 64)
            hk = slice(h, h + 1)
            for cc in range(4):
                kfs = kfP[par][rb, cc * 128:(cc + 1) * 128]
                qfs = qfP[par][rb, cc * 128:(cc + 1) * 128]
                kts = ktP[par][:, hi * 256 + cc * 64: hi * 256 + (cc + 1) * 64]
                vts = vtP[par][:, hi * 256 + cc * 64: hi * 256 + (cc + 1) * 64]
                if not kv_only:
                    self.mm(b1[:, par * 128:par * 128 + 128], kfs, qfs)
                    p.op("vector", lambda v, Mt0=Mt0, par=par: v.tensor_tensor(
                        Mt0, b1[:, par * 128:par * 128 + 128], self.c_mask01,
                        ALU.mult),
                        r=[b1[:, par * 128:par * 128 + 128], self.c_mask01],
                        w=[Mt0])
                    self.mm(bY[:, 0:64], Mt0, vts, start=True, stop=False)
                    self.mm(bY[:, 0:64], qfs, KVBF[rb, hv], start=False,
                            stop=False)
                    self.mm(bY[:, 64:65], Mt0, self.cb_ones[:, 0:1],
                            start=False, stop=False)
                    self.mm(bY[:, 64:65], qfs, KSBF[rb, hk], start=False,
                            stop=True)
                    p.op("vector", lambda v, dmax=dmax, bY=bY: v.tensor_scalar(
                        dmax, bY[:, 64:65], 1e-6, None, ALU.max),
                        r=[bY[:, 64:65]], w=[dmax])
                    p.op("vector", lambda v, rden=rden, dmax=dmax: v.reciprocal(
                        rden, dmax), r=[dmax], w=[rden])
                    p.op("vector", lambda v, ao0=ao0, rden=rden, bY=bY:
                         v.tensor_scalar(ao0, bY[:, 0:64], rden[:, 0:1], None,
                                         ALU.mult),
                         r=[bY[:, 0:64], rden], w=[ao0])
                    sl = bb[0:64, 512 + par * 128: 512 + par * 128 + 128]
                    self.p.op("tensor",
                              lambda t, o=sl, i=ao0, idn=self.cb_ident[0:128, 0:128]:
                              t.transpose(o, i, idn),
                              r=[ao0, self.cb_ident[0:128, 0:128]], w=[sl],
                              pe="single")
                    dst = AOF[(h % 2) * 64:(h % 2) * 64 + 64,
                              cc * 1024 + (h // 2) * 128:
                              cc * 1024 + (h // 2) * 128 + 128]
                    p.op("vector", lambda v, d=dst, sl=sl: v.tensor_copy(d, sl),
                         r=[sl], w=[dst])
                self.mm(b5[0:64, par * 80:par * 80 + 64], kts, vts,
                        start=True, stop=False)
                self.mm(b5[0:64, par * 80 + 64:par * 80 + 65], kts,
                        self.cb_ones[:, 0:1], start=False, stop=True)
                kv = self.KV[:, hv]
                ks = self.KS[:, hk]
                p.op("vector", lambda v, kv=kv, par=par: v.tensor_tensor(
                    kv, kv, b5[0:64, par * 80:par * 80 + 64], ALU.add),
                    r=[kv, b5[0:64, par * 80:par * 80 + 64]], w=[kv])
                p.op("vector", lambda v, ks=ks, par=par: v.tensor_tensor(
                    ks, ks, b5[0:64, par * 80 + 64:par * 80 + 65], ALU.add),
                    r=[ks, b5[0:64, par * 80 + 64:par * 80 + 65]], w=[ks])
                p.op("scalar", lambda s, kv=kv, hv=hv, rb=rb: s.copy(
                    KVBF[rb, hv], kv), r=[kv], w=[KVBF[rb, hv]])
                p.op("scalar", lambda s, ks=ks, hk=hk, rb=rb: s.copy(
                    KSBF[rb, hk], ks), r=[ks], w=[KSBF[rb, hk]])

        def emit_wout(tb):
            bN = [b0, b1, b2, b3]
            for cc in range(4):
                tt = tb * 4 + cc
                for nb in range(2):
                    bo = bN[cc % 2 * 2 + nb]
                    for g in range(8):
                        self.mm(bo[:, 0:512],
                                AOF[:, cc * 1024 + g * 128: cc * 1024 + (g + 1) * 128],
                                WO[:, g * 1024 + nb * 512: g * 1024 + (nb + 1) * 512],
                                start=(g == 0), stop=False)
                    self.mm(bo[:, 0:512], self.cb_ones[0:1, 0:128],
                            self.br_out[0:1, nb * 512:(nb + 1) * 512],
                            start=False, stop=True)
                    xs = self.X[:, tt * 1024 + nb * 512:
                                tt * 1024 + (nb + 1) * 512]
                    p.op("vector", lambda v, xs=xs, bo=bo: v.tensor_tensor(
                        xs, xs, bo[:, 0:512], ALU.add),
                        r=[xs, bo[:, 0:512]], w=[xs])

        loads(0)
        emit_features(0)
        for idx, (tb, hp) in enumerate(pairs):
            if idx + 1 < len(pairs):
                loads(idx + 1)
                emit_features(idx + 1)
            emit_scan(idx, 0)
            emit_scan(idx, 1)
            if not kv_only and hp == 7:
                emit_wout(tb)
        if kv_only:
            p.op("vector", lambda v: v.tensor_scalar(
                self.KVI[:, :], self.KV[:, :], self.sm_oddm[0:64, 0:1], None,
                ALU.mult), r=[self.KV[:, :], self.sm_oddm], w=[self.KVI[:, :]])
            p.op("vector", lambda v: v.tensor_scalar(
                self.KSI[:, :], self.KS[:, :], self.sm_oddm[0:64, 0:1], None,
                ALU.mult), r=[self.KS[:, :], self.sm_oddm], w=[self.KSI[:, :]])

    def l1_attn_kv(self):
        p = self.p
        S16, S32, S = self.S16, self.S32, self.SMALL
        W = self.WBUF
        kfP = [S16[:, 4608:5120], W[:, 12800:13312]]
        ktP = [S16[:, 5120:5632], W[:, 13312:13824]]
        vtP = [S16[:, 5632:6144], W[:, 13824:14336]]
        qnP = [W[:, 10240:10752], W[:, 10752:11264]]
        q2P = [W[:, 11264:11776], W[:, 11776:12288]]
        rnbP = [W[0:2, 15040:15552], W[0:2, 15552:16064]]
        argP = [S32[:, 256:768], S32[:, 0:512]]
        ssP = [S32[0:2, 256:768], S32[0:2, 0:512]]
        bkP = [S[:, 491:492], S[:, 471:472]]
        WQP = [W[:, 16384:18432], W[:, 18432:20480]]
        WVP = [W[:, 8192:9216], W[:, 9216:10240]]
        BVP = [self.CB16[0:1, 256:384], self.CB16[0:1, 384:512]]
        b0, b1, b2, b3, b4, b5, b6 = self.psum
        bb = self.psumb
        featB = [b0, b2]
        accB = [b1, b4, b5, b6]
        p.op("vector", lambda v: v.memset(self.KVI[:, :], 0.0),
             w=[self.KVI[:, :]])
        p.op("vector", lambda v: v.memset(self.KSI[:, :], 0.0),
             w=[self.KSI[:, :]])
        pairs = [(tb, hp) for tb in range(4) for hp in range(8)]

        def loads(idx):
            tb, hp = pairs[idx]
            par = idx % 2
            p.dma("gpsimd", WQP[par][:, 1024:2048], self.i_kprL[hp], self.s_w)
            p.dma("gpsimd", WVP[par][:, 0:1024], self.i_vL2[hp], self.s_w)
            p.dma("gpsimd", bkP[par],
                  self.i_bqkvrow[0:1, 1024 + hp * 128: 1024 + (hp + 1) * 128]
                  .rearrange("o f -> (o f)")[:, None], self.s_w)
            p.dma("gpsimd", BVP[par],
                  self.i_bqkvrow[0:1, 2048 + hp * 128: 2048 + (hp + 1) * 128],
                  self.s_w)

        def feats(idx):
            tb, hp = pairs[idx]
            par = idx % 2
            qn, q2 = qnP[par], q2P[par]
            rnb, argsb, ssb = rnbP[par], argP[par], ssP[par]
            bf = featB[par]
            WQ, WV, BV = WQP[par], WVP[par], BVP[par]
            bcol = bkP[par]
            for dc in range(8):
                self.mm(bf[:, 0:512], WQ[:, 1024 + dc * 128: 1024 + (dc + 1) * 128],
                        self.XF[:, dc * T + tb * 512: dc * T + (tb + 1) * 512],
                        start=(dc == 0), stop=(dc == 7))
            p.op("vector", lambda v, bcol=bcol, qn=qn, bf=bf:
                 v.tensor_scalar(qn, bf[:, 0:512], bcol, None, ALU.add),
                 r=[bf[:, 0:512], bcol], w=[qn])
            p.op("scalar", lambda s, q2=q2, qn=qn: s.activation(
                q2, qn, AF.Square), r=[qn], w=[q2])
            self.mm(bf[0:2, 0:512], self.cb_sel2, q2)
            p.op("vector", lambda v, ssb=ssb, bf=bf: v.tensor_scalar(
                ssb, bf[0:2, 0:512], 1e-24, None, ALU.max),
                r=[bf[0:2, 0:512]], w=[ssb])
            p.op("scalar", lambda s, ssb=ssb: s.activation(ssb, ssb, AF.Ln),
                 r=[ssb], w=[ssb])
            p.op("scalar", lambda s, rnb=rnb, ssb=ssb: s.activation(
                rnb, ssb, AF.Exp, scale=-0.5), r=[ssb], w=[rnb])
            self.mm(b3[:, 0:512], self.CBX[:, :], rnb)
            p.op("vector", lambda v, qn=qn: v.tensor_tensor(
                qn, qn, b3[:, 0:512], ALU.mult),
                r=[qn, b3[:, 0:512]], w=[qn])
            for hi in range(2):
                h = 2 * hp + hi
                rb = slice(hi * 64, hi * 64 + 64)
                om = self.OM[:, h * 64:(h + 1) * 64] if hi == 0 else \
                    self.OM2[64:128, h * 64:(h + 1) * 64]
                bk = bf if hi == 0 else b3
                self.mm(bk[0:64, 0:512], om, qn[rb, :])
                arg = argsb[rb, :]
                p.op("vector", lambda v, arg=arg, bk=bk: v.tensor_scalar(
                    arg, bk[0:64, 0:512], self.sm_s2col, self.sm_negscol,
                    ALU.mult, ALU.add),
                    r=[bk[0:64, 0:512], self.sm_s2col, self.sm_negscol],
                    w=[arg])
                p.op("vector", lambda v, arg=arg: v.tensor_scalar(
                    arg, arg, -20.0, 20.0, ALU.max, ALU.min),
                    r=[arg], w=[arg])
                feat = kfP[par][rb, :]
                p.op("scalar", lambda s, feat=feat, arg=arg: s.activation(
                    feat, arg, AF.Exp), r=[arg], w=[feat])
                tmp = S[0:64, 486 + par * 2 + hi: 487 + par * 2 + hi]
                ks = self.KSI[:, h:h + 1]
                p.op("vector", lambda v, tmp=tmp, feat=feat: v.reduce_sum(
                    tmp, feat, axis=AX.X), r=[feat], w=[tmp])
                p.op("vector", lambda v, ks=ks, tmp=tmp: v.tensor_tensor(
                    ks, ks, tmp, ALU.add), r=[ks, tmp], w=[ks])
            for hi in range(2):
                rb = slice(hi * 64, hi * 64 + 64)
                idn = self.cb_ident[rb, rb]
                for cc in range(4):
                    sl = bb[0:128, par * 256 + cc * 64: par * 256 + (cc + 1) * 64]
                    kslice = kfP[par][rb, cc * 128:(cc + 1) * 128]
                    self.p.op("tensor",
                              lambda t, o=sl, i=kslice, idn=idn:
                              t.transpose(o, i, idn),
                              r=[kslice, idn], w=[sl], pe="single")
                    dst = ktP[par][:, hi * 256 + cc * 64: hi * 256 + (cc + 1) * 64]
                    p.op("scalar", lambda s, d=dst, sl=sl: s.copy(d, sl),
                         r=[sl], w=[dst])
            for cc in range(4):
                bvs = bf[:, cc * 128:(cc + 1) * 128]
                for dc in range(8):
                    self.mm(bvs,
                            self.XF[:, dc * T + tb * 512 + cc * 128:
                                    dc * T + tb * 512 + (cc + 1) * 128],
                            WV[:, dc * 128:(dc + 1) * 128],
                            start=(dc == 0), stop=False)
                self.mm(bvs, self.cb_ones[0:1, :], BV[0:1, 0:128],
                        start=False, stop=True)
                for hi in range(2):
                    src = bf[:, cc * 128 + hi * 64: cc * 128 + (hi + 1) * 64]
                    dst = vtP[par][:, hi * 256 + cc * 64: hi * 256 + (cc + 1) * 64]
                    p.op("vector", lambda v, d=dst, s_=src: v.tensor_copy(d, s_),
                         r=[src], w=[dst])

        def accum(idx):
            tb, hp = pairs[idx]
            par = idx % 2
            for hi in range(2):
                h = 2 * hp + hi
                bA = accB[(idx * 2 + hi) % 4]
                for cc in range(4):
                    kts = ktP[par][:, hi * 256 + cc * 64: hi * 256 + (cc + 1) * 64]
                    vts = vtP[par][:, hi * 256 + cc * 64: hi * 256 + (cc + 1) * 64]
                    self.mm(bA[0:64, 0:64], kts, vts,
                            start=(cc == 0), stop=(cc == 3))
                kv = self.KVI[:, h * 64:(h + 1) * 64]
                p.op("vector", lambda v, kv=kv, bA=bA: v.tensor_tensor(
                    kv, kv, bA[0:64, 0:64], ALU.add),
                    r=[kv, bA[0:64, 0:64]], w=[kv])

        loads(0)
        feats(0)
        for idx in range(len(pairs)):
            if idx + 1 < len(pairs):
                loads(idx + 1)
                feats(idx + 1)
            accum(idx)
        p.op("vector", lambda v: v.tensor_scalar(
            self.KVI[:, :], self.KVI[:, :], self.sm_oddm[0:64, 0:1], None,
            ALU.mult), r=[self.KVI[:, :], self.sm_oddm], w=[self.KVI[:, :]])
        p.op("vector", lambda v: v.tensor_scalar(
            self.KSI[:, :], self.KSI[:, :], self.sm_oddm[0:64, 0:1], None,
            ALU.mult), r=[self.KSI[:, :], self.sm_oddm], w=[self.KSI[:, :]])

    def l1_moe(self):
        p = self.p
        S16, S32 = self.S16, self.S32
        c10 = S32[:, 0:160]
        W2G = self.WBUF[:, 8192:12288]
        SA = S16[:, 4096:6144]
        H = S16[:, 0:4096]
        b0, b1 = self.psum[0], self.psum[1]
        bRP = [self.psum[6], self.psum[5]]
        p.op("vector", lambda v: v.memset(c10, 1.0), w=[c10])
        RT = S32
        for t in range(NT):
            bR = bRP[t % 2]
            base = 176 + t * 37
            lg = RT[:, base:base + 8]
            mska = RT[:, base + 8:base + 16]
            mskb = RT[:, base + 16:base + 24]
            lmask = RT[:, base + 24:base + 32]
            m1c = RT[:, base + 32:base + 33]
            m2c = RT[:, base + 33:base + 34]
            w1c = RT[:, base + 34:base + 35]
            w2c = RT[:, base + 35:base + 36]
            for dc in range(8):
                self.mm(bR[:, 0:8], self.XF[:, dc * T + t * 128: dc * T + (t + 1) * 128],
                        self.RWsb[:, dc * 8:(dc + 1) * 8],
                        start=(dc == 0), stop=False)
            self.mm(bR[:, 0:8], self.cb_ones[0:1, :], self.br_eb,
                    start=False, stop=True)
            p.op("vector", lambda v, lg=lg, bR=bR: v.tensor_copy(lg, bR[:, 0:8]),
                 r=[bR[:, 0:8]], w=[lg])
            p.op("vector", lambda v, m1c=m1c, lg=lg: v.reduce_max(
                m1c, lg, axis=AX.X), r=[lg], w=[m1c])
            p.op("vector", lambda v, mska=mska, lg=lg, m1c=m1c: v.tensor_scalar(
                mska, lg, m1c[:, 0:1], None, ALU.is_equal),
                r=[lg, m1c], w=[mska])
            p.op("vector", lambda v, lmask=lmask, mska=mska, lg=lg:
                 v.scalar_tensor_tensor(lmask, mska, -1e30, lg, ALU.mult, ALU.add),
                 r=[mska, lg], w=[lmask])
            p.op("vector", lambda v, m2c=m2c, lmask=lmask: v.reduce_max(
                m2c, lmask, axis=AX.X), r=[lmask], w=[m2c])
            p.op("vector", lambda v, mskb=mskb, lg=lg, m2c=m2c: v.tensor_scalar(
                mskb, lg, m2c[:, 0:1], None, ALU.is_equal),
                r=[lg, m2c], w=[mskb])
            p.op("vector", lambda v, w1c=w1c, m1c=m1c, m2c=m2c: v.tensor_sub(
                w1c, m1c, m2c), r=[m1c, m2c], w=[w1c])
            p.op("scalar", lambda s, w1c=w1c: s.activation(w1c, w1c, AF.Sigmoid),
                 r=[w1c], w=[w1c])
            p.op("vector", lambda v, w2c=w2c, m1c=m1c, m2c=m2c: v.tensor_sub(
                w2c, m2c, m1c), r=[m1c, m2c], w=[w2c])
            p.op("scalar", lambda s, w2c=w2c: s.activation(w2c, w2c, AF.Sigmoid),
                 r=[w2c], w=[w2c])
            dst = c10[:, t * 10: t * 10 + 8]
            p.op("vector", lambda v, d=dst, mska=mska, w1c=w1c: v.tensor_scalar(
                d, mska, w1c[:, 0:1], None, ALU.mult), r=[mska, w1c], w=[dst])
            p.op("vector", lambda v, d=dst, mskb=mskb, w2c=w2c:
                 v.scalar_tensor_tensor(d, mskb, w2c[:, 0:1], d, ALU.mult, ALU.add),
                 r=[mskb, w2c, dst], w=[dst])
        W1P = [self.WBUF[:, 0:4096], self.WBUF[:, 16384:20480]]
        W3P = [self.WBUF[:, 4096:8192], self.WBUF[:, 12288:16384]]
        groups = [(e, g) for e in range(NE) for g in range(4)]

        def load13(idx, par):
            e, g = groups[idx]
            for j in range(4):
                hc = g * 4 + j
                p.dma("gpsimd", W1P[par][:, j * 1024:(j + 1) * 1024],
                      self.i_mw1L[e, hc], self.s_w)
                p.dma("gpsimd", W3P[par][:, j * 1024:(j + 1) * 1024],
                      self.i_mw3L[e, hc], self.s_w)

        def load2(idx):
            e, g = groups[idx]
            for j in range(4):
                hc = g * 4 + j
                p.dma("gpsimd", W2G[:, j * 1024:(j + 1) * 1024],
                      self.i_mw2T[e, hc * 128:(hc + 1) * 128, :], self.s_w)

        load13(0, 0)
        load2(0)
        for gi, (e, g) in enumerate(groups):
            par = gi % 2
            if gi + 1 < len(groups):
                load13(gi + 1, 1 - par)
            W1G = W1P[par]
            W3G = W3P[par]
            for tp2 in range(2):
                tbs = (2 * tp2, 2 * tp2 + 1)
                for j in range(4):
                    baP = (b0, self.psum[2]) if j % 2 == 0 else (self.psum[1], self.psum[3])
                    saP = (SA[:, 0:512], SA[:, 512:1024]) if j % 2 == 0 else \
                        (SA[:, 1024:1536], SA[:, 1536:2048])
                    for dc in range(8):
                        for ti, tb in enumerate(tbs):
                            self.mm(baP[ti][:, 0:512],
                                    W1G[:, j * 1024 + dc * 128: j * 1024 + (dc + 1) * 128],
                                    self.XF[:, dc * T + tb * 512: dc * T + (tb + 1) * 512],
                                    start=(dc == 0), stop=(dc == 7))
                    for ti in range(2):
                        p.op("scalar", lambda s, ba=baP[ti], sa=saP[ti]: s.activation(
                            sa, ba[:, 0:512], AF.Silu), r=[baP[ti][:, 0:512]], w=[saP[ti]])
                    for dc in range(8):
                        for ti, tb in enumerate(tbs):
                            self.mm(baP[ti][:, 0:512],
                                    W3G[:, j * 1024 + dc * 128: j * 1024 + (dc + 1) * 128],
                                    self.XF[:, dc * T + tb * 512: dc * T + (tb + 1) * 512],
                                    start=(dc == 0), stop=(dc == 7))
                    for ti, tb in enumerate(tbs):
                        hd = H[:, (ti * 4 + j) * 512:(ti * 4 + j + 1) * 512]
                        p.op("vector", lambda v, hd=hd, ba=baP[ti], sa=saP[ti]:
                             v.tensor_tensor(hd, sa, ba[:, 0:512], ALU.mult),
                             r=[saP[ti], baP[ti][:, 0:512]], w=[hd])
                for ti, tb in enumerate(tbs):
                    for q in range(4):
                        tt = tb * 4 + q
                        for nb in range(2):
                            bo = self.psum[4 + (q * 2 + nb) % 3]
                            for j in range(4):
                                self.mm(bo[:, 0:512],
                                        H[:, (ti * 4 + j) * 512 + q * 128:
                                          (ti * 4 + j) * 512 + q * 128 + 128],
                                        W2G[:, j * 1024 + nb * 512: j * 1024 + (nb + 1) * 512],
                                        start=(j == 0), stop=(j == 3))
                            xs = self.X[:, tt * 1024 + nb * 512:
                                        tt * 1024 + (nb + 1) * 512]
                            ccol = c10[:, tt * 10 + e: tt * 10 + e + 1]
                            p.op("vector", lambda v, xs=xs, bo=bo, ccol=ccol:
                                 v.scalar_tensor_tensor(xs, bo[:, 0:512], ccol,
                                                        xs, ALU.mult, ALU.add),
                                 r=[bo[:, 0:512], ccol, xs], w=[xs])
            if gi + 1 < len(groups):
                load2(gi + 1)


_NC_CACHE = None


def _build_program():
    global _NC_CACHE
    if _NC_CACHE is None:
        m = MK()
        m.setup()
        m.run_l0(m.i_px, save_sinit=True)
        m.fmconv()
        m.l1_attn_kv()
        m.run_l0(m.i_x)
        m.fmconv()
        m.l1_attn(kv_only=False)
        m.postln()
        m.fmconv()
        m.l1_moe()
        m.postln()
        m.store_out()
        _NC_CACHE = m.finalize()
    return _NC_CACHE


def kernel(**inputs):
    from concourse.bass_utils import run_bass_kernel_spmd
    per_core = host_prep(inputs)
    nc = _build_program()
    res = run_bass_kernel_spmd(nc, per_core, core_ids=list(range(8)))
    out = np.zeros((4, 4096, 1024), np.float32)
    for c in range(8):
        b, h = c // 2, c % 2
        out[b, h * 2048:(h + 1) * 2048] = res.results[c]["out"]
    return out
